# version 07
# baseline (speedup 1.0000x reference)
import numpy as np
import ml_dtypes
import contextlib
import concourse.bass as bass
import concourse.mybir as mybir
from concourse.bass_utils import run_bass_kernel_spmd

dt = mybir.dt
AF = mybir.ActivationFunctionType
AL = mybir.AluOpType
bf16 = ml_dtypes.bfloat16

B, N, DM = 2, 4096, 2048
NH, HD = 16, 128
NT = 1024
FC = 16
I5 = 5632
GU_O = 1408
DN_F = 256
NIC = 44


def build_nc(dbg=(), stage=99):
    nc = bass.Bass("TRN2", target_bir_lowering=False)

    xT = nc.dram_tensor("xT", [DM, NT], dt.float32, kind="ExternalInput").ap()
    cT = nc.dram_tensor("cT", [DM, B], dt.float32, kind="ExternalInput").ap()
    wmod_s = nc.dram_tensor("wmod_s", [DM, 1536], dt.float32, kind="ExternalInput").ap()
    wqkvT = nc.dram_tensor("wqkvT", [DM, 6144], dt.float32, kind="ExternalInput").ap()
    wgateT = nc.dram_tensor("wgateT", [DM, DM], dt.float32, kind="ExternalInput").ap()
    woutT = nc.dram_tensor("woutT", [DM, DM], dt.float32, kind="ExternalInput").ap()
    s_row = nc.dram_tensor("s_row", [1, NH], dt.float32, kind="ExternalInput").ap()
    normw = nc.dram_tensor("normw", [128, FC], dt.float32, kind="ExternalInput").ap()
    gu16 = nc.dram_tensor("gu16", [512, GU_O], dt.int16, kind="ExternalInput").ap()
    gusc = nc.dram_tensor("gusc", [FC, GU_O], dt.float16, kind="ExternalInput").ap()
    dn16 = nc.dram_tensor("dn16", [1408, DN_F], dt.int16, kind="ExternalInput").ap()
    dnsc = nc.dram_tensor("dnsc", [NIC, DN_F], dt.float16, kind="ExternalInput").ap()
    identb_in = nc.dram_tensor("identb_in", [128, 128], dt.bfloat16, kind="ExternalInput").ap()
    identf_in = nc.dram_tensor("identf_in", [128, 128], dt.float32, kind="ExternalInput").ap()
    onesb_in = nc.dram_tensor("onesb_in", [128, 1], dt.bfloat16, kind="ExternalInput").ap()
    onesf_in = nc.dram_tensor("onesf_in", [1, 128], dt.float32, kind="ExternalInput").ap()
    arrB4_in = nc.dram_tensor("arrB4_in", [128, NT], dt.float32, kind="ExternalInput").ap()
    idxT_in = nc.dram_tensor("idxT_in", [128, 2, 256], dt.float32, kind="ExternalInput").ap()
    kdcolT_in = nc.dram_tensor("kdcolT_in", [128, 2], dt.float32, kind="ExternalInput").ap()
    c256_in = nc.dram_tensor("c256_in", [128, 4], dt.float32, kind="ExternalInput").ap()
    expo_in = nc.dram_tensor("expo_in", [128, 4], dt.float32, kind="ExternalInput").ap()
    shcol_in = nc.dram_tensor("shcol_in", [128, 1], dt.int16, kind="ExternalInput").ap()
    lshcol_in = nc.dram_tensor("lshcol_in", [128, 1], dt.int16, kind="ExternalInput").ap()
    isb1_in = nc.dram_tensor("isb1_in", [128, 2], dt.float32, kind="ExternalInput").ap()
    eps_in = nc.dram_tensor("eps_in", [1, 2], dt.float32, kind="ExternalInput").ap()

    wqkvT_bf = nc.dram_tensor("wqkvT_bf", [DM, 6144], dt.bfloat16).ap()
    wgateT_bf = nc.dram_tensor("wgateT_bf", [DM, DM], dt.bfloat16).ap()
    woutT_bf = nc.dram_tensor("woutT_bf", [DM, DM], dt.bfloat16).ap()
    wmod_bf = nc.dram_tensor("wmod_bf", [DM, 1536], dt.bfloat16).ap()
    mod_in = nc.dram_tensor("mod_in", [B, 1536], dt.float32).ap()
    mod_all = nc.dram_tensor("mod_all", [B * 8, 1536], dt.float32).ap()
    dbt_dram = nc.dram_tensor("dbt_dram", [NH * 128, 512], dt.float32).ap()
    kc_in = nc.dram_tensor("kc_in", [NH * HD, HD], dt.float32).ap()
    kc_all = nc.dram_tensor("kc_all", [4 * NH * HD, HD], dt.float32).ap()
    gu_loc = nc.dram_tensor("gu_loc", [DM, GU_O], dt.bfloat16).ap()
    gu_all = nc.dram_tensor("gu_all", [8 * DM, GU_O], dt.bfloat16).ap()
    dn_loc = nc.dram_tensor("dn_loc", [I5, DN_F], dt.bfloat16).ap()
    dn_all = nc.dram_tensor("dn_all", [8 * I5, DN_F], dt.bfloat16).ap()
    x2_dram = nc.dram_tensor("x2_dram", [DM, NT], dt.float32).ap()
    y_out = nc.dram_tensor("y_out", [NT, DM], dt.float32, kind="ExternalOutput").ap()

    dbg_outs = {}
    for name, shape, d_ in [
        ("dbg_mod", [128, 96], dt.float32),
        ("dbg_modin", [B, 1536], dt.float32),
        ("dbg_wmodbf", [DM, 1536], dt.bfloat16),
        ("dbg_reps", [128, 2, NT], dt.bfloat16),
        ("dbg_wqkvbf", [DM, 256], dt.bfloat16),
        ("dbg_modall", [16, 1536], dt.float32),
        ("dbg_h", [128, FC, NT], dt.bfloat16),
        ("dbg_o", [128, FC, NT], dt.bfloat16),
        ("dbg_x2", [DM, NT], dt.float32),
        ("dbg_act", [128, NIC, NT], dt.bfloat16),
        ("dbg_guw", [DM, GU_O], dt.bfloat16),
        ("dbg_y", [128, FC, NT], dt.bfloat16),
        ("dbg_dnw", [I5, DN_F], dt.bfloat16),
        ("dbg_kc", [NH * HD, HD], dt.float32),
    ]:
        if name in dbg:
            dbg_outs[name] = nc.dram_tensor(name, shape, d_, kind="ExternalOutput").ap()

    es = contextlib.ExitStack()
    with es:
        E = es.enter_context
        h_fm = E(nc.sbuf_tensor("h_fm", [128, FC, NT], dt.bfloat16))
        big = E(nc.sbuf_tensor("big", [128, NIC, NT], dt.bfloat16))
        strm = E(nc.sbuf_tensor("strm", [128, 4, NT], dt.bfloat16))
        dscr = E(nc.sbuf_tensor("dscr", [128, 22528], dt.uint8))
        lall = E(nc.sbuf_tensor("lall", [128, 48, HD], dt.bfloat16))
        rows = E(nc.sbuf_tensor("rows", [1, 2, NT], dt.float32))
        identb = E(nc.sbuf_tensor("identb", [128, 128], dt.bfloat16))
        identf = E(nc.sbuf_tensor("identf", [128, 128], dt.float32))
        onesb = E(nc.sbuf_tensor("onesb", [128, 1], dt.bfloat16))
        onesf = E(nc.sbuf_tensor("onesf", [1, 128], dt.float32))
        idxT = E(nc.sbuf_tensor("idxT", [128, 2, 256], dt.float32))
        kdcolT = E(nc.sbuf_tensor("kdcolT", [128, 2], dt.float32))
        c256B = E(nc.sbuf_tensor("c256B", [128, 4], dt.float32))
        expoB = E(nc.sbuf_tensor("expoB", [128, 4], dt.float32))
        shcol = E(nc.sbuf_tensor("shcol", [128, 1], dt.int16))
        lshcol = E(nc.sbuf_tensor("lshcol", [128, 1], dt.int16))
        isb1 = E(nc.sbuf_tensor("isb1", [128, 2], dt.float32))
        epsc = E(nc.sbuf_tensor("epsc", [1, 2], dt.float32))
        negsB = E(nc.sbuf_tensor("negsB", [128, NH], dt.float32))
        normw_c = E(nc.sbuf_tensor("normw_c", [128, FC], dt.float32))
        cT_sb = E(nc.sbuf_tensor("cT_sb", [128, FC, B], dt.float32))
        cT_bf = E(nc.sbuf_tensor("cT_bf", [128, FC, B], dt.bfloat16))
        mod_cols = E(nc.sbuf_tensor("mod_cols", [128, 96], dt.float32))
        dcol = E(nc.sbuf_tensor("dcol", [128, FC, 4], dt.float32))
        gscr = E(nc.sbuf_tensor("gscr", [128, FC, 2], dt.float32))
        DBT = E(nc.sbuf_tensor("DBT", [128, 4, 256], dt.float32))
        kdcol_all = E(nc.sbuf_tensor("kdcol_all", [128, NH, 2], dt.float32))
        bdj_all = E(nc.sbuf_tensor("bdj_all", [128, NH, 4], dt.float32))
        coef_all = E(nc.sbuf_tensor("coef_all", [128, NH, 4], dt.float32))
        kv = E(nc.sbuf_tensor("kv", [128, 2, HD], dt.float32))
        kvin = E(nc.sbuf_tensor("kvin", [128, HD], dt.float32))
        Sj = E(nc.sbuf_tensor("Sj", [128, 4, HD], dt.bfloat16))
        scrb = E(nc.sbuf_tensor("scrb", [128, NT], dt.bfloat16))
        sqscr = E(nc.sbuf_tensor("sqscr", [128, NT], dt.bfloat16))
        repA = E(nc.sbuf_tensor("repA", [128, NT], dt.bfloat16))
        repB = E(nc.sbuf_tensor("repB", [128, NT], dt.bfloat16))
        modin2 = E(nc.sbuf_tensor("modin2", [B, 2, 128], dt.float32))
        stg = E(nc.sbuf_tensor("stg", [128, 2, NT], dt.float32))
        stg2 = E(nc.sbuf_tensor("stg2", [128, 2, NT], dt.float32))

        def dview(off_b, shape, d_):
            n = 1
            for s in shape[1:]:
                n *= s
            isz = {dt.int16: 2, dt.float16: 2, dt.bfloat16: 2, dt.float32: 4}[d_]
            assert off_b % 4 == 0 and off_b + n * isz <= 24 * 1024
            ap = dscr[:, off_b:off_b + n * isz].bitcast(d_)
            if len(shape) == 3:
                ap = ap.rearrange("p (a b) -> p a b", a=shape[1])
            return ap

        dec_w = [dview(0, [128, GU_O], dt.int16), dview(2816, [128, GU_O], dt.int16)]
        dec_t0 = dview(5632, [128, GU_O], dt.int16)
        dec_t1 = dview(8448, [128, GU_O], dt.int16)
        dec_sc = [dview(11264, [128, GU_O], dt.float16), dview(14080, [128, GU_O], dt.float16)]
        dec_o = [dview(16896, [128, GU_O], dt.bfloat16), dview(19712, [128, GU_O], dt.bfloat16)]
        qdB4 = dview(0, [128, NT], dt.float32)
        fscr = dview(4096, [128, NT], dt.float32)
        arrB4 = dview(8192, [128, NT], dt.float32)
        repF = dview(12288, [128, NT], dt.float32)
        kcbuf = dview(16896, [128, 4, HD], dt.float32)
        xtile = [stg[:, 0, :], stg[:, 1, :]]
        xorig = [stg2[:, 0, :], stg2[:, 1, :]]
        modB0 = stg2[0:96, 0, 0:128]
        modB1 = stg2[0:96, 0, 128:256]
        modsel = stg2[:, 0, 256:352]

        sLD = E(nc.semaphore("sLD"))
        sGP = E(nc.semaphore("sGP"))
        sSC = E(nc.semaphore("sSC"))
        sPE = E(nc.semaphore("sPE"))
        sDV = E(nc.semaphore("sDV"))
        sAC = E(nc.semaphore("sAC"))
        sPO = E(nc.semaphore("sPO"))
        sCC = E(nc.semaphore("sCC"))
        sCAST = E(nc.semaphore("sCAST"))
        sDL = [E(nc.semaphore("sDL0")), E(nc.semaphore("sDL1"))]
        sDS = [E(nc.semaphore("sDS0")), E(nc.semaphore("sDS1"))]
        sAGM = E(nc.semaphore("sAGM"))
        sAGG = E(nc.semaphore("sAGG"))
        sAGD = E(nc.semaphore("sAGD"))
        sAGK = E(nc.semaphore("sAGK"))
        SEMX = {}
        for _n in ("CN", "SA", "SB", "MST", "M1", "XL", "XOa", "XOb",
                   "SPa", "SPb", "OUTa", "OUTb", "CW"):
            SEMX[_n] = E(nc.semaphore("s" + _n))

        mainA = E(nc.psum_tensor("mainA", [128, 2, 512], dt.float32))
        mainB = E(nc.psum_tensor("mainB", [128, 2, 512], dt.float32))
        psQK = E(nc.psum_tensor("psQK", [128, 2, 256], dt.float32))
        psIN = E(nc.psum_tensor("psIN", [128, 2, 256], dt.float32))
        psT = E(nc.psum_tensor("psT", [128, 4, 128], dt.float32))
        psS = E(nc.psum_tensor("psS", [128, 512], dt.float32))

        stat_ap = [
            psQK[0:1, :, :].rearrange("p a b -> p (a b)"),
            psIN[0:1, :, :].rearrange("p a b -> p (a b)"),
            psS[0:1, :],
            psT[0:1, :, :].rearrange("p a b -> p (a b)"),
        ]
        block = E(nc.Block())

        prog = {k: [] for k in ("sync", "gpsimd", "tensor", "vector", "scalar")}
        C = {k: 0 for k in ("LD", "GP", "SC", "PE", "DV", "AC", "PO", "CC")}
        SEM = {"LD": sLD, "GP": sGP, "SC": sSC, "PE": sPE, "DV": sDV,
               "AC": sAC, "PO": sPO, "CC": sCC}

        import sys
        sched = {k: [] for k in prog}

        def _ln():
            f = sys._getframe(2)
            return f.f_lineno

        def em(eng, fn):
            prog[eng].append(fn)

        def wait(eng, cname, val):
            if val is None:
                return
            if isinstance(val, tuple):
                wait(eng, val[0], val[1])
                return
            if callable(val):
                def f(e, g=val, cn=cname):
                    v = g()
                    if v is None:
                        return
                    if isinstance(v, tuple):
                        cn2, vv = v
                    else:
                        cn2, vv = cn, v
                    if vv:
                        e.wait_ge(SEM[cn2], vv)
                em(eng, f)
                sched[eng].append(("w", cname, val, _ln()))
                return
            if val <= 0:
                return
            em(eng, lambda e, s=SEM[cname], v=val: e.wait_ge(s, v))
            sched[eng].append(("w", cname, val, _ln()))

        def waitg(eng, guard):
            if guard is None:
                return
            if callable(guard):
                wait(eng, "PE", guard)
                return
            wait(eng, guard[0], guard[1])

        ENG_CN = {"vector": "DV", "scalar": "AC", "gpsimd": "PO"}

        def op(eng, fn, cname=None, n=1):
            if eng in ("vector", "scalar"):
                cn = ENG_CN[eng]
                if cname is not None and cname != cn:
                    C[cname] += n
                    em(eng, lambda e, f=fn, s=SEM[cname], n=n: f(e).then_inc(s, n))
                    sched[eng].append(("i", cname, n, _ln()))
                    return C[cname]
                prev = C[cn]
                if prev:
                    em(eng, lambda e, s=SEM[cn], v=prev: e.wait_ge(s, v))
                    sched[eng].append(("w", cn, prev, _ln()))
                C[cn] += 1
                em(eng, lambda e, f=fn, s=SEM[cn]: f(e).then_inc(s, 1))
                sched[eng].append(("i", cn, 1, _ln()))
                return C[cn]
            if cname is None:
                em(eng, lambda e, f=fn: f(e))
                sched[eng].append(("i", None, 0, _ln()))
                return None
            C[cname] += n
            em(eng, lambda e, f=fn, s=SEM[cname], n=n: f(e).then_inc(s, n))
            sched[eng].append(("i", cname, n, _ln()))
            return C[cname]

        def acsync():
            wait("scalar", "AC", C["AC"])

        import collections
        NCNT = collections.defaultdict(int)

        def lds(dst, src, ch, eng="sync"):
            NCNT[ch] += 16
            em(eng, lambda e, d=dst, s=src, h=SEMX[ch]: e.dma_start(
                out=d, in_=s).then_inc(h, 16))
            sched[eng].append(("i", ch, 16, _ln()))
            return NCNT[ch]

        def waitn(eng, ch, val):
            if val is None or (isinstance(val, int) and val <= 0):
                return
            if callable(val):
                def getv(g=val):
                    v = g()
                    if isinstance(v, tuple):
                        v = v[1]
                    return v or 0

                def f(e, h=SEMX[ch], gv=getv):
                    v = gv()
                    if v:
                        e.wait_ge(h, v)
                em(eng, f)
                sched[eng].append(("w", ch, getv, _ln()))
                return
            em(eng, lambda e, h=SEMX[ch], v=val: e.wait_ge(h, v))
            sched[eng].append(("w", ch, val, _ln()))

        def ld(dst, src, eng="sync"):
            return op(eng, lambda e, d=dst, s=src: e.dma_start(out=d, in_=s), "LD", 16)

        late = {}
        marks = {}

        def mark(st):
            marks[st] = {e: len(prog[e]) for e in prog}
            marks[st]["cnt"] = dict(C)
            marks[st]["dl"] = list(dl_cnt) if "dl_cnt" in dir() else None

        def lateget(key, kind):
            return lambda: (kind, late.get(key, 0))

        for i in range(4):
            c0, c1 = i * 1536, (i + 1) * 1536
            em("gpsimd", lambda e, c0=c0, c1=c1: e.dma_start(
                out=wqkvT_bf[:, c0:c1], in_=wqkvT[:, c0:c1]).then_inc(sCAST, 16))
        em("gpsimd", lambda e: e.dma_start(out=wgateT_bf, in_=wgateT).then_inc(sCAST, 16))
        em("gpsimd", lambda e: e.dma_start(out=woutT_bf, in_=woutT).then_inc(sCAST, 16))
        for _ in range(6):
            sched["gpsimd"].append(("i", None, 0, 0))
        CASTS_ALL = 6 * 16

        def wait_cast(eng):
            em(eng, lambda e: e.wait_ge(sCAST, CASTS_ALL))
            sched[eng].append(("i", None, 0, 0))
        cast_pts = {}

        for d_, s_ in [(identb[:], identb_in), (identf[:], identf_in),
                       (onesb[:], onesb_in), (onesf[:], onesf_in),
                       (idxT[:], idxT_in), (kdcolT[:], kdcolT_in),
                       (c256B[:], c256_in), (expoB[:], expo_in),
                       (shcol[:], shcol_in), (lshcol[:], lshcol_in),
                       (isb1[:], isb1_in), (epsc[:], eps_in),
                       (negsB[:], bass.AP(s_row.tensor, 0, [[0, 128], [1, NH]])),
                       (normw_c[:], normw),
                       (cT_sb[:], bass.AP(cT.tensor, 0, [[B, 128], [128 * B, FC], [1, B]]))]:
            lds(d_, s_, "CN")
        consts_ld = NCNT["CN"]

        waitn("vector", "CN", consts_ld)
        op("vector", lambda e: e.tensor_scalar(out=negsB[:], in0=negsB[:], scalar1=-1.0,
                                               scalar2=None, op0=AL.mult), "DV")
        negs_rdy = C["DV"]
        waitn("scalar", "CN", consts_ld)
        op("scalar", lambda e: e.activation(out=cT_sb[:], in_=cT_sb[:], func=AF.Silu), "AC")
        ct_rdy = C["AC"]
        wait("scalar", "DV", negs_rdy)
        for h in range(NH):
            op("scalar", lambda e, h=h: e.activation(
                out=bdj_all[:, h, :], in_=c256B[:], func=AF.Exp, scale=negsB[:, h:h + 1]))
            last = h == NH - 1
            op("scalar", lambda e, h=h: e.activation(
                out=coef_all[:, h, :], in_=expoB[:], func=AF.Exp, scale=negsB[:, h:h + 1]),
                "AC" if last else None)
        dbt_st = {}
        for h in range(NH):
            sl = h % 2
            op("scalar", lambda e, h=h: e.activation(
                out=kdcol_all[:, h, :], in_=kdcolT[:], func=AF.Exp,
                scale=negsB[:, h:h + 1]))
            if h >= 2:
                wait("scalar", "SC", dbt_st[h - 2])
            a = op("scalar", lambda e, h=h, sl=sl: e.activation(
                out=DBT[:, 2 * sl:2 * sl + 2, :], in_=idxT[:], func=AF.Exp,
                scale=negsB[:, h:h + 1]), "AC")
            wait("scalar", "AC", a)
            dbt_st[h] = op("scalar", lambda e, h=h, sl=sl: e.dma_start(
                out=bass.AP(dbt_dram.tensor, h * 128 * 512, [[512, 128], [1, 512]]),
                in_=DBT[:, 2 * sl:2 * sl + 2, :]), "SC", 16)
        kdcl_rdy = C["AC"]

        mod_ld = {}
        NMC = 12

        def modbuf(bs):
            t = big[:, 32 + 4 * bs:36 + 4 * bs, :].rearrange("p a b -> p (a b)")
            return t.bitcast(dt.float32).rearrange("p (a b) -> p a b", a=FC)

        def emit_mod_load(mc):
            bs = mc % 2
            if mc >= 2:
                wait("sync", "PE", late[("modmm", mc - 2)])
            s_ = bass.AP(wmod_s.tensor, mc * 128, [[1536, 128], [128 * 1536, FC], [1, 128]])
            mod_ld[mc] = ("SA" if bs == 0 else "SB", lds(modbuf(bs), s_,
                                                         "SA" if bs == 0 else "SB"))

        emit_mod_load(0)
        emit_mod_load(1)
        for mc in range(NMC):
            bs = mc % 2
            waitn("tensor", mod_ld[mc][0], mod_ld[mc][1])
            wait("tensor", "AC", ct_rdy)
            if mc >= 2:
                wait("tensor", "DV", late[("modev", mc - 2)])
            rhs_t = modbuf(bs)
            for fc in range(FC):
                lastm = fc == FC - 1
                op("tensor", lambda e, fc=fc, bs=bs, r=rhs_t, lastm=lastm: e.matmul(
                    mainA[0:2, bs, 0:128], cT_sb[:, fc, :], r[:, fc, :],
                    start=(fc == 0), stop=lastm), "PE" if lastm else None)
            late[("modmm", mc)] = C["PE"]
            wait("vector", "PE", lateget(("modmm", min(mc + 1, NMC - 1)), "PE"))
            if mc >= 2:
                waitn("vector", "MST", late[("modst", mc - 2)])
            late[("modev", mc)] = op("vector", lambda e, mc=mc, bs=bs: e.tensor_copy(
                out=modin2[:, bs, :], in_=mainA[0:2, bs, 0:128]), "DV")
            if mc + 2 < NMC:
                emit_mod_load(mc + 2)
            wait("sync", "DV", lateget(("modev", min(mc + 1, NMC - 1)), "DV"))
            late[("modst", mc)] = lds(mod_in[:, mc * 128:(mc + 1) * 128],
                                      modin2[:, bs, :], "MST")
        main_guard = {0: ("DV", late[("modev", NMC - 1)]), 1: None}
        modst = late[("modst", NMC - 1)]
        waitn("gpsimd", "MST", modst)
        em("gpsimd", lambda e: e.collective_compute(
            "AllGather", AL.bypass, replica_groups=[list(range(8))],
            ins=[mod_in.opt()], outs=[mod_all.opt()]).then_inc(sAGM, 1))
        C["CC"] += 1
        sched["gpsimd"].append(("i", "CC", 1, 0))
        modag = 1
        em("sync", lambda e: e.wait_ge(sAGM, 1))
        sched["sync"].append(("w", "CC", 1, 0))
        lds(modB0, bass.AP(mod_all.tensor, 0, [[2 * 1536, 8], [128, 12], [1, 128]]), "M1")
        modrows_ld = lds(modB1, bass.AP(mod_all.tensor, 1536,
                                        [[2 * 1536, 8], [128, 12], [1, 128]]), "M1")
        waitn("tensor", "M1", modrows_ld)
        op("tensor", lambda e: e.matmul(psQK[:, 0, 0:96], modB0, identf[0:96, 0:96],
                                        start=True, stop=True))
        op("tensor", lambda e: e.matmul(psQK[:, 1, 0:96], modB1, identf[0:96, 0:96],
                                        start=True, stop=True), "PE")
        modtr = C["PE"]
        wait("vector", "PE", modtr)
        op("vector", lambda e: e.tensor_scalar(out=modsel, in0=psQK[:, 1, 0:96],
                                               scalar1=isb1[:, 0:1], scalar2=None, op0=AL.mult))
        op("vector", lambda e: e.scalar_tensor_tensor(
            out=mod_cols[:], in0=psQK[:, 0, 0:96], scalar=isb1[:, 1:2], in1=modsel,
            op0=AL.mult, op1=AL.add), "DV")
        modc_rdy = C["DV"]
        if "dbg_mod" in dbg_outs:
            wait("sync", "DV", modc_rdy)
            ld(dbg_outs["dbg_mod"], mod_cols[:])
        if "dbg_wmodbf" in dbg_outs:
            wait_cast("sync")
            ld(dbg_outs["dbg_wmodbf"], wmod_bf)
        if "dbg_wqkvbf" in dbg_outs:
            wait_cast("sync")
            ld(dbg_outs["dbg_wqkvbf"], wqkvT_bf[:, 0:256])
        if "dbg_modin" in dbg_outs:
            waitn("sync", "MST", modst)
            ld(dbg_outs["dbg_modin"], mod_in)
        if "dbg_modall" in dbg_outs:
            wait("sync", "CC", modag)
            ld(dbg_outs["dbg_modall"], mod_all)
        op("vector", lambda e: e.tensor_scalar(out=dcol[:, :, 0], in0=mod_cols[:, 16:32],
                                               scalar1=1.0, scalar2=None, op0=AL.add))
        op("vector", lambda e: e.tensor_scalar(out=dcol[:, :, 1], in0=mod_cols[:, 64:80],
                                               scalar1=1.0, scalar2=None, op0=AL.add))
        for j, lo in ((2, 32), (3, 80)):
            op("vector", lambda e, lo=lo: e.tensor_scalar(
                out=gscr[:, :, 0], in0=mod_cols[:, lo:lo + 16], scalar1=0.25, scalar2=None,
                op0=AL.max))
            op("vector", lambda e, lo=lo: e.tensor_scalar(
                out=gscr[:, :, 1], in0=mod_cols[:, lo:lo + 16], scalar1=-0.25, scalar2=None,
                op0=AL.min))
            op("vector", lambda e: e.tensor_tensor(
                out=gscr[:, :, 0], in0=gscr[:, :, 0], in1=gscr[:, :, 1], op=AL.subtract))
            op("vector", lambda e, lo=lo, j=j: e.tensor_scalar(
                out=dcol[:, :, j], in0=mod_cols[:, lo:lo + 16], scalar1=0.0, scalar2=None,
                op0=AL.is_ge))
            op("vector", lambda e, j=j: e.tensor_tensor(
                out=dcol[:, :, j], in0=dcol[:, :, j], in1=gscr[:, :, 0], op=AL.mult))
            op("vector", lambda e, j=j: e.tensor_tensor(
                out=dcol[:, :, j], in0=dcol[:, :, j], in1=gscr[:, :, 1], op=AL.add), "DV")
        dcol_rdy = C["DV"]
        mark(1)

        def xslot(fc):
            t = big[:, 2 * fc:2 * fc + 2, :].rearrange("p a b -> p (a b)")
            return t.bitcast(dt.float32)
        for fc in range(FC):
            src = bass.AP(xT.tensor, fc * 128 * NT, [[NT, 128], [1, NT]])
            lds(xslot(fc), src, "XL")
        xl_all = NCNT["XL"]
        sqslot = [sqscr[:], scrb[:]]
        for fc in range(FC):
            waitn("scalar", "XL", xl_all)
            if fc >= 2:
                wait("scalar", "PE", lateget(("st1", fc - 2), "PE"))
            late[("xsq", fc)] = op("scalar", lambda e, fc=fc: e.activation(
                out=sqslot[fc % 2], in_=xslot(fc), func=AF.Square), "AC")
            waitn("vector", "XL", xl_all)
            late[("xcp", fc)] = op("vector", lambda e, fc=fc: e.tensor_copy(
                out=h_fm[:, fc, :], in_=xslot(fc)), "DV")
        for fc in range(FC):
            lag = min(fc + 1, FC - 1)
            wait("tensor", "DV", late[("xcp", lag)])
            wait("tensor", "AC", late[("xsq", lag)])
            if fc == 0:
                wait("tensor", "DV", modc_rdy)
            first, lastf = fc == 0, fc == FC - 1
            op("tensor", lambda e, fc=fc, first=first, lastf=lastf: e.matmul(
                stat_ap[0], onesb[:], h_fm[:, fc, 0:512], start=first, stop=lastf))
            op("tensor", lambda e, fc=fc, first=first, lastf=lastf: e.matmul(
                stat_ap[1], onesb[:], h_fm[:, fc, 512:1024], start=first, stop=lastf))
            op("tensor", lambda e, fc=fc, first=first, lastf=lastf: e.matmul(
                stat_ap[2], onesb[:], sqslot[fc % 2][:, 0:512], start=first, stop=lastf))
            late[("st1", fc)] = op("tensor", lambda e, fc=fc, lastf=lastf, first=first: e.matmul(
                stat_ap[3], onesb[:], sqslot[fc % 2][:, 512:1024], start=first, stop=lastf),
                "PE")
        ln1_stats = C["PE"]

        def replicate(src_row, dst, guardB):
            waitg("tensor", guardB)
            for half in range(2):
                op("tensor", lambda e, h=half, s=src_row: e.matmul(
                    mainB[:, h, :], onesf[:], s[:, h * 512:(h + 1) * 512],
                    start=True, stop=True), "PE" if half == 1 else None)
            wait("vector", "PE", C["PE"])
            return op("vector", lambda e, t=dst: e.tensor_copy(
                out=t, in_=mainB[:].rearrange("p a b -> p (a b)")), "DV")

        def stats_pipeline(epsi, stats_pt, rms_only, mean_dst, rstd_dst, guardB):
            wait("scalar", "PE", stats_pt)
            rep_m = None
            if not rms_only:
                op("scalar", lambda e: e.activation(out=rows[:, 0, 0:512], in_=stat_ap[0],
                                                    func=AF.Copy, scale=1.0 / DM), "AC")
                mean_rdy = op("scalar", lambda e: e.activation(
                    out=rows[:, 0, 512:1024], in_=stat_ap[1],
                    func=AF.Copy, scale=1.0 / DM), "AC")
                wait("tensor", "AC", mean_rdy)
                rep_m = replicate(rows[:, 0, :], mean_dst, guardB)
                wait("scalar", "PE", C["PE"])
                acsync()
                op("scalar", lambda e: e.activation(out=rows[:, 1, :], in_=rows[:, 0, :],
                                                    func=AF.Square), "AC")
                acsync()
            op("scalar", lambda e: e.activation(out=rows[:, 0, 0:512], in_=stat_ap[2],
                                                func=AF.Copy, scale=1.0 / DM), "AC")
            pss_free = op("scalar", lambda e: e.activation(
                out=rows[:, 0, 512:1024], in_=stat_ap[3],
                func=AF.Copy, scale=1.0 / DM), "AC")
            wait("vector", "AC", pss_free)
            if not rms_only:
                dv = op("vector", lambda e: e.tensor_tensor(
                    out=rows[:, 0, :], in0=rows[:, 0, :], in1=rows[:, 1, :],
                    op=AL.subtract), "DV")
                wait("scalar", "DV", dv)
            op("scalar", lambda e, epsi=epsi: e.activation(
                out=rows[:, 0, :], in_=rows[:, 0, :],
                func=AF.Sqrt, bias=epsc[0:1, epsi:epsi + 1]), "AC")
            wait("vector", "AC", C["AC"])
            rr = op("vector", lambda e: e.reciprocal(out=rows[:, 1, :], in_=rows[:, 0, :]), "DV")
            wait("tensor", "DV", rr)
            gb2 = ("DV", rep_m) if rep_m is not None else guardB
            rep_r = replicate(rows[:, 1, :], rstd_dst, gb2)
            return rep_r, pss_free

        ln1_rep, pss_free = stats_pipeline(0, ln1_stats, False, repA[:], repB[:],
                                           main_guard[1])
        main_guard[1] = ("DV", ln1_rep)
        if "dbg_reps" in dbg_outs:
            wait("sync", "DV", ln1_rep)
            ld(dbg_outs["dbg_reps"][:, 0, :], repA[:])
            ld(dbg_outs["dbg_reps"][:, 1, :], repB[:])
        wait("vector", "DV", ln1_rep)
        wait("vector", "DV", dcol_rdy)
        for fc in range(FC):
            op("vector", lambda e, fc=fc: e.tensor_tensor(
                out=scrb[:], in0=h_fm[:, fc, :], in1=repA[:], op=AL.subtract))
            op("vector", lambda e, fc=fc: e.tensor_tensor(
                out=h_fm[:, fc, :], in0=scrb[:], in1=repB[:], op=AL.mult))
            op("vector", lambda e, fc=fc: e.tensor_scalar(
                out=h_fm[:, fc, :], in0=h_fm[:, fc, :], scalar1=dcol[:, fc:fc + 1, 0],
                scalar2=mod_cols[:, fc:fc + 1], op0=AL.mult, op1=AL.add),
                "DV" if fc == FC - 1 else None)
        h_rdy = C["DV"]
        if "dbg_h" in dbg_outs:
            wait("sync", "DV", h_rdy)
            ld(dbg_outs["dbg_h"], h_fm[:])
        mark(2)

        dec_steps = []
        dl_cnt = [0, 0]
        ds_cnt = [0, 0]

        def emit_decode(n_ch, w_ap, sc_ap, o_ap, t0, t1, src16, srcsc, dst, width, store_l,
                        key):
            for i in range(n_ch):
                bs = i % 2
                if i >= 2:
                    em("gpsimd", lambda e, bs=bs, v=ds_cnt[bs] * 16: e.wait_ge(sDS[bs], v))
                    sched["gpsimd"].append(("w", "DS%d" % bs, ds_cnt[bs] * 16, 0))
                em("gpsimd", lambda e, i=i, bs=bs: e.dma_start(
                    out=w_ap[bs], in_=bass.AP(src16.tensor, 32 * i * width,
                                              [[width, 32], [0, 4], [1, width]])
                ).then_inc(sDL[bs], 16))
                em("gpsimd", lambda e, i=i, bs=bs: e.dma_start(
                    out=sc_ap[bs], in_=bass.AP(srcsc.tensor, i * width,
                                               [[0, 128], [1, width]])
                ).then_inc(sDL[bs], 16))
                C["GP"] += 32
                sched["gpsimd"].append(("i", "DL%d" % bs, 32, 0))
                dl_cnt[bs] += 2
                ldpt = (bs, dl_cnt[bs] * 16)

                def vec_step(i=i, bs=bs, ldpt=ldpt, key=key):
                    em("vector", lambda e, b=ldpt[0], v=ldpt[1]: e.wait_ge(sDL[b], v))
                    sched["vector"].append(("w", "DL%d" % ldpt[0], ldpt[1], 0))
                    waitn("vector", "CN", consts_ld)
                    op("vector", lambda e: e.tensor_scalar(
                        out=t0, in0=w_ap[bs], scalar1=shcol[:, 0:1], scalar2=7,
                        op0=AL.logical_shift_right, op1=AL.bitwise_and))
                    op("vector", lambda e: e.tensor_scalar(
                        out=t1, in0=t0, scalar1=28, scalar2=512,
                        op0=AL.add, op1=AL.mult))
                    op("vector", lambda e: e.tensor_scalar(
                        out=t0, in0=w_ap[bs], scalar1=lshcol[:, 0:1], scalar2=-32768,
                        op0=AL.logical_shift_left, op1=AL.bitwise_and))
                    op("vector", lambda e: e.tensor_scalar(
                        out=w_ap[bs].bitcast(dt.float16), in0=t1.bitcast(dt.float16),
                        scalar1=2.0, scalar2=1.0, op0=AL.mult, op1=AL.subtract))
                    op("vector", lambda e: e.tensor_tensor(
                        out=w_ap[bs].bitcast(dt.float16), in0=t1.bitcast(dt.float16),
                        in1=w_ap[bs].bitcast(dt.float16), op=AL.min))
                    op("vector", lambda e: e.tensor_tensor(
                        out=w_ap[bs], in0=w_ap[bs], in1=t0, op=AL.bitwise_xor))
                    late[(key, i)] = op("vector", lambda e: e.tensor_tensor(
                        out=o_ap[bs], in0=w_ap[bs].bitcast(dt.float16), in1=sc_ap[bs],
                        op=AL.mult), "DV")
                dec_steps.append(vec_step)
                wait("gpsimd", "DV", lateget((key, i), "DV"))
                em("gpsimd", lambda e, i=i, bs=bs: e.dma_start(
                    out=dst[i * 128:(i + 1) * 128, :], in_=o_ap[bs]).then_inc(sDS[bs], 16))
                C["GP"] += 16
                sched["gpsimd"].append(("i", "DS%d" % bs, 16, 0))
                ds_cnt[bs] += 1
                store_l.append(C["GP"])

        gu_store = []
        emit_decode(FC, dec_w, dec_sc, dec_o, dec_t0, dec_t1, gu16, gusc, gu_loc, GU_O,
                    gu_store, "gudec")
        em("gpsimd", lambda e, v=ds_cnt[0] * 16: e.wait_ge(sDS[0], v))
        sched["gpsimd"].append(("w", "DS0", ds_cnt[0] * 16, 0))
        em("gpsimd", lambda e, v=ds_cnt[1] * 16: e.wait_ge(sDS[1], v))
        sched["gpsimd"].append(("w", "DS1", ds_cnt[1] * 16, 0))
        em("gpsimd", lambda e: e.collective_compute(
            "AllGather", AL.bypass, replica_groups=[list(range(8))],
            ins=[gu_loc.opt()], outs=[gu_all.opt()]).then_inc(sAGG, 1))
        C["CC"] += 1
        sched["gpsimd"].append(("i", "CC", 1, 0))
        gu_ag = ("AGG", 1)
        if "dbg_guw" in dbg_outs:
            em("sync", lambda e: e.wait_ge(sAGG, 1))
            sched["sync"].append(("w", "CC", 2, 0))
            ld(dbg_outs["dbg_guw"], gu_loc)
        dn_w = [dview(16896, [128, DN_F], dt.int16), dview(17408, [128, DN_F], dt.int16)]
        dn_t0 = dview(17920, [128, DN_F], dt.int16)
        dn_t1 = dview(18432, [128, DN_F], dt.int16)
        dn_sc = [dview(18944, [128, DN_F], dt.float16), dview(19456, [128, DN_F], dt.float16)]
        dn_o = [dview(19968, [128, DN_F], dt.bfloat16), dview(20480, [128, DN_F], dt.bfloat16)]
        dn_store = []
        emit_decode(NIC, dn_w, dn_sc, dn_o, dn_t0, dn_t1, dn16, dnsc, dn_loc, DN_F, dn_store,
                    "dndec")
        for step in dec_steps:
            step()
        em("gpsimd", lambda e, v=ds_cnt[0] * 16: e.wait_ge(sDS[0], v))
        sched["gpsimd"].append(("w", "DS0", ds_cnt[0] * 16, 0))
        em("gpsimd", lambda e, v=ds_cnt[1] * 16: e.wait_ge(sDS[1], v))
        sched["gpsimd"].append(("w", "DS1", ds_cnt[1] * 16, 0))
        em("gpsimd", lambda e: e.collective_compute(
            "AllGather", AL.bypass, replica_groups=[list(range(8))],
            ins=[dn_loc.opt()], outs=[dn_all.opt()]).then_inc(sAGD, 1))
        C["CC"] += 1
        sched["gpsimd"].append(("i", "CC", 1, 0))
        dn_ag = ("AGD", 1)
        if "dbg_dnw" in dbg_outs:
            em("sync", lambda e: e.wait_ge(sAGD, 1))
            sched["sync"].append(("w", "CC", 3, 0))
            ld(dbg_outs["dbg_dnw"], dn_loc)

        mark(3)
        o_all = big[:, 0:16, :]
        q_all = big[:, 16:32, :]
        k_fm_s = [big[:, 32, :], big[:, 37, :]]
        kdk_tm_s = [big[:, 33, :].rearrange("p (a b) -> p a b", a=8),
                    big[:, 38, :].rearrange("p (a b) -> p a b", a=8)]
        v_tm_s = [big[:, 34, :].rearrange("p (a b) -> p a b", a=8),
                  big[:, 39, :].rearrange("p (a b) -> p a b", a=8)]
        qkD_s = [big[:, 35, :].rearrange("p (a b) -> p a b", a=2),
                 big[:, 40, :].rearrange("p (a b) -> p a b", a=2)]
        vscr = big[:, 36, :]

        qkv_ld = []
        for ck in range(48):
            bs = ck % 2
            wait_cast("sync")
            if ck >= 2:
                wait("sync", "PE", lateget(("qkvuse", ck - 2), "PE"))
            src = bass.AP(wqkvT_bf.tensor, ck * 128, [[6144, 128], [128 * 6144, FC], [1, 128]])
            dst = strm[:, bs * 2:bs * 2 + 2, :].rearrange("p a b -> p (a b)")
            dst = dst.rearrange("p (a b) -> p a b", a=FC)
            qkv_ld.append(("SA" if bs == 0 else "SB", lds(dst, src, "SA" if bs == 0 else "SB")))

        def strm_tile(bs):
            t = strm[:, bs * 2:bs * 2 + 2, :].rearrange("p a b -> p (a b)")
            return t.rearrange("p (a b) -> p a b", a=FC)

        def proj(oc_global, psum, rhs_src, guard, ldpt_list, usekey):
            ck = oc_global
            bs = ck % 2
            waitn("tensor", ldpt_list[ck][0], ldpt_list[ck][1])
            waitg("tensor", guard)
            for fc in range(FC):
                for th in range(2):
                    lastm = fc == FC - 1 and th == 1
                    op("tensor", lambda e, fc=fc, th=th, bs=bs, ps=psum, r=rhs_src:
                       e.matmul(ps[:, th, :], strm_tile(bs)[:, fc, :],
                                r[:, fc, th * 512:(th + 1) * 512],
                                start=(fc == 0), stop=(fc == FC - 1)),
                       "PE" if lastm else None)
            late[(usekey, ck)] = max(late.get((usekey, ck), 0), C["PE"])
            return C["PE"]

        kc_st = []
        qk_ev = modc_rdy
        in_ev = None
        kvup = None
        prev_vtm = {0: None, 1: None}
        prev_kdk = {0: None, 1: None}
        dbt_rdy_s = {}
        wait("tensor", "DV", h_rdy)

        def proj_head(h):
            sl = h % 2
            k_fm = k_fm_s[sl]
            kdk_tm = kdk_tm_s[sl]
            v_tm = v_tm_s[sl]
            qpt = proj(3 * h + 0, mainA, h_fm, main_guard[0], qkv_ld, "qkvuse")
            wait("scalar", "PE", qpt)
            main_guard[0] = ("AC", op("scalar", lambda e, h=h: e.activation(
                out=q_all[:, h, :], in_=mainA[:].rearrange("p a b -> p (a b)"),
                func=AF.Silu), "AC"))
            kpt = proj(3 * h + 1, mainB, h_fm, main_guard[1], qkv_ld, "qkvuse")
            wait("scalar", "PE", kpt)
            main_guard[1] = ("AC", op("scalar", lambda e, k_fm=k_fm: e.activation(
                out=k_fm[:], in_=mainB[:].rearrange("p a b -> p (a b)"),
                func=AF.Silu), "AC"))
            k_ev = main_guard[1][1]
            if h >= 2:
                wait("scalar", "DV", lateget(("qkdone", h - 2), "DV"))
            dbt_rdy_s[h] = op("scalar", lambda e, h=h, sl=sl: e.dma_start(
                out=DBT[:, 2 * sl:2 * sl + 2, :],
                in_=bass.AP(dbt_dram.tensor, h * 128 * 512, [[512, 128], [1, 512]])),
                "SC", 16)
            vpt = proj(3 * h + 2, mainA, h_fm, main_guard[0], qkv_ld, "qkvuse")
            wait("scalar", "PE", vpt)
            main_guard[0] = ("AC", op("scalar", lambda e: e.activation(
                out=vscr[:], in_=mainA[:].rearrange("p a b -> p (a b)"),
                func=AF.Silu), "AC"))
            v_ev = main_guard[0][1]

            wait("tensor", "AC", k_ev)
            if h == 0:
                wait("tensor", "AC", pss_free)
            if prev_vtm[sl] is not None:
                wait("tensor", "DV", prev_vtm[sl])
            if prev_vtm[1 - sl] is not None:
                wait("tensor", "DV", prev_vtm[1 - sl])
            kdk_ev = None
            for half in range(2):
                for i in range(4):
                    g = half * 4 + i
                    op("tensor", lambda e, g=g, i=i, k_fm=k_fm: e.matmul(
                        psT[:, i, :], k_fm[:, g * 128:(g + 1) * 128], identb[:],
                        start=True, stop=True), "PE" if i == 3 else None)
                ktr = C["PE"]
                wait("vector", "PE", ktr)
                if half == 0:
                    if h == 0:
                        wait("vector", "AC", kdcl_rdy)
                    wait("vector", "SC", dbt_rdy_s[h])
                for i in range(4):
                    g = half * 4 + i
                    op("vector", lambda e, g=g, i=i, h=h, kdk_tm=kdk_tm: e.tensor_scalar(
                        out=kdk_tm[:, g, :], in0=psT[:, i, :],
                        scalar1=kdcol_all[:, h, (g % 2):(g % 2) + 1], scalar2=None,
                        op0=AL.mult), "DV" if i == 3 else None)
                kdk_ev = C["DV"]
                if half == 0:
                    wait("tensor", "DV", kdk_ev)
            prev_kdk[sl] = kdk_ev
            late[("kdkdone", h)] = kdk_ev
            wait("tensor", "AC", v_ev)
            wait("tensor", "DV", kdk_ev)
            vtm_ev = None
            for half in range(2):
                for i in range(4):
                    g = half * 4 + i
                    op("tensor", lambda e, g=g, i=i: e.matmul(
                        psT[:, i, :], vscr[:, g * 128:(g + 1) * 128], identb[:],
                        start=True, stop=True), "PE" if i == 3 else None)
                vtr = C["PE"]
                wait("vector", "PE", vtr)
                vtm_ev = op("vector", lambda e, half=half, v_tm=v_tm: e.tensor_copy(
                    out=v_tm[:, half * 4:half * 4 + 4, :].rearrange("p a b -> p (a b)"),
                    in_=psT[:].rearrange("p a b -> p (a b)")), "DV")
                if half == 0:
                    wait("tensor", "DV", vtm_ev)
            prev_vtm[sl] = vtm_ev
            late[("projdone", h)] = (kdk_ev, vtm_ev)

        def attn_head(h):
            nonlocal qk_ev, in_ev, kvup
            sl = h % 2
            k_fm = k_fm_s[sl]
            kdk_tm = kdk_tm_s[sl]
            v_tm = v_tm_s[sl]
            qkD = qkD_s[sl]
            kdk_ev, vtm_ev = late[("projdone", h)]
            for j in range(4):
                wait("tensor", "DV", qk_ev)
                for nt in range(2):
                    op("tensor", lambda e, j=j, nt=nt, h=h, k_fm=k_fm: e.matmul(
                        psQK[:, nt, :],
                        k_fm[:, j * 256 + nt * 128: j * 256 + nt * 128 + 128],
                        q_all[:, h, j * 256:(j + 1) * 256], start=True, stop=True),
                        "PE" if nt == 1 else None)
                qkpt = C["PE"]
                wait("vector", "PE", qkpt)
                for nt in range(2):
                    op("vector", lambda e, nt=nt, qkD=qkD, sl=sl: e.tensor_tensor(
                        out=qkD[:, nt, 0:256], in0=psQK[:, nt, :],
                        in1=DBT[:, 2 * sl + nt, :],
                        op=AL.mult), "DV" if nt == 1 else None)
                qk_ev = C["DV"]
                wait("tensor", "DV", qk_ev)
                wait("tensor", "DV", in_ev)
                wait("tensor", "DV", kvup)
                wait("tensor", "DV", vtm_ev)
                for nt in range(2):
                    op("tensor", lambda e, j=j, nt=nt, v_tm=v_tm, qkD=qkD: e.matmul(
                        psIN[:, 0, :], v_tm[:, j * 2 + nt, :], qkD[:, nt, 0:256],
                        start=(nt == 0), stop=(nt == 1)), "PE" if nt == 1 else None)
                inpt = C["PE"]
                wait("vector", "PE", inpt)
                in_ev = op("vector", lambda e, j=j, h=h: e.tensor_copy(
                    out=o_all[:, h, j * 256:(j + 1) * 256], in_=psIN[:, 0, :]), "DV")
                wait("tensor", "DV", in_ev)
                wait("tensor", "DV", kdk_ev)
                for nt in range(2):
                    op("tensor", lambda e, j=j, nt=nt, kdk_tm=kdk_tm, v_tm=v_tm: e.matmul(
                        psIN[:, 1, 0:128], kdk_tm[:, j * 2 + nt, :], v_tm[:, j * 2 + nt, :],
                        start=(nt == 0), stop=(nt == 1)), "PE" if nt == 1 else None)
                cpt = C["PE"]
                wait("vector", "PE", cpt)
                kvb = kv[:, sl, :]
                if j == 0:
                    if h > 1:
                        wait("vector", "SC", kc_st[h - 2])
                    kvup_pt = op("vector", lambda e, kvb=kvb: e.tensor_copy(
                        out=kvb, in_=psIN[:, 1, 0:128]), "DV")
                else:
                    kvup_pt = op("vector", lambda e, h=h, kvb=kvb: e.scalar_tensor_tensor(
                        out=kvb, in0=kvb, scalar=bdj_all[:, h, 0:1],
                        in1=psIN[:, 1, 0:128], op0=AL.mult, op1=AL.add), "DV")
                kvup = kvup_pt
                if j < 3:
                    op("vector", lambda e, h=h, j=j, kvb=kvb: e.tensor_copy(
                        out=lall[:, 3 * h + j, :], in_=kvb), "DV")
            late[("qkdone", h)] = qk_ev
            wait("scalar", "DV", C["DV"])
            kc_st.append(op("scalar", lambda e, h=h, sl=sl: e.dma_start(
                out=kc_in[h * HD:(h + 1) * HD, :], in_=kv[:, sl, :]), "SC", 16))

        proj_head(0)
        for h in range(NH):
            if h + 1 < NH:
                proj_head(h + 1)
            attn_head(h)
        if "dbg_kc" in dbg_outs:
            wait("sync", "SC", kc_st[-1])
            ld(dbg_outs["dbg_kc"], kc_in)

        mark(4)
        wait("gpsimd", "SC", kc_st[-1])
        em("gpsimd", lambda e: e.collective_compute(
            "AllGather", AL.bypass, replica_groups=[[0, 1, 2, 3], [4, 5, 6, 7]],
            ins=[kc_in.opt()], outs=[kc_all.opt()]).then_inc(sAGK, 1))
        C["CC"] += 1
        sched["gpsimd"].append(("i", "CC", 1, 0))
        kc_ag = C["CC"]

        em("sync", lambda e: e.wait_ge(sAGG, 1))
        sched["sync"].append(("w", "CC", 2, 0))
        arr_ld = lds(arrB4, arrB4_in, "M1")
        inter_ev = {0: ("DV", qk_ev), 1: ("DV", max(in_ev or 0, kvup or 0))}
        prev_sj_use = None
        prev_kvin = None
        for h in range(NH):
            em("scalar", lambda e: e.wait_ge(sAGK, 1))
            sched["scalar"].append(("w", "CC", kc_ag, 0))
            if prev_kvin is not None:
                wait("scalar", "DV", prev_kvin)
            kcl = op("scalar", lambda e, h=h: e.dma_start(
                out=kcbuf[:], in_=bass.AP(kc_all.tensor, h * HD * HD,
                                          [[HD, 128], [NH * HD * HD, 4], [1, HD]])), "SC", 16)
            if h == 0:
                waitn("scalar", "M1", arr_ld)
            op("scalar", lambda e, h=h: e.activation(
                out=qdB4, in_=arrB4, func=AF.Exp, scale=negsB[:, h:h + 1]), "AC")
            qd_rdy = C["AC"]
            wait("vector", "SC", kcl)
            op("vector", lambda e, h=h: e.tensor_scalar(
                out=kvin[:], in0=kcbuf[:, 0, :], scalar1=coef_all[:, h, 0:1],
                scalar2=None, op0=AL.mult))
            for g in (1, 2):
                op("vector", lambda e, h=h, g=g: e.scalar_tensor_tensor(
                    out=kvin[:], in0=kcbuf[:, g, :], scalar=coef_all[:, h, g:g + 1],
                    in1=kvin[:], op0=AL.mult, op1=AL.add),
                    "DV" if g == 2 else None)
            prev_kvin = C["DV"]
            if prev_sj_use is not None:
                wait("vector", "PE", prev_sj_use)
            op("vector", lambda e: e.tensor_copy(out=Sj[:, 0, :], in_=kvin[:]))
            for j in (1, 2, 3):
                op("vector", lambda e, h=h, j=j: e.scalar_tensor_tensor(
                    out=Sj[:, j, :], in0=kvin[:], scalar=bdj_all[:, h, j - 1:j],
                    in1=lall[:, 3 * h + j - 1, :], op0=AL.mult, op1=AL.add),
                    "DV" if j == 3 else None)
            s_rdy = C["DV"]
            for j in range(4):
                ps = psQK if j % 2 == 0 else psIN
                wait("tensor", "DV", s_rdy)
                waitg("tensor", inter_ev[j % 2])
                op("tensor", lambda e, h=h, j=j, ps=ps: e.matmul(
                    ps[:, 0, :], Sj[:, j, :], q_all[:, h, j * 256:(j + 1) * 256],
                    start=True, stop=True), "PE")
                ipt = C["PE"]
                if j == 3:
                    prev_sj_use = ipt
                wait("vector", "PE", ipt)
                wait("vector", "AC", qd_rdy)
                op("vector", lambda e, j=j, ps=ps: e.tensor_tensor(
                    out=scrb[:, 0:256], in0=ps[:, 0, :], in1=qdB4[:, j * 256:(j + 1) * 256],
                    op=AL.mult))
                inter_ev[j % 2] = ("DV", op("vector", lambda e, h=h, j=j: e.tensor_tensor(
                    out=o_all[:, h, j * 256:(j + 1) * 256],
                    in0=o_all[:, h, j * 256:(j + 1) * 256],
                    in1=scrb[:, 0:256], op=AL.add), "DV"))
        o_rdy = C["DV"]
        if "dbg_o" in dbg_outs:
            wait("sync", "DV", o_rdy)
            ld(dbg_outs["dbg_o"], o_all)
        mark(5)

        wait("scalar", "AC", pss_free)
        for fc in range(FC):
            wait("scalar", "DV", o_rdy)
            if fc >= 1:
                wait("scalar", "PE", lateget(("st2", fc - 1), "PE"))
            a = op("scalar", lambda e, fc=fc: e.activation(
                out=sqscr[:], in_=o_all[:, fc, :], func=AF.Square), "AC")
            wait("tensor", "AC", a)
            if fc == 0:
                wait("tensor", "AC", pss_free)
                wait("tensor", "DV", o_rdy)
                wait("tensor", "DV", prev_vtm[0])
                wait("tensor", "DV", prev_vtm[1])
            first, lastf = fc == 0, fc == FC - 1
            op("tensor", lambda e, first=first, lastf=lastf: e.matmul(
                stat_ap[2], onesb[:], sqscr[:, 0:512], start=first, stop=lastf))
            late[("st2", fc)] = op("tensor", lambda e, lastf=lastf, first=first: e.matmul(
                stat_ap[3], onesb[:], sqscr[:, 512:1024], start=first, stop=lastf), "PE")
        rms_stats = C["PE"]
        rms_rep, pss_free = stats_pipeline(1, rms_stats, True, None, repB[:],
                                           main_guard[1])
        main_guard[1] = ("DV", rms_rep)

        g_fm = q_all
        gate_ld = []
        for ck in range(16):
            bs = ck % 2
            wait_cast("sync")
            if ck < 2:
                wait("sync", "PE", lateget(("qkvuse", 46 + ck), "PE"))
            else:
                wait("sync", "PE", lateget(("gateuse", ck - 2), "PE"))
            src = bass.AP(wgateT_bf.tensor, ck * 128, [[DM, 128], [128 * DM, FC], [1, 128]])
            dst = strm[:, bs * 2:bs * 2 + 2, :].rearrange("p a b -> p (a b)")
            dst = dst.rearrange("p (a b) -> p a b", a=FC)
            gate_ld.append(("SA" if bs == 0 else "SB", lds(dst, src, "SA" if bs == 0 else "SB")))
        first_gate = True
        for oc in range(FC):
            ps = mainA if oc % 2 == 0 else mainB
            guard = main_guard[oc % 2]
            if first_gate:
                wait("tensor", "DV", o_rdy)
                first_gate = False
            pt = proj(oc, ps, h_fm, guard, gate_ld, "gateuse")
            wait("scalar", "PE", pt)
            main_guard[oc % 2] = ("AC", op("scalar", lambda e, oc=oc, ps=ps: e.activation(
                out=g_fm[:, oc, :], in_=ps[:].rearrange("p a b -> p (a b)"),
                func=AF.Sigmoid), "AC"))
        gate_done = main_guard[1][1]
        wait("vector", "AC", main_guard[0][1])
        wait("vector", "AC", gate_done)
        wait("vector", "DV", rms_rep)
        for fc in range(FC):
            op("vector", lambda e, fc=fc: e.tensor_tensor(
                out=scrb[:], in0=o_all[:, fc, :], in1=repB[:], op=AL.mult))
            op("vector", lambda e, fc=fc: e.scalar_tensor_tensor(
                out=o_all[:, fc, :], in0=scrb[:], scalar=normw_c[:, fc:fc + 1],
                in1=g_fm[:, fc, :], op0=AL.mult, op1=AL.mult),
                "DV" if fc == FC - 1 else None)
        gated_rdy = C["DV"]

        wout_ld = {}
        xo_ld = {}

        def emit_outproj_loads(fc):
            bs = fc % 2
            wait_cast("sync")
            if fc < 2:
                wait("sync", "PE", lateget(("gateuse", 14 + fc), "PE"))
            else:
                wait("sync", "PE", late[("woutuse", fc - 2)])
            s_ = bass.AP(woutT_bf.tensor, fc * 128, [[DM, 128], [128 * DM, FC], [1, 128]])
            d_ = strm[:, bs * 2:bs * 2 + 2, :].rearrange("p a b -> p (a b)")
            d_ = d_.rearrange("p (a b) -> p a b", a=FC)
            wout_ld[fc] = ("SA" if bs == 0 else "SB", lds(d_, s_, "SA" if bs == 0 else "SB"))
            if fc >= 2:
                wait("sync", "DV", late[("x2ev", fc - 2)])
            s_ = bass.AP(xT.tensor, fc * 128 * NT, [[NT, 128], [1, NT]])
            xo_ld[fc] = lds(xorig[bs], s_, "XOa" if bs == 0 else "XOb")

        emit_outproj_loads(0)
        emit_outproj_loads(1)
        x2st = []
        for fc in range(FC):
            bs = fc % 2
            ps = mainA if fc % 2 == 0 else mainB
            wait("tensor", "DV", gated_rdy)
            waitn("tensor", wout_ld[fc][0], wout_ld[fc][1])
            waitg("tensor", main_guard[fc % 2])
            wbs = fc % 2
            for oc in range(FC):
                for th in range(2):
                    lastm = oc == FC - 1 and th == 1

                    def mm(e, fc=fc, oc=oc, th=th, wbs=wbs, ps=ps):
                        t = strm_tile(wbs)
                        return e.matmul(ps[:, th, :], t[:, oc, :],
                                        o_all[:, oc, th * 512:(th + 1) * 512],
                                        start=(oc == 0), stop=(oc == FC - 1))
                    op("tensor", mm, "PE" if lastm else None)
            late[("woutuse", fc)] = C["PE"]
            opj = C["PE"]
            wait("vector", "PE", opj)
            waitn("vector", "XOa" if fc % 2 == 0 else "XOb", xo_ld[fc])
            if fc >= 2:
                waitn("vector", "SPa" if fc % 2 == 0 else "SPb",
                      lateget(("x2st", fc - 2), "LD"))
            op("vector", lambda e, fc=fc, bs=bs, ps=ps: e.scalar_tensor_tensor(
                out=xtile[bs], in0=ps[:].rearrange("p a b -> p (a b)"),
                scalar=dcol[:, fc:fc + 1, 2], in1=xorig[bs], op0=AL.mult, op1=AL.add), "DV")
            x2ev = C["DV"]
            late[("x2ev", fc)] = x2ev
            main_guard[fc % 2] = ("DV", x2ev)
            op("vector", lambda e, fc=fc, bs=bs: e.tensor_copy(
                out=h_fm[:, fc, :], in_=xtile[bs]), "DV")
            x2bf = C["DV"]
            wait("scalar", "DV", x2ev)
            if fc >= 1:
                wait("scalar", "PE", lateget(("st3", fc - 1), "PE"))
            a = op("scalar", lambda e, bs=bs: e.activation(
                out=sqscr[:], in_=xtile[bs], func=AF.Square), "AC")
            wait("tensor", "DV", x2bf)
            wait("tensor", "AC", a)
            if fc == 0:
                wait("tensor", "AC", pss_free)
            first, lastf = fc == 0, fc == FC - 1
            op("tensor", lambda e, fc=fc, first=first, lastf=lastf: e.matmul(
                stat_ap[0], onesb[:], h_fm[:, fc, 0:512], start=first, stop=lastf))
            op("tensor", lambda e, fc=fc, first=first, lastf=lastf: e.matmul(
                stat_ap[1], onesb[:], h_fm[:, fc, 512:1024], start=first, stop=lastf))
            op("tensor", lambda e, fc=fc, first=first, lastf=lastf: e.matmul(
                stat_ap[2], onesb[:], sqscr[:, 0:512], start=first, stop=lastf))
            late[("st3", fc)] = op("tensor", lambda e, fc=fc, lastf=lastf, first=first: e.matmul(
                stat_ap[3], onesb[:], sqscr[:, 512:1024], start=first, stop=lastf), "PE")
            wait("sync", "DV", x2ev)
            st = lds(bass.AP(x2_dram.tensor, fc * 128 * NT, [[NT, 128], [1, NT]]),
                     xtile[bs], "SPa" if bs == 0 else "SPb")
            late[("x2st", fc)] = st
            x2st.append(st)
            if fc + 2 < FC:
                emit_outproj_loads(fc + 2)
        ln2_stats = C["PE"]
        if "dbg_x2" in dbg_outs:
            waitn("sync", "SPa", lambda: NCNT["SPa"])
            waitn("sync", "SPb", lambda: NCNT["SPb"])
            ld(dbg_outs["dbg_x2"], x2_dram)
        ln2_rep, pss_free = stats_pipeline(0, ln2_stats, False, repA[:], repB[:],
                                           main_guard[1])
        main_guard[1] = ("DV", ln2_rep)
        wait("vector", "DV", ln2_rep)
        for fc in range(FC):
            op("vector", lambda e, fc=fc: e.tensor_tensor(
                out=scrb[:], in0=h_fm[:, fc, :], in1=repA[:], op=AL.subtract))
            op("vector", lambda e, fc=fc: e.tensor_tensor(
                out=h_fm[:, fc, :], in0=scrb[:], in1=repB[:], op=AL.mult))
            op("vector", lambda e, fc=fc: e.tensor_scalar(
                out=h_fm[:, fc, :], in0=h_fm[:, fc, :], scalar1=dcol[:, fc:fc + 1, 1],
                scalar2=mod_cols[:, 48 + fc:48 + fc + 1], op0=AL.mult, op1=AL.add),
                "DV" if fc == FC - 1 else None)
        h2_rdy = C["DV"]
        mark(6)

        act = big[:, 0:NIC, :]

        def gu_tile(bs):
            t = strm[:, bs * 2:bs * 2 + 2, :].rearrange("p a b -> p (a b)")
            return t.rearrange("p (a b) -> p a b", a=FC)

        gu_ldp = {}

        def emit_gu_load(i, half):
            oc = i if half == 0 else 44 + i
            bs = half
            em("sync", lambda e: e.wait_ge(sAGG, 1))
            sched["sync"].append(("w", "CC", 2, 0))
            if i == 0:
                wait("sync", "PE", lateget(("woutuse", 14 + half), "PE"))
            else:
                wait("sync", "PE", late[("guuse", (i - 1) if half == 0 else 44 + (i - 1))])
            rnk, jj = oc // 11, oc % 11
            s_ = bass.AP(gu_all.tensor, rnk * DM * GU_O + jj * 128,
                         [[GU_O, 128], [128 * GU_O, FC], [1, 128]])
            gu_ldp[oc] = ("SA" if bs == 0 else "SB", lds(gu_tile(bs), s_,
                                                         "SA" if bs == 0 else "SB"))

        emit_gu_load(0, 0)
        emit_gu_load(0, 1)
        for i in range(NIC):
            for half, oc in ((0, i), (1, 44 + i)):
                bs = half
                ps = mainA if half == 0 else mainB
                waitn("tensor", gu_ldp[oc][0], gu_ldp[oc][1])
                if i == 0 and half == 0:
                    wait("tensor", "DV", h2_rdy)
                waitg("tensor", main_guard[half])
                for fc in range(FC):
                    for th in range(2):
                        lastm = fc == FC - 1 and th == 1
                        op("tensor", lambda e, fc=fc, th=th, bs=bs, ps=ps: e.matmul(
                            ps[:, th, :], gu_tile(bs)[:, fc, :],
                            h_fm[:, fc, th * 512:(th + 1) * 512],
                            start=(fc == 0), stop=(fc == FC - 1)), "PE" if lastm else None)
                late[("guuse", oc)] = C["PE"]
                if i + 1 < NIC:
                    emit_gu_load(i + 1, half)
                if half == 0:
                    gpt = C["PE"]
                else:
                    upt = C["PE"]
            wait("scalar", "PE", gpt)
            main_guard[0] = ("AC", op("scalar", lambda e: e.activation(
                out=scrb[:], in_=mainA[:].rearrange("p a b -> p (a b)"), func=AF.Silu), "AC"))
            wait("vector", "PE", upt)
            wait("vector", "AC", main_guard[0][1])
            main_guard[1] = ("DV", op("vector", lambda e, i=i: e.tensor_tensor(
                out=act[:, i, :], in0=mainB[:].rearrange("p a b -> p (a b)"),
                in1=scrb[:], op=AL.mult), "DV"))
        act_rdy = main_guard[1][1]
        if "dbg_act" in dbg_outs:
            wait("sync", "DV", act_rdy)
            ld(dbg_outs["dbg_act"], act)
        mark(7)

        y_fm = h_fm

        def dn_tile(q):
            flat = strm[:].rearrange("p a b -> p (a b)")
            off = 0 if q % 2 == 0 else 2048
            return flat[:, off:off + 11 * 128].rearrange("p (a b) -> p a b", a=11)

        dn_ldp = {}
        qidx = 0
        for fc2 in range(FC):
            g_, coff = fc2 // 2, (fc2 % 2) * 128
            for q in range(4):
                em("sync", lambda e: e.wait_ge(sAGD, 1))
                sched["sync"].append(("w", "CC", 3, 0))
                if qidx < 2:
                    wait("sync", "PE", lateget(("guuse", 86 + qidx), "PE"))
                else:
                    wait("sync", "PE", lateget(("dnuse", qidx - 2), "PE"))
                src = bass.AP(dn_all.tensor,
                              g_ * I5 * DN_F + q * 11 * 128 * DN_F + coff,
                              [[DN_F, 128], [128 * DN_F, 11], [1, 128]])
                ch_ = "SA" if q % 2 == 0 else "SB"
                dn_ldp[(fc2, q)] = (ch_, lds(dn_tile(q), src, ch_))
                qidx += 1
        qidx = 0
        for fc2 in range(FC):
            ps = mainA if fc2 % 2 == 0 else mainB
            if fc2 == 0:
                wait("tensor", "DV", act_rdy)
            waitg("tensor", main_guard[fc2 % 2])
            for q in range(4):
                waitn("tensor", dn_ldp[(fc2, q)][0], dn_ldp[(fc2, q)][1])
                for icq in range(11):
                    ic = q * 11 + icq
                    for th in range(2):
                        lastm = icq == 10 and th == 1
                        op("tensor", lambda e, q=q, icq=icq, ic=ic, th=th, ps=ps: e.matmul(
                            ps[:, th, :], dn_tile(q)[:, icq, :],
                            act[:, ic, th * 512:(th + 1) * 512],
                            start=(ic == 0), stop=(ic == NIC - 1)), "PE" if lastm else None)
                late[("dnuse", qidx)] = C["PE"]
                qidx += 1
            ypt = C["PE"]
            wait("vector", "PE", ypt)
            yev = op("vector", lambda e, fc2=fc2, ps=ps: e.tensor_copy(
                out=y_fm[:, fc2, :], in_=ps[:].rearrange("p a b -> p (a b)")), "DV")
            main_guard[fc2 % 2] = ("DV", yev)
            wait("scalar", "DV", yev)
            if fc2 >= 1:
                wait("scalar", "PE", lateget(("st4", fc2 - 1), "PE"))
            a = op("scalar", lambda e, fc2=fc2: e.activation(
                out=sqscr[:], in_=y_fm[:, fc2, :], func=AF.Square), "AC")
            wait("tensor", "AC", a)
            if fc2 == 0:
                wait("tensor", "AC", pss_free)
            first, lastf = fc2 == 0, fc2 == FC - 1
            op("tensor", lambda e, first=first, lastf=lastf: e.matmul(
                stat_ap[2], onesb[:], sqscr[:, 0:512], start=first, stop=lastf))
            late[("st4", fc2)] = op("tensor", lambda e, lastf=lastf, first=first: e.matmul(
                stat_ap[3], onesb[:], sqscr[:, 512:1024], start=first, stop=lastf), "PE")
        y_stats = C["PE"]
        if "dbg_y" in dbg_outs:
            wait("sync", "DV", main_guard[0][1])
            wait("sync", "DV", main_guard[1][1])
            ld(dbg_outs["dbg_y"], y_fm[:])
        y_rep, pss_free = stats_pipeline(1, y_stats, True, None, repF,
                                         main_guard[1])
        main_guard[1] = ("DV", y_rep)

        waitn("sync", "SPa", lambda: None if not x2st else NCNT["SPa"])
        waitn("sync", "SPb", lambda: NCNT["SPb"])
        x2re = []
        for fc in range(2):
            src = bass.AP(x2_dram.tensor, fc * 128 * NT, [[NT, 128], [1, NT]])
            x2re.append(lds(xorig[fc], src, "XOa" if fc % 2 == 0 else "XOb"))
        tr_guard = {0: None, 1: None}
        tr_pe = {}
        for fc in range(FC):
            bs = fc % 2
            wait("vector", "DV", y_rep)
            waitn("vector", "XOa" if fc % 2 == 0 else "XOb", x2re[fc])
            if fc >= 2:
                wait("vector", "PE", tr_pe[fc - 2])
            op("vector", lambda e, fc=fc: e.tensor_tensor(
                out=fscr, in0=y_fm[:, fc, :], in1=repF, op=AL.mult))
            op("vector", lambda e, fc=fc, bs=bs: e.scalar_tensor_tensor(
                out=xtile[bs], in0=fscr, scalar=dcol[:, fc:fc + 1, 3], in1=xorig[bs],
                op0=AL.mult, op1=AL.add), "DV")
            cmb = C["DV"]
            ps = mainA if fc % 2 == 0 else mainB
            wait("tensor", "DV", cmb)
            waitg("tensor", main_guard[fc % 2] if fc < 2 else tr_guard[fc % 2])
            for tc in range(8):
                op("tensor", lambda e, tc=tc, bs=bs, ps=ps: e.matmul(
                    ps[:].rearrange("p a b -> p (a b)")[:, tc * 128:(tc + 1) * 128],
                    xtile[bs][:, tc * 128:(tc + 1) * 128], identf[:],
                    start=True, stop=True), "PE" if tc == 7 else None)
            tr_pe[fc] = C["PE"]
            wait("vector", "PE", tr_pe[fc])
            if fc >= 2:
                waitn("vector", "OUTa" if fc % 2 == 0 else "OUTb",
                      lateget(("outst", fc - 2), "LD"))
            tr_guard[fc % 2] = ("DV", op("vector", lambda e, fc=fc, ps=ps, bs=bs: e.tensor_copy(
                out=stg2[:, bs, :], in_=ps[:].rearrange("p a b -> p (a b)")), "DV"))
            wait("sync", "DV", tr_guard[fc % 2][1])
            dst = bass.AP(y_out.tensor, fc * 128, [[DM, 128], [128 * DM, 8], [1, 128]])
            late[("outst", fc)] = lds(dst, stg2[:, bs, :].rearrange("p (a b) -> p a b", a=8),
                                      "OUTa" if bs == 0 else "OUTb")
            if fc + 2 < FC:
                src2 = bass.AP(x2_dram.tensor, (fc + 2) * 128 * NT, [[NT, 128], [1, NT]])
                x2re.append(lds(xorig[bs], src2, "XOa" if bs == 0 else "XOb"))
        waitn("sync", "OUTa", lambda: NCNT["OUTa"])
        waitn("sync", "OUTb", lambda: NCNT["OUTb"])
        wait("sync", "LD", C["LD"])
        for _ch in ("CN", "SA", "SB", "MST", "M1", "XL", "XOa", "XOb", "SPa", "SPb"):
            waitn("sync", _ch, lambda c=_ch: NCNT[c] or None)

        nc._sched = sched
        if stage < 99 and stage in marks:
            mk = marks[stage]
            for e in prog:
                del prog[e][mk[e]:]
                del sched[e][mk[e]:]
            cnt = mk["cnt"]
            em("sync", lambda e: e.wait_ge(sLD, cnt["LD"]))
            em("scalar", lambda e: e.wait_ge(sSC, cnt["SC"]))
            em("gpsimd", lambda e: e.wait_ge(sGP, 0) if cnt["GP"] == 0 else e.wait_ge(sCAST, CASTS_ALL))
        @block.sync
        def _(e):
            for fn in prog["sync"]:
                fn(e)

        @block.gpsimd
        def _(e):
            for fn in prog["gpsimd"]:
                fn(e)

        @block.tensor
        def _(e):
            for fn in prog["tensor"]:
                fn(e)

        @block.vector
        def _(e):
            for fn in prog["vector"]:
                fn(e)

        @block.scalar
        def _(e):
            for fn in prog["scalar"]:
                fn(e)

    return nc


def _consts(core):
    arr = np.arange(1, 257, dtype=np.float32)
    arrB4 = np.tile(np.tile(arr, 4)[None, :], (128, 1))
    m = np.arange(256)
    d = m[None, :] - m[:, None]
    idxT = np.where(d >= 0, d, 1e5).astype(np.float32)
    idxT = np.ascontiguousarray(idxT.reshape(2, 128, 256))
    idxT = np.ascontiguousarray(idxT.transpose(1, 0, 2))
    p = np.arange(128)
    kdcolT = np.zeros((128, 2), np.float32)
    for t in range(2):
        kdcolT[:, t] = 255.0 - (128 * t + p)
    c256 = np.zeros((128, 4), np.float32)
    c256[:, :] = np.array([256.0, 512.0, 768.0, 1024.0])[None, :]
    chunk = core % 4
    expo = np.full((128, 4), 1e9, np.float32)
    for g in range(4):
        if g < chunk:
            expo[:, g] = 1024.0 * (chunk - 1 - g)
    j = (p % 4).astype(np.int16)
    shcol = (4 * j).reshape(128, 1)
    lshcol = (12 - 4 * j).reshape(128, 1)
    isb1 = np.zeros((128, 2), np.float32)
    isb1[:, 0] = 1.0 if core >= 4 else 0.0
    isb1[:, 1] = 1.0 - isb1[:, 0]
    return dict(
        identb_in=np.eye(128, dtype=bf16),
        identf_in=np.eye(128, dtype=np.float32),
        onesb_in=np.ones((128, 1), dtype=bf16),
        onesf_in=np.ones((1, 128), dtype=np.float32),
        arrB4_in=arrB4.astype(np.float32), idxT_in=idxT, kdcolT_in=kdcolT,
        c256_in=c256, expo_in=expo, shcol_in=shcol, lshcol_in=lshcol,
        isb1_in=isb1,
        eps_in=np.array([[1e-5, 1e-6]], dtype=np.float32),
    )


def make_inmaps(inputs):
    x = np.asarray(inputs["L0_attn_hidden_states"], dtype=np.float32)
    slope = np.asarray(inputs["L0_attn_slope_rate"], dtype=np.float32).reshape(1, NH)
    wqkv = np.asarray(inputs["L0_attn_w_qkv"], dtype=np.float32)
    wgate = np.asarray(inputs["L0_attn_w_gate"], dtype=np.float32)
    wout = np.asarray(inputs["L0_attn_w_out"], dtype=np.float32)
    normw = np.ascontiguousarray(
        np.asarray(inputs["L0_attn_norm_w"], dtype=np.float32).reshape(FC, 128).T)
    gup = np.asarray(inputs["L0_ffn_gate_up_packed"], dtype=np.int32)
    gus = np.asarray(inputs["L0_ffn_gate_up_scales"], dtype=np.float16)
    dnp = np.asarray(inputs["L0_ffn_down_packed"], dtype=np.int32)
    dns = np.asarray(inputs["L0_ffn_down_scales"], dtype=np.float16)
    cc = np.asarray(inputs["L0_c"], dtype=np.float32)
    wmod = np.asarray(inputs["L0_wmod"], dtype=np.float32)

    wqkvT = np.ascontiguousarray(wqkv.T)
    wgateT = np.ascontiguousarray(wgate.T)
    woutT = np.ascontiguousarray(wout.T)
    cTm = np.ascontiguousarray(cc.T)
    gu16T = np.ascontiguousarray(gup.view(np.int16).T)
    dn16T = np.ascontiguousarray(dnp.view(np.int16).T)

    maps = []
    for r in range(8):
        b, ch = r // 4, r % 4
        t0 = ch * NT
        m = dict(
            xT=np.ascontiguousarray(x[b, t0:t0 + NT, :].T),
            cT=cTm,
            wmod_s=np.ascontiguousarray(wmod[:, 1536 * r:1536 * (r + 1)]),
            wqkvT=wqkvT, wgateT=wgateT, woutT=woutT,
            s_row=slope, normw=normw,
            gu16=np.ascontiguousarray(gu16T[:, GU_O * r:GU_O * (r + 1)]),
            gusc=np.ascontiguousarray(gus[:, GU_O * r:GU_O * (r + 1)]),
            dn16=np.ascontiguousarray(dn16T[:, DN_F * r:DN_F * (r + 1)]),
            dnsc=np.ascontiguousarray(dns[:, DN_F * r:DN_F * (r + 1)]),
        )
        m.update(_consts(r))
        maps.append(m)
    return maps


def kernel(**inputs):
    maps = make_inmaps(inputs)
    nc = build_nc()
    res = run_bass_kernel_spmd(nc, maps, core_ids=list(range(8)))
    out = np.zeros((B, N, DM), dtype=np.float32)
    for r in range(8):
        b, ch = r // 4, r % 4
        out[b, ch * NT:(ch + 1) * NT, :] = res.results[r]["y_out"]
    return out


# version 10
# speedup vs baseline: 1.0145x; 1.0145x over previous
import numpy as np
import ml_dtypes
import contextlib
import concourse.bass as bass
import concourse.mybir as mybir
from concourse.bass_utils import run_bass_kernel_spmd

dt = mybir.dt
AF = mybir.ActivationFunctionType
AL = mybir.AluOpType
bf16 = ml_dtypes.bfloat16

B, N, DM = 2, 4096, 2048
NH, HD = 16, 128
NT = 1024
FC = 16
I5 = 5632
GU_O = 1408
DN_F = 256
NIC = 44


def build_nc(dbg=(), stage=99):
    nc = bass.Bass("TRN2", target_bir_lowering=False)

    xT = nc.dram_tensor("xT", [DM, NT], dt.float32, kind="ExternalInput").ap()
    cT = nc.dram_tensor("cT", [DM, B], dt.float32, kind="ExternalInput").ap()
    wmod_s = nc.dram_tensor("wmod_s", [DM, 1536], dt.float32, kind="ExternalInput").ap()
    wqkvT = nc.dram_tensor("wqkvT", [DM, 6144], dt.float32, kind="ExternalInput").ap()
    wgateT = nc.dram_tensor("wgateT", [DM, DM], dt.float32, kind="ExternalInput").ap()
    woutT = nc.dram_tensor("woutT", [DM, DM], dt.float32, kind="ExternalInput").ap()
    s_row = nc.dram_tensor("s_row", [1, NH], dt.float32, kind="ExternalInput").ap()
    normw = nc.dram_tensor("normw", [128, FC], dt.float32, kind="ExternalInput").ap()
    gu16 = nc.dram_tensor("gu16", [512, GU_O], dt.int16, kind="ExternalInput").ap()
    gusc = nc.dram_tensor("gusc", [FC, GU_O], dt.float16, kind="ExternalInput").ap()
    dn16 = nc.dram_tensor("dn16", [1408, DN_F], dt.int16, kind="ExternalInput").ap()
    dnsc = nc.dram_tensor("dnsc", [NIC, DN_F], dt.float16, kind="ExternalInput").ap()
    identb_in = nc.dram_tensor("identb_in", [128, 128], dt.bfloat16, kind="ExternalInput").ap()
    identf_in = nc.dram_tensor("identf_in", [128, 128], dt.float32, kind="ExternalInput").ap()
    onesb_in = nc.dram_tensor("onesb_in", [128, 1], dt.bfloat16, kind="ExternalInput").ap()
    onesf_in = nc.dram_tensor("onesf_in", [1, 128], dt.float32, kind="ExternalInput").ap()
    arrB4_in = nc.dram_tensor("arrB4_in", [128, NT], dt.float32, kind="ExternalInput").ap()
    idxT_in = nc.dram_tensor("idxT_in", [128, 2, 256], dt.float32, kind="ExternalInput").ap()
    kdcolT_in = nc.dram_tensor("kdcolT_in", [128, 2], dt.float32, kind="ExternalInput").ap()
    c256_in = nc.dram_tensor("c256_in", [128, 4], dt.float32, kind="ExternalInput").ap()
    expo_in = nc.dram_tensor("expo_in", [128, 4], dt.float32, kind="ExternalInput").ap()
    shcol_in = nc.dram_tensor("shcol_in", [128, 1], dt.int16, kind="ExternalInput").ap()
    lshcol_in = nc.dram_tensor("lshcol_in", [128, 1], dt.int16, kind="ExternalInput").ap()
    isb1_in = nc.dram_tensor("isb1_in", [128, 2], dt.float32, kind="ExternalInput").ap()
    eps_in = nc.dram_tensor("eps_in", [1, 2], dt.float32, kind="ExternalInput").ap()

    wqkvT_bf = nc.dram_tensor("wqkvT_bf", [DM, 6144], dt.bfloat16).ap()
    wgateT_bf = nc.dram_tensor("wgateT_bf", [DM, DM], dt.bfloat16).ap()
    woutT_bf = nc.dram_tensor("woutT_bf", [DM, DM], dt.bfloat16).ap()
    wmod_bf = nc.dram_tensor("wmod_bf", [DM, 1536], dt.bfloat16).ap()
    mod_in = nc.dram_tensor("mod_in", [B, 1536], dt.float32).ap()
    mod_all = nc.dram_tensor("mod_all", [B * 8, 1536], dt.float32).ap()
    dbt_dram = nc.dram_tensor("dbt_dram", [NH * 128, 512], dt.float32).ap()
    kc_in = nc.dram_tensor("kc_in", [NH * HD, HD], dt.float32).ap()
    kc_all = nc.dram_tensor("kc_all", [4 * NH * HD, HD], dt.float32).ap()
    gu_loc = nc.dram_tensor("gu_loc", [DM, GU_O], dt.bfloat16).ap()
    gu_all = nc.dram_tensor("gu_all", [8 * DM, GU_O], dt.bfloat16).ap()
    dn_loc = nc.dram_tensor("dn_loc", [I5, DN_F], dt.bfloat16).ap()
    dn_all = nc.dram_tensor("dn_all", [8 * I5, DN_F], dt.bfloat16).ap()
    x2_dram = nc.dram_tensor("x2_dram", [DM, NT], dt.float32).ap()
    y_out = nc.dram_tensor("y_out", [NT, DM], dt.float32, kind="ExternalOutput").ap()

    dbg_outs = {}
    for name, shape, d_ in [
        ("dbg_mod", [128, 96], dt.float32),
        ("dbg_modin", [B, 1536], dt.float32),
        ("dbg_wmodbf", [DM, 1536], dt.bfloat16),
        ("dbg_reps", [128, 2, NT], dt.bfloat16),
        ("dbg_wqkvbf", [DM, 256], dt.bfloat16),
        ("dbg_modall", [16, 1536], dt.float32),
        ("dbg_h", [128, FC, NT], dt.bfloat16),
        ("dbg_o", [128, FC, NT], dt.bfloat16),
        ("dbg_x2", [DM, NT], dt.float32),
        ("dbg_act", [128, NIC, NT], dt.bfloat16),
        ("dbg_guw", [DM, GU_O], dt.bfloat16),
        ("dbg_y", [128, FC, NT], dt.bfloat16),
        ("dbg_dnw", [I5, DN_F], dt.bfloat16),
        ("dbg_kc", [NH * HD, HD], dt.float32),
    ]:
        if name in dbg:
            dbg_outs[name] = nc.dram_tensor(name, shape, d_, kind="ExternalOutput").ap()

    es = contextlib.ExitStack()
    with es:
        E = es.enter_context
        h_fm = E(nc.sbuf_tensor("h_fm", [128, FC, NT], dt.bfloat16))
        big = E(nc.sbuf_tensor("big", [128, NIC, NT], dt.bfloat16))
        strm = E(nc.sbuf_tensor("strm", [128, 4, NT], dt.bfloat16))
        dscr = E(nc.sbuf_tensor("dscr", [128, 22528], dt.uint8))
        lall = E(nc.sbuf_tensor("lall", [128, 48, HD], dt.bfloat16))
        rows = E(nc.sbuf_tensor("rows", [1, 2, NT], dt.float32))
        identb = E(nc.sbuf_tensor("identb", [128, 128], dt.bfloat16))
        identf = E(nc.sbuf_tensor("identf", [128, 128], dt.float32))
        onesb = E(nc.sbuf_tensor("onesb", [128, 1], dt.bfloat16))
        onesf = E(nc.sbuf_tensor("onesf", [1, 128], dt.float32))
        idxT = E(nc.sbuf_tensor("idxT", [128, 2, 256], dt.float32))
        kdcolT = E(nc.sbuf_tensor("kdcolT", [128, 2], dt.float32))
        c256B = E(nc.sbuf_tensor("c256B", [128, 4], dt.float32))
        expoB = E(nc.sbuf_tensor("expoB", [128, 4], dt.float32))
        shcol = E(nc.sbuf_tensor("shcol", [128, 1], dt.int16))
        lshcol = E(nc.sbuf_tensor("lshcol", [128, 1], dt.int16))
        isb1 = E(nc.sbuf_tensor("isb1", [128, 2], dt.float32))
        epsc = E(nc.sbuf_tensor("epsc", [1, 2], dt.float32))
        negsB = E(nc.sbuf_tensor("negsB", [128, NH], dt.float32))
        normw_c = E(nc.sbuf_tensor("normw_c", [128, FC], dt.float32))
        cT_sb = E(nc.sbuf_tensor("cT_sb", [128, FC, B], dt.float32))
        cT_bf = E(nc.sbuf_tensor("cT_bf", [128, FC, B], dt.bfloat16))
        mod_cols = E(nc.sbuf_tensor("mod_cols", [128, 96], dt.float32))
        dcol = E(nc.sbuf_tensor("dcol", [128, FC, 4], dt.float32))
        gscr = E(nc.sbuf_tensor("gscr", [128, FC, 2], dt.float32))
        DBT = E(nc.sbuf_tensor("DBT", [128, 4, 256], dt.float32))
        kdcol_all = E(nc.sbuf_tensor("kdcol_all", [128, NH, 2], dt.float32))
        bdj_all = E(nc.sbuf_tensor("bdj_all", [128, NH, 4], dt.float32))
        coef_all = E(nc.sbuf_tensor("coef_all", [128, NH, 4], dt.float32))
        kv = E(nc.sbuf_tensor("kv", [128, 2, HD], dt.float32))
        kvin = E(nc.sbuf_tensor("kvin", [128, HD], dt.float32))
        Sj = E(nc.sbuf_tensor("Sj", [128, 4, HD], dt.bfloat16))
        scrb = E(nc.sbuf_tensor("scrb", [128, NT], dt.bfloat16))
        sqscr = E(nc.sbuf_tensor("sqscr", [128, NT], dt.bfloat16))
        repA = E(nc.sbuf_tensor("repA", [128, NT], dt.bfloat16))
        repB = E(nc.sbuf_tensor("repB", [128, NT], dt.bfloat16))
        modin2 = E(nc.sbuf_tensor("modin2", [B, 2, 128], dt.float32))
        stg = E(nc.sbuf_tensor("stg", [128, 2, NT], dt.float32))
        stg2 = E(nc.sbuf_tensor("stg2", [128, 2, NT], dt.float32))

        def dview(off_b, shape, d_):
            n = 1
            for s in shape[1:]:
                n *= s
            isz = {dt.int16: 2, dt.float16: 2, dt.bfloat16: 2, dt.float32: 4}[d_]
            assert off_b % 4 == 0 and off_b + n * isz <= 24 * 1024
            ap = dscr[:, off_b:off_b + n * isz].bitcast(d_)
            if len(shape) == 3:
                ap = ap.rearrange("p (a b) -> p a b", a=shape[1])
            return ap

        dec_w = [dview(0, [128, GU_O], dt.int16), dview(2816, [128, GU_O], dt.int16)]
        dec_t0 = dview(5632, [128, GU_O], dt.int16)
        dec_t1 = dview(8448, [128, GU_O], dt.int16)
        dec_sc = [dview(11264, [128, GU_O], dt.float16), dview(14080, [128, GU_O], dt.float16)]
        dec_o = [dview(16896, [128, GU_O], dt.bfloat16), dview(19712, [128, GU_O], dt.bfloat16)]
        qdB4 = dview(0, [128, NT], dt.float32)
        fscr = dview(4096, [128, NT], dt.float32)
        arrB4 = dview(8192, [128, NT], dt.float32)
        repF = dview(12288, [128, NT], dt.float32)
        kcbuf = dview(16896, [128, 4, HD], dt.float32)
        xtile = [stg[:, 0, :], stg[:, 1, :]]
        xorig = [stg2[:, 0, :], stg2[:, 1, :]]
        modB0 = stg2[0:96, 0, 0:128]
        modB1 = stg2[0:96, 0, 128:256]
        modsel = stg2[:, 0, 256:352]

        sLD = E(nc.semaphore("sLD"))
        sGP = E(nc.semaphore("sGP"))
        sSC = E(nc.semaphore("sSC"))
        sPE = E(nc.semaphore("sPE"))
        sDV = E(nc.semaphore("sDV"))
        sAC = E(nc.semaphore("sAC"))
        sPO = E(nc.semaphore("sPO"))
        sCC = E(nc.semaphore("sCC"))
        sCAST = E(nc.semaphore("sCAST"))
        sDL = [E(nc.semaphore("sDL0")), E(nc.semaphore("sDL1"))]
        sDS = [E(nc.semaphore("sDS0")), E(nc.semaphore("sDS1"))]
        sAGM = E(nc.semaphore("sAGM"))
        sAGG = E(nc.semaphore("sAGG"))
        sAGD = E(nc.semaphore("sAGD"))
        sAGK = E(nc.semaphore("sAGK"))
        SEMX = {}
        for _n in ("CN", "SA", "SB", "MST", "M1", "XL", "XOa", "XOb",
                   "SPa", "SPb", "OUTa", "OUTb", "CW"):
            SEMX[_n] = E(nc.semaphore("s" + _n))

        mainA = E(nc.psum_tensor("mainA", [128, 2, 512], dt.float32))
        mainB = E(nc.psum_tensor("mainB", [128, 2, 512], dt.float32))
        psQK = E(nc.psum_tensor("psQK", [128, 2, 256], dt.float32))
        psIN = E(nc.psum_tensor("psIN", [128, 2, 256], dt.float32))
        psT = E(nc.psum_tensor("psT", [128, 4, 128], dt.float32))
        psS = E(nc.psum_tensor("psS", [128, 512], dt.float32))

        stat_ap = [
            psQK[0:1, :, :].rearrange("p a b -> p (a b)"),
            psIN[0:1, :, :].rearrange("p a b -> p (a b)"),
            psS[0:1, :],
            psT[0:1, :, :].rearrange("p a b -> p (a b)"),
        ]
        block = E(nc.Block())

        prog = {k: [] for k in ("sync", "gpsimd", "tensor", "vector", "scalar")}
        C = {k: 0 for k in ("LD", "GP", "SC", "PE", "DV", "AC", "PO", "CC")}
        SEM = {"LD": sLD, "GP": sGP, "SC": sSC, "PE": sPE, "DV": sDV,
               "AC": sAC, "PO": sPO, "CC": sCC}

        import sys
        sched = {k: [] for k in prog}

        def _ln():
            f = sys._getframe(2)
            return f.f_lineno

        def em(eng, fn):
            prog[eng].append(fn)

        def wait(eng, cname, val):
            if val is None:
                return
            if isinstance(val, tuple):
                wait(eng, val[0], val[1])
                return
            if callable(val):
                def f(e, g=val, cn=cname):
                    v = g()
                    if v is None:
                        return
                    if isinstance(v, tuple):
                        cn2, vv = v
                    else:
                        cn2, vv = cn, v
                    if vv:
                        e.wait_ge(SEM[cn2], vv)
                em(eng, f)
                sched[eng].append(("w", cname, val, _ln()))
                return
            if val <= 0:
                return
            em(eng, lambda e, s=SEM[cname], v=val: e.wait_ge(s, v))
            sched[eng].append(("w", cname, val, _ln()))

        def waitg(eng, guard):
            if guard is None:
                return
            if callable(guard):
                wait(eng, "PE", guard)
                return
            wait(eng, guard[0], guard[1])

        ENG_CN = {"vector": "DV", "scalar": "AC", "gpsimd": "PO"}

        def op(eng, fn, cname=None, n=1):
            if eng in ("vector", "scalar"):
                cn = ENG_CN[eng]
                if cname is not None and cname != cn:
                    C[cname] += n
                    em(eng, lambda e, f=fn, s=SEM[cname], n=n: f(e).then_inc(s, n))
                    sched[eng].append(("i", cname, n, _ln()))
                    return C[cname]
                prev = C[cn]
                if prev:
                    em(eng, lambda e, s=SEM[cn], v=prev: e.wait_ge(s, v))
                    sched[eng].append(("w", cn, prev, _ln()))
                C[cn] += 1
                em(eng, lambda e, f=fn, s=SEM[cn]: f(e).then_inc(s, 1))
                sched[eng].append(("i", cn, 1, _ln()))
                return C[cn]
            if cname is None:
                em(eng, lambda e, f=fn: f(e))
                sched[eng].append(("i", None, 0, _ln()))
                return None
            C[cname] += n
            em(eng, lambda e, f=fn, s=SEM[cname], n=n: f(e).then_inc(s, n))
            sched[eng].append(("i", cname, n, _ln()))
            return C[cname]

        def acsync():
            wait("scalar", "AC", C["AC"])

        import collections
        NCNT = collections.defaultdict(int)

        def lds(dst, src, ch, eng="sync"):
            NCNT[ch] += 16
            em(eng, lambda e, d=dst, s=src, h=SEMX[ch]: e.dma_start(
                out=d, in_=s).then_inc(h, 16))
            sched[eng].append(("i", ch, 16, _ln()))
            return NCNT[ch]

        def waitn(eng, ch, val):
            if val is None or (isinstance(val, int) and val <= 0):
                return
            if callable(val):
                def getv(g=val):
                    v = g()
                    if isinstance(v, tuple):
                        v = v[1]
                    return v or 0

                def f(e, h=SEMX[ch], gv=getv):
                    v = gv()
                    if v:
                        e.wait_ge(h, v)
                em(eng, f)
                sched[eng].append(("w", ch, getv, _ln()))
                return
            em(eng, lambda e, h=SEMX[ch], v=val: e.wait_ge(h, v))
            sched[eng].append(("w", ch, val, _ln()))

        def ld(dst, src, eng="sync"):
            return op(eng, lambda e, d=dst, s=src: e.dma_start(out=d, in_=s), "LD", 16)

        late = {}
        marks = {}

        def mark(st):
            marks[st] = {e: len(prog[e]) for e in prog}
            marks[st]["cnt"] = dict(C)
            marks[st]["dl"] = list(dl_cnt) if "dl_cnt" in dir() else None

        def lateget(key, kind):
            return lambda: (kind, late.get(key, 0))

        for i in range(4):
            c0, c1 = i * 1536, (i + 1) * 1536
            em("gpsimd", lambda e, c0=c0, c1=c1: e.dma_start(
                out=wqkvT_bf[:, c0:c1], in_=wqkvT[:, c0:c1]).then_inc(sCAST, 16))
        em("gpsimd", lambda e: e.dma_start(out=wgateT_bf, in_=wgateT).then_inc(sCAST, 16))
        em("gpsimd", lambda e: e.dma_start(out=woutT_bf, in_=woutT).then_inc(sCAST, 16))
        for _ in range(6):
            sched["gpsimd"].append(("i", None, 0, 0))
        CASTS_ALL = 6 * 16

        def wait_cast(eng):
            em(eng, lambda e: e.wait_ge(sCAST, CASTS_ALL))
            sched[eng].append(("i", None, 0, 0))
        cast_pts = {}

        for d_, s_ in [(identb[:], identb_in), (identf[:], identf_in),
                       (onesb[:], onesb_in), (onesf[:], onesf_in),
                       (idxT[:], idxT_in), (kdcolT[:], kdcolT_in),
                       (c256B[:], c256_in), (expoB[:], expo_in),
                       (shcol[:], shcol_in), (lshcol[:], lshcol_in),
                       (isb1[:], isb1_in), (epsc[:], eps_in),
                       (negsB[:], bass.AP(s_row.tensor, 0, [[0, 128], [1, NH]])),
                       (normw_c[:], normw),
                       (cT_sb[:], bass.AP(cT.tensor, 0, [[B, 128], [128 * B, FC], [1, B]]))]:
            lds(d_, s_, "CN")
        consts_ld = NCNT["CN"]

        waitn("vector", "CN", consts_ld)
        op("vector", lambda e: e.tensor_scalar(out=negsB[:], in0=negsB[:], scalar1=-1.0,
                                               scalar2=None, op0=AL.mult), "DV")
        negs_rdy = C["DV"]
        waitn("scalar", "CN", consts_ld)
        op("scalar", lambda e: e.activation(out=cT_sb[:], in_=cT_sb[:], func=AF.Silu), "AC")
        ct_rdy = C["AC"]
        wait("scalar", "DV", negs_rdy)
        for h in range(NH):
            op("scalar", lambda e, h=h: e.activation(
                out=bdj_all[:, h, :], in_=c256B[:], func=AF.Exp, scale=negsB[:, h:h + 1]))
            last = h == NH - 1
            op("scalar", lambda e, h=h: e.activation(
                out=coef_all[:, h, :], in_=expoB[:], func=AF.Exp, scale=negsB[:, h:h + 1]),
                "AC" if last else None)
        dbt_st = {}
        for h in range(NH):
            sl = h % 2
            op("scalar", lambda e, h=h: e.activation(
                out=kdcol_all[:, h, :], in_=kdcolT[:], func=AF.Exp,
                scale=negsB[:, h:h + 1]))
            if h >= 2:
                wait("scalar", "SC", dbt_st[h - 2])
            a = op("scalar", lambda e, h=h, sl=sl: e.activation(
                out=DBT[:, 2 * sl:2 * sl + 2, :], in_=idxT[:], func=AF.Exp,
                scale=negsB[:, h:h + 1]), "AC")
            wait("scalar", "AC", a)
            dbt_st[h] = op("scalar", lambda e, h=h, sl=sl: e.dma_start(
                out=bass.AP(dbt_dram.tensor, h * 128 * 512, [[512, 128], [1, 512]]),
                in_=DBT[:, 2 * sl:2 * sl + 2, :]), "SC", 16)
        kdcl_rdy = C["AC"]

        mod_ld = {}
        NMC = 12

        def modbuf(bs):
            t = big[:, 32 + 4 * bs:36 + 4 * bs, :].rearrange("p a b -> p (a b)")
            return t.bitcast(dt.float32).rearrange("p (a b) -> p a b", a=FC)

        def emit_mod_load(mc):
            bs = mc % 2
            if mc >= 2:
                wait("sync", "PE", late[("modmm", mc - 2)])
            s_ = bass.AP(wmod_s.tensor, mc * 128, [[1536, 128], [128 * 1536, FC], [1, 128]])
            mod_ld[mc] = ("SA" if bs == 0 else "SB", lds(modbuf(bs), s_,
                                                         "SA" if bs == 0 else "SB"))

        emit_mod_load(0)
        emit_mod_load(1)
        for mc in range(NMC):
            bs = mc % 2
            waitn("tensor", mod_ld[mc][0], mod_ld[mc][1])
            wait("tensor", "AC", ct_rdy)
            if mc >= 2:
                wait("tensor", "DV", late[("modev", mc - 2)])
            rhs_t = modbuf(bs)
            for fc in range(FC):
                lastm = fc == FC - 1
                op("tensor", lambda e, fc=fc, bs=bs, r=rhs_t, lastm=lastm: e.matmul(
                    mainA[0:2, bs, 0:128], cT_sb[:, fc, :], r[:, fc, :],
                    start=(fc == 0), stop=lastm), "PE" if lastm else None)
            late[("modmm", mc)] = C["PE"]
            wait("vector", "PE", lateget(("modmm", min(mc + 1, NMC - 1)), "PE"))
            if mc >= 2:
                waitn("vector", "MST", late[("modst", mc - 2)])
            late[("modev", mc)] = op("vector", lambda e, mc=mc, bs=bs: e.tensor_copy(
                out=modin2[:, bs, :], in_=mainA[0:2, bs, 0:128]), "DV")
            if mc + 2 < NMC:
                emit_mod_load(mc + 2)
            wait("sync", "DV", lateget(("modev", min(mc + 1, NMC - 1)), "DV"))
            late[("modst", mc)] = lds(mod_in[:, mc * 128:(mc + 1) * 128],
                                      modin2[:, bs, :], "MST")
        main_guard = {0: ("DV", late[("modev", NMC - 1)]), 1: None}
        modst = late[("modst", NMC - 1)]
        waitn("gpsimd", "MST", modst)
        em("gpsimd", lambda e: e.collective_compute(
            "AllGather", AL.bypass, replica_groups=[list(range(8))],
            ins=[mod_in.opt()], outs=[mod_all.opt()]).then_inc(sAGM, 1))
        C["CC"] += 1
        sched["gpsimd"].append(("i", "CC", 1, 0))
        modag = 1
        em("sync", lambda e: e.wait_ge(sAGM, 1))
        sched["sync"].append(("w", "CC", 1, 0))
        lds(modB0, bass.AP(mod_all.tensor, 0, [[2 * 1536, 8], [128, 12], [1, 128]]), "M1")
        modrows_ld = lds(modB1, bass.AP(mod_all.tensor, 1536,
                                        [[2 * 1536, 8], [128, 12], [1, 128]]), "M1")
        waitn("tensor", "M1", modrows_ld)
        op("tensor", lambda e: e.matmul(psQK[:, 0, 0:96], modB0, identf[0:96, 0:96],
                                        start=True, stop=True))
        op("tensor", lambda e: e.matmul(psQK[:, 1, 0:96], modB1, identf[0:96, 0:96],
                                        start=True, stop=True), "PE")
        modtr = C["PE"]
        wait("vector", "PE", modtr)
        op("vector", lambda e: e.tensor_scalar(out=modsel, in0=psQK[:, 1, 0:96],
                                               scalar1=isb1[:, 0:1], scalar2=None, op0=AL.mult))
        op("vector", lambda e: e.scalar_tensor_tensor(
            out=mod_cols[:], in0=psQK[:, 0, 0:96], scalar=isb1[:, 1:2], in1=modsel,
            op0=AL.mult, op1=AL.add), "DV")
        modc_rdy = C["DV"]
        if "dbg_mod" in dbg_outs:
            wait("sync", "DV", modc_rdy)
            ld(dbg_outs["dbg_mod"], mod_cols[:])
        if "dbg_wmodbf" in dbg_outs:
            wait_cast("sync")
            ld(dbg_outs["dbg_wmodbf"], wmod_bf)
        if "dbg_wqkvbf" in dbg_outs:
            wait_cast("sync")
            ld(dbg_outs["dbg_wqkvbf"], wqkvT_bf[:, 0:256])
        if "dbg_modin" in dbg_outs:
            waitn("sync", "MST", modst)
            ld(dbg_outs["dbg_modin"], mod_in)
        if "dbg_modall" in dbg_outs:
            wait("sync", "CC", modag)
            ld(dbg_outs["dbg_modall"], mod_all)
        op("vector", lambda e: e.tensor_scalar(out=dcol[:, :, 0], in0=mod_cols[:, 16:32],
                                               scalar1=1.0, scalar2=None, op0=AL.add))
        op("vector", lambda e: e.tensor_scalar(out=dcol[:, :, 1], in0=mod_cols[:, 64:80],
                                               scalar1=1.0, scalar2=None, op0=AL.add))
        for j, lo in ((2, 32), (3, 80)):
            op("vector", lambda e, lo=lo: e.tensor_scalar(
                out=gscr[:, :, 0], in0=mod_cols[:, lo:lo + 16], scalar1=0.25, scalar2=None,
                op0=AL.max))
            op("vector", lambda e, lo=lo: e.tensor_scalar(
                out=gscr[:, :, 1], in0=mod_cols[:, lo:lo + 16], scalar1=-0.25, scalar2=None,
                op0=AL.min))
            op("vector", lambda e: e.tensor_tensor(
                out=gscr[:, :, 0], in0=gscr[:, :, 0], in1=gscr[:, :, 1], op=AL.subtract))
            op("vector", lambda e, lo=lo, j=j: e.tensor_scalar(
                out=dcol[:, :, j], in0=mod_cols[:, lo:lo + 16], scalar1=0.0, scalar2=None,
                op0=AL.is_ge))
            op("vector", lambda e, j=j: e.tensor_tensor(
                out=dcol[:, :, j], in0=dcol[:, :, j], in1=gscr[:, :, 0], op=AL.mult))
            op("vector", lambda e, j=j: e.tensor_tensor(
                out=dcol[:, :, j], in0=dcol[:, :, j], in1=gscr[:, :, 1], op=AL.add), "DV")
        dcol_rdy = C["DV"]
        mark(1)

        def xslot(fc):
            t = big[:, 2 * fc:2 * fc + 2, :].rearrange("p a b -> p (a b)")
            return t.bitcast(dt.float32)
        for fc in range(FC):
            src = bass.AP(xT.tensor, fc * 128 * NT, [[NT, 128], [1, NT]])
            lds(xslot(fc), src, "XL")
        xl_all = NCNT["XL"]
        sqslot = [sqscr[:], scrb[:]]
        for fc in range(FC):
            waitn("scalar", "XL", xl_all)
            if fc >= 2:
                wait("scalar", "PE", lateget(("st1", fc - 2), "PE"))
            late[("xsq", fc)] = op("scalar", lambda e, fc=fc: e.activation(
                out=sqslot[fc % 2], in_=xslot(fc), func=AF.Square), "AC")
            waitn("vector", "XL", xl_all)
            late[("xcp", fc)] = op("vector", lambda e, fc=fc: e.tensor_copy(
                out=h_fm[:, fc, :], in_=xslot(fc)), "DV")
        for fc in range(FC):
            lag = min(fc + 1, FC - 1)
            wait("tensor", "DV", late[("xcp", lag)])
            wait("tensor", "AC", late[("xsq", lag)])
            if fc == 0:
                wait("tensor", "DV", modc_rdy)
            first, lastf = fc == 0, fc == FC - 1
            op("tensor", lambda e, fc=fc, first=first, lastf=lastf: e.matmul(
                stat_ap[0], onesb[:], h_fm[:, fc, 0:512], start=first, stop=lastf))
            op("tensor", lambda e, fc=fc, first=first, lastf=lastf: e.matmul(
                stat_ap[1], onesb[:], h_fm[:, fc, 512:1024], start=first, stop=lastf))
            op("tensor", lambda e, fc=fc, first=first, lastf=lastf: e.matmul(
                stat_ap[2], onesb[:], sqslot[fc % 2][:, 0:512], start=first, stop=lastf))
            late[("st1", fc)] = op("tensor", lambda e, fc=fc, lastf=lastf, first=first: e.matmul(
                stat_ap[3], onesb[:], sqslot[fc % 2][:, 512:1024], start=first, stop=lastf),
                "PE")
        ln1_stats = C["PE"]

        def replicate(src_row, dst, guardB):
            waitg("tensor", guardB)
            for half in range(2):
                op("tensor", lambda e, h=half, s=src_row: e.matmul(
                    mainB[:, h, :], onesf[:], s[:, h * 512:(h + 1) * 512],
                    start=True, stop=True), "PE" if half == 1 else None)
            wait("vector", "PE", C["PE"])
            return op("vector", lambda e, t=dst: e.tensor_copy(
                out=t, in_=mainB[:].rearrange("p a b -> p (a b)")), "DV")

        def stats_pipeline(epsi, stats_pt, rms_only, mean_dst, rstd_dst, guardB):
            wait("scalar", "PE", stats_pt)
            rep_m = None
            if not rms_only:
                op("scalar", lambda e: e.activation(out=rows[:, 0, 0:512], in_=stat_ap[0],
                                                    func=AF.Copy, scale=1.0 / DM), "AC")
                mean_rdy = op("scalar", lambda e: e.activation(
                    out=rows[:, 0, 512:1024], in_=stat_ap[1],
                    func=AF.Copy, scale=1.0 / DM), "AC")
                wait("tensor", "AC", mean_rdy)
                rep_m = replicate(rows[:, 0, :], mean_dst, guardB)
                wait("scalar", "PE", C["PE"])
                acsync()
                op("scalar", lambda e: e.activation(out=rows[:, 1, :], in_=rows[:, 0, :],
                                                    func=AF.Square), "AC")
                acsync()
            op("scalar", lambda e: e.activation(out=rows[:, 0, 0:512], in_=stat_ap[2],
                                                func=AF.Copy, scale=1.0 / DM), "AC")
            pss_free = op("scalar", lambda e: e.activation(
                out=rows[:, 0, 512:1024], in_=stat_ap[3],
                func=AF.Copy, scale=1.0 / DM), "AC")
            wait("vector", "AC", pss_free)
            if not rms_only:
                dv = op("vector", lambda e: e.tensor_tensor(
                    out=rows[:, 0, :], in0=rows[:, 0, :], in1=rows[:, 1, :],
                    op=AL.subtract), "DV")
                wait("scalar", "DV", dv)
            op("scalar", lambda e, epsi=epsi: e.activation(
                out=rows[:, 0, :], in_=rows[:, 0, :],
                func=AF.Sqrt, bias=epsc[0:1, epsi:epsi + 1]), "AC")
            wait("vector", "AC", C["AC"])
            rr = op("vector", lambda e: e.reciprocal(out=rows[:, 1, :], in_=rows[:, 0, :]), "DV")
            wait("tensor", "DV", rr)
            gb2 = ("DV", rep_m) if rep_m is not None else guardB
            rep_r = replicate(rows[:, 1, :], rstd_dst, gb2)
            return rep_r, pss_free

        ln1_rep, pss_free = stats_pipeline(0, ln1_stats, False, repA[:], repB[:],
                                           main_guard[1])
        main_guard[1] = ("DV", ln1_rep)
        if "dbg_reps" in dbg_outs:
            wait("sync", "DV", ln1_rep)
            ld(dbg_outs["dbg_reps"][:, 0, :], repA[:])
            ld(dbg_outs["dbg_reps"][:, 1, :], repB[:])
        wait("vector", "DV", ln1_rep)
        wait("vector", "DV", dcol_rdy)
        for fc in range(FC):
            op("vector", lambda e, fc=fc: e.tensor_tensor(
                out=scrb[:], in0=h_fm[:, fc, :], in1=repA[:], op=AL.subtract))
            op("vector", lambda e, fc=fc: e.tensor_tensor(
                out=h_fm[:, fc, :], in0=scrb[:], in1=repB[:], op=AL.mult))
            op("vector", lambda e, fc=fc: e.tensor_scalar(
                out=h_fm[:, fc, :], in0=h_fm[:, fc, :], scalar1=dcol[:, fc:fc + 1, 0],
                scalar2=mod_cols[:, fc:fc + 1], op0=AL.mult, op1=AL.add),
                "DV" if fc == FC - 1 else None)
        h_rdy = C["DV"]
        if "dbg_h" in dbg_outs:
            wait("sync", "DV", h_rdy)
            ld(dbg_outs["dbg_h"], h_fm[:])
        mark(2)

        dec_steps = []
        dl_cnt = [0, 0]
        ds_cnt = [0, 0]

        def emit_decode(n_ch, w_ap, sc_ap, o_ap, t0, t1, src16, srcsc, dst, width, store_l,
                        key):
            for i in range(n_ch):
                bs = i % 2
                if i >= 2:
                    em("gpsimd", lambda e, bs=bs, v=ds_cnt[bs] * 16: e.wait_ge(sDS[bs], v))
                    sched["gpsimd"].append(("w", "DS%d" % bs, ds_cnt[bs] * 16, 0))
                em("gpsimd", lambda e, i=i, bs=bs: e.dma_start(
                    out=w_ap[bs], in_=bass.AP(src16.tensor, 32 * i * width,
                                              [[width, 32], [0, 4], [1, width]])
                ).then_inc(sDL[bs], 16))
                em("gpsimd", lambda e, i=i, bs=bs: e.dma_start(
                    out=sc_ap[bs], in_=bass.AP(srcsc.tensor, i * width,
                                               [[0, 128], [1, width]])
                ).then_inc(sDL[bs], 16))
                C["GP"] += 32
                sched["gpsimd"].append(("i", "DL%d" % bs, 32, 0))
                dl_cnt[bs] += 2
                ldpt = (bs, dl_cnt[bs] * 16)

                def vec_step(i=i, bs=bs, ldpt=ldpt, key=key):
                    em("vector", lambda e, b=ldpt[0], v=ldpt[1]: e.wait_ge(sDL[b], v))
                    sched["vector"].append(("w", "DL%d" % ldpt[0], ldpt[1], 0))
                    waitn("vector", "CN", consts_ld)
                    op("vector", lambda e: e.tensor_scalar(
                        out=t0, in0=w_ap[bs], scalar1=shcol[:, 0:1], scalar2=7,
                        op0=AL.logical_shift_right, op1=AL.bitwise_and))
                    op("vector", lambda e: e.tensor_scalar(
                        out=t1, in0=t0, scalar1=28, scalar2=512,
                        op0=AL.add, op1=AL.mult))
                    op("vector", lambda e: e.tensor_scalar(
                        out=t0, in0=w_ap[bs], scalar1=lshcol[:, 0:1], scalar2=-32768,
                        op0=AL.logical_shift_left, op1=AL.bitwise_and))
                    op("vector", lambda e: e.tensor_scalar(
                        out=w_ap[bs].bitcast(dt.float16), in0=t1.bitcast(dt.float16),
                        scalar1=2.0, scalar2=1.0, op0=AL.mult, op1=AL.subtract))
                    op("vector", lambda e: e.tensor_tensor(
                        out=w_ap[bs].bitcast(dt.float16), in0=t1.bitcast(dt.float16),
                        in1=w_ap[bs].bitcast(dt.float16), op=AL.min))
                    op("vector", lambda e: e.tensor_tensor(
                        out=w_ap[bs], in0=w_ap[bs], in1=t0, op=AL.bitwise_xor))
                    late[(key, i)] = op("vector", lambda e: e.tensor_tensor(
                        out=o_ap[bs], in0=w_ap[bs].bitcast(dt.float16), in1=sc_ap[bs],
                        op=AL.mult), "DV")
                dec_steps.append(vec_step)
                wait("gpsimd", "DV", lateget((key, i), "DV"))
                em("gpsimd", lambda e, i=i, bs=bs: e.dma_start(
                    out=dst[i * 128:(i + 1) * 128, :], in_=o_ap[bs]).then_inc(sDS[bs], 16))
                C["GP"] += 16
                sched["gpsimd"].append(("i", "DS%d" % bs, 16, 0))
                ds_cnt[bs] += 1
                store_l.append(C["GP"])

        gu_store = []
        emit_decode(FC, dec_w, dec_sc, dec_o, dec_t0, dec_t1, gu16, gusc, gu_loc, GU_O,
                    gu_store, "gudec")
        em("gpsimd", lambda e, v=ds_cnt[0] * 16: e.wait_ge(sDS[0], v))
        sched["gpsimd"].append(("w", "DS0", ds_cnt[0] * 16, 0))
        em("gpsimd", lambda e, v=ds_cnt[1] * 16: e.wait_ge(sDS[1], v))
        sched["gpsimd"].append(("w", "DS1", ds_cnt[1] * 16, 0))
        em("gpsimd", lambda e: e.collective_compute(
            "AllGather", AL.bypass, replica_groups=[list(range(8))],
            ins=[gu_loc.opt()], outs=[gu_all.opt()]).then_inc(sAGG, 1))
        C["CC"] += 1
        sched["gpsimd"].append(("i", "CC", 1, 0))
        gu_ag = ("AGG", 1)
        if "dbg_guw" in dbg_outs:
            em("sync", lambda e: e.wait_ge(sAGG, 1))
            sched["sync"].append(("w", "CC", 2, 0))
            ld(dbg_outs["dbg_guw"], gu_loc)
        dn_w = [dview(16896, [128, DN_F], dt.int16), dview(17408, [128, DN_F], dt.int16)]
        dn_t0 = dview(17920, [128, DN_F], dt.int16)
        dn_t1 = dview(18432, [128, DN_F], dt.int16)
        dn_sc = [dview(18944, [128, DN_F], dt.float16), dview(19456, [128, DN_F], dt.float16)]
        dn_o = [dview(19968, [128, DN_F], dt.bfloat16), dview(20480, [128, DN_F], dt.bfloat16)]
        dn_store = []
        emit_decode(NIC, dn_w, dn_sc, dn_o, dn_t0, dn_t1, dn16, dnsc, dn_loc, DN_F, dn_store,
                    "dndec")
        for step in dec_steps:
            step()
        em("gpsimd", lambda e, v=ds_cnt[0] * 16: e.wait_ge(sDS[0], v))
        sched["gpsimd"].append(("w", "DS0", ds_cnt[0] * 16, 0))
        em("gpsimd", lambda e, v=ds_cnt[1] * 16: e.wait_ge(sDS[1], v))
        sched["gpsimd"].append(("w", "DS1", ds_cnt[1] * 16, 0))
        em("gpsimd", lambda e: e.collective_compute(
            "AllGather", AL.bypass, replica_groups=[list(range(8))],
            ins=[dn_loc.opt()], outs=[dn_all.opt()]).then_inc(sAGD, 1))
        C["CC"] += 1
        sched["gpsimd"].append(("i", "CC", 1, 0))
        dn_ag = ("AGD", 1)
        if "dbg_dnw" in dbg_outs:
            em("sync", lambda e: e.wait_ge(sAGD, 1))
            sched["sync"].append(("w", "CC", 3, 0))
            ld(dbg_outs["dbg_dnw"], dn_loc)

        mark(3)
        o_all = big[:, 0:16, :]
        q_all = big[:, 16:32, :]
        k_fm_s = [big[:, 32, :], big[:, 37, :]]
        kdk_tm_s = [big[:, 33, :].rearrange("p (a b) -> p a b", a=8),
                    big[:, 38, :].rearrange("p (a b) -> p a b", a=8)]
        v_tm_s = [big[:, 34, :].rearrange("p (a b) -> p a b", a=8),
                  big[:, 39, :].rearrange("p (a b) -> p a b", a=8)]
        qkD_s = [big[:, 35, :].rearrange("p (a b) -> p a b", a=2),
                 big[:, 40, :].rearrange("p (a b) -> p a b", a=2)]
        vscr = big[:, 36, :]

        qkv_ld = []
        for ck in range(48):
            bs = ck % 2
            wait_cast("sync")
            if ck >= 2:
                wait("sync", "PE", lateget(("qkvuse", ck - 2), "PE"))
            src = bass.AP(wqkvT_bf.tensor, ck * 128, [[6144, 128], [128 * 6144, FC], [1, 128]])
            dst = strm[:, bs * 2:bs * 2 + 2, :].rearrange("p a b -> p (a b)")
            dst = dst.rearrange("p (a b) -> p a b", a=FC)
            qkv_ld.append(("SA" if bs == 0 else "SB", lds(dst, src, "SA" if bs == 0 else "SB")))

        def strm_tile(bs):
            t = strm[:, bs * 2:bs * 2 + 2, :].rearrange("p a b -> p (a b)")
            return t.rearrange("p (a b) -> p a b", a=FC)

        def proj(oc_global, psum, rhs_src, guard, ldpt_list, usekey):
            ck = oc_global
            bs = ck % 2
            waitn("tensor", ldpt_list[ck][0], ldpt_list[ck][1])
            waitg("tensor", guard)
            for fc in range(FC):
                for th in range(2):
                    lastm = fc == FC - 1 and th == 1
                    op("tensor", lambda e, fc=fc, th=th, bs=bs, ps=psum, r=rhs_src:
                       e.matmul(ps[:, th, :], strm_tile(bs)[:, fc, :],
                                r[:, fc, th * 512:(th + 1) * 512],
                                start=(fc == 0), stop=(fc == FC - 1)),
                       "PE" if lastm else None)
            late[(usekey, ck)] = max(late.get((usekey, ck), 0), C["PE"])
            return C["PE"]

        kc_st = []
        kv_done = {}
        qk_ev = modc_rdy
        in_ev = None
        kvup = None
        prev_vtm = {0: None, 1: None}
        prev_kdk = {0: None, 1: None}
        dbt_rdy_s = {}
        wait("tensor", "DV", h_rdy)

        def proj_head(h):
            sl = h % 2
            k_fm = k_fm_s[sl]
            kdk_tm = kdk_tm_s[sl]
            v_tm = v_tm_s[sl]
            qpt = proj(3 * h + 0, mainA, h_fm, main_guard[0], qkv_ld, "qkvuse")
            wait("scalar", "PE", qpt)
            main_guard[0] = ("AC", op("scalar", lambda e, h=h: e.activation(
                out=q_all[:, h, :], in_=mainA[:].rearrange("p a b -> p (a b)"),
                func=AF.Silu), "AC"))
            kpt = proj(3 * h + 1, mainB, h_fm, main_guard[1], qkv_ld, "qkvuse")
            wait("scalar", "PE", kpt)
            main_guard[1] = ("AC", op("scalar", lambda e, k_fm=k_fm: e.activation(
                out=k_fm[:], in_=mainB[:].rearrange("p a b -> p (a b)"),
                func=AF.Silu), "AC"))
            k_ev = main_guard[1][1]
            if h >= 2:
                wait("scalar", "DV", lateget(("qkdone", h - 2), "DV"))
            dbt_rdy_s[h] = op("scalar", lambda e, h=h, sl=sl: e.dma_start(
                out=DBT[:, 2 * sl:2 * sl + 2, :],
                in_=bass.AP(dbt_dram.tensor, h * 128 * 512, [[512, 128], [1, 512]])),
                "SC", 16)
            vpt = proj(3 * h + 2, mainA, h_fm, main_guard[0], qkv_ld, "qkvuse")
            wait("scalar", "PE", vpt)
            main_guard[0] = ("AC", op("scalar", lambda e: e.activation(
                out=vscr[:], in_=mainA[:].rearrange("p a b -> p (a b)"),
                func=AF.Silu), "AC"))
            v_ev = main_guard[0][1]
            if h >= 2:
                wait("scalar", "DV", kv_done[h - 2])
                kc_st.append(op("scalar", lambda e, g=h - 2: e.dma_start(
                    out=kc_in[g * HD:(g + 1) * HD, :], in_=kv[:, g % 2, :]), "SC", 16))

            wait("tensor", "AC", k_ev)
            if h == 0:
                wait("tensor", "AC", pss_free)
            if prev_vtm[sl] is not None:
                wait("tensor", "DV", prev_vtm[sl])
            if prev_vtm[1 - sl] is not None:
                wait("tensor", "DV", prev_vtm[1 - sl])
            kdk_ev = None
            for half in range(2):
                for i in range(4):
                    g = half * 4 + i
                    op("tensor", lambda e, g=g, i=i, k_fm=k_fm: e.matmul(
                        psT[:, i, :], k_fm[:, g * 128:(g + 1) * 128], identb[:],
                        start=True, stop=True), "PE" if i == 3 else None)
                ktr = C["PE"]
                wait("vector", "PE", ktr)
                if half == 0 and h == 0:
                    wait("vector", "AC", kdcl_rdy)
                for i in range(4):
                    g = half * 4 + i
                    op("vector", lambda e, g=g, i=i, h=h, kdk_tm=kdk_tm: e.tensor_scalar(
                        out=kdk_tm[:, g, :], in0=psT[:, i, :],
                        scalar1=kdcol_all[:, h, (g % 2):(g % 2) + 1], scalar2=None,
                        op0=AL.mult), "DV" if i == 3 else None)
                kdk_ev = C["DV"]
                if half == 0:
                    wait("tensor", "DV", kdk_ev)
            prev_kdk[sl] = kdk_ev
            late[("kdkdone", h)] = kdk_ev
            wait("tensor", "AC", v_ev)
            wait("tensor", "DV", kdk_ev)
            vtm_ev = None
            for half in range(2):
                for i in range(4):
                    g = half * 4 + i
                    op("tensor", lambda e, g=g, i=i: e.matmul(
                        psT[:, i, :], vscr[:, g * 128:(g + 1) * 128], identb[:],
                        start=True, stop=True), "PE" if i == 3 else None)
                vtr = C["PE"]
                wait("vector", "PE", vtr)
                vtm_ev = op("vector", lambda e, half=half, v_tm=v_tm: e.tensor_copy(
                    out=v_tm[:, half * 4:half * 4 + 4, :].rearrange("p a b -> p (a b)"),
                    in_=psT[:].rearrange("p a b -> p (a b)")), "DV")
                if half == 0:
                    wait("tensor", "DV", vtm_ev)
            prev_vtm[sl] = vtm_ev
            late[("projdone", h)] = (kdk_ev, vtm_ev)

        def attn_head(h):
            nonlocal qk_ev, in_ev, kvup
            sl = h % 2
            k_fm = k_fm_s[sl]
            kdk_tm = kdk_tm_s[sl]
            v_tm = v_tm_s[sl]
            qkD = qkD_s[sl]
            kdk_ev, vtm_ev = late[("projdone", h)]
            wait("vector", "SC", dbt_rdy_s[h])
            for j in range(4):
                wait("tensor", "DV", qk_ev)
                for nt in range(2):
                    op("tensor", lambda e, j=j, nt=nt, h=h, k_fm=k_fm: e.matmul(
                        psQK[:, nt, :],
                        k_fm[:, j * 256 + nt * 128: j * 256 + nt * 128 + 128],
                        q_all[:, h, j * 256:(j + 1) * 256], start=True, stop=True),
                        "PE" if nt == 1 else None)
                qkpt = C["PE"]
                wait("vector", "PE", qkpt)
                op("vector", lambda e, qkD=qkD, sl=sl: e.tensor_tensor(
                    out=qkD[:, 0:2, 0:256], in0=psQK[:],
                    in1=DBT[:, 2 * sl:2 * sl + 2, :], op=AL.mult), "DV")
                qk_ev = C["DV"]
                wait("tensor", "DV", qk_ev)
                wait("tensor", "DV", in_ev)
                wait("tensor", "DV", kvup)
                wait("tensor", "DV", vtm_ev)
                for nt in range(2):
                    op("tensor", lambda e, j=j, nt=nt, v_tm=v_tm, qkD=qkD: e.matmul(
                        psIN[:, 0, :], v_tm[:, j * 2 + nt, :], qkD[:, nt, 0:256],
                        start=(nt == 0), stop=(nt == 1)), "PE" if nt == 1 else None)
                inpt = C["PE"]
                wait("vector", "PE", inpt)
                in_ev = op("vector", lambda e, j=j, h=h: e.tensor_copy(
                    out=o_all[:, h, j * 256:(j + 1) * 256], in_=psIN[:, 0, :]), "DV")
                wait("tensor", "DV", in_ev)
                wait("tensor", "DV", kdk_ev)
                for nt in range(2):
                    op("tensor", lambda e, j=j, nt=nt, kdk_tm=kdk_tm, v_tm=v_tm: e.matmul(
                        psIN[:, 1, 0:128], kdk_tm[:, j * 2 + nt, :], v_tm[:, j * 2 + nt, :],
                        start=(nt == 0), stop=(nt == 1)), "PE" if nt == 1 else None)
                cpt = C["PE"]
                wait("vector", "PE", cpt)
                dst = kv[:, sl, :] if j == 3 else lall[:, 3 * h + j, :]
                if j == 0:
                    kvup_pt = op("vector", lambda e, dst=dst: e.tensor_copy(
                        out=dst, in_=psIN[:, 1, 0:128]), "DV")
                else:
                    if j == 3 and h > 1:
                        wait("vector", "SC", kc_st[h - 2])
                    src = lall[:, 3 * h + j - 1, :]
                    kvup_pt = op("vector", lambda e, h=h, dst=dst, src=src:
                                 e.scalar_tensor_tensor(
                        out=dst, in0=src, scalar=bdj_all[:, h, 0:1],
                        in1=psIN[:, 1, 0:128], op0=AL.mult, op1=AL.add), "DV")
                kvup = kvup_pt
            late[("qkdone", h)] = qk_ev
            kv_done[h] = C["DV"]

        for h in range(NH):
            proj_head(h)
            attn_head(h)
        for g in (NH - 2, NH - 1):
            wait("scalar", "DV", kv_done[g])
            kc_st.append(op("scalar", lambda e, g=g: e.dma_start(
                out=kc_in[g * HD:(g + 1) * HD, :], in_=kv[:, g % 2, :]), "SC", 16))
        if "dbg_kc" in dbg_outs:
            wait("sync", "SC", kc_st[-1])
            ld(dbg_outs["dbg_kc"], kc_in)

        mark(4)
        wait("gpsimd", "SC", kc_st[-1])
        em("gpsimd", lambda e: e.collective_compute(
            "AllGather", AL.bypass, replica_groups=[[0, 1, 2, 3], [4, 5, 6, 7]],
            ins=[kc_in.opt()], outs=[kc_all.opt()]).then_inc(sAGK, 1))
        C["CC"] += 1
        sched["gpsimd"].append(("i", "CC", 1, 0))
        kc_ag = C["CC"]

        em("sync", lambda e: e.wait_ge(sAGG, 1))
        sched["sync"].append(("w", "CC", 2, 0))
        arr_ld = lds(arrB4, arrB4_in, "M1")
        inter_ev = {0: ("DV", qk_ev), 1: ("DV", max(in_ev or 0, kvup or 0))}
        prev_sj_use = None
        prev_kvin = None
        for h in range(NH):
            em("scalar", lambda e: e.wait_ge(sAGK, 1))
            sched["scalar"].append(("w", "CC", kc_ag, 0))
            if prev_kvin is not None:
                wait("scalar", "DV", prev_kvin)
            kcl = op("scalar", lambda e, h=h: e.dma_start(
                out=kcbuf[:], in_=bass.AP(kc_all.tensor, h * HD * HD,
                                          [[HD, 128], [NH * HD * HD, 4], [1, HD]])), "SC", 16)
            if h == 0:
                waitn("scalar", "M1", arr_ld)
            op("scalar", lambda e, h=h: e.activation(
                out=qdB4, in_=arrB4, func=AF.Exp, scale=negsB[:, h:h + 1]), "AC")
            qd_rdy = C["AC"]
            wait("vector", "SC", kcl)
            op("vector", lambda e, h=h: e.tensor_scalar(
                out=kvin[:], in0=kcbuf[:, 0, :], scalar1=coef_all[:, h, 0:1],
                scalar2=None, op0=AL.mult))
            for g in (1, 2):
                op("vector", lambda e, h=h, g=g: e.scalar_tensor_tensor(
                    out=kvin[:], in0=kcbuf[:, g, :], scalar=coef_all[:, h, g:g + 1],
                    in1=kvin[:], op0=AL.mult, op1=AL.add),
                    "DV" if g == 2 else None)
            prev_kvin = C["DV"]
            if prev_sj_use is not None:
                wait("vector", "PE", prev_sj_use)
            op("vector", lambda e: e.tensor_copy(out=Sj[:, 0, :], in_=kvin[:]))
            for j in (1, 2, 3):
                op("vector", lambda e, h=h, j=j: e.scalar_tensor_tensor(
                    out=Sj[:, j, :], in0=kvin[:], scalar=bdj_all[:, h, j - 1:j],
                    in1=lall[:, 3 * h + j - 1, :], op0=AL.mult, op1=AL.add),
                    "DV" if j == 3 else None)
            s_rdy = C["DV"]
            for j in range(4):
                ps = psQK if j % 2 == 0 else psIN
                wait("tensor", "DV", s_rdy)
                waitg("tensor", inter_ev[j % 2])
                op("tensor", lambda e, h=h, j=j, ps=ps: e.matmul(
                    ps[:, 0, :], Sj[:, j, :], q_all[:, h, j * 256:(j + 1) * 256],
                    start=True, stop=True), "PE")
                ipt = C["PE"]
                if j == 3:
                    prev_sj_use = ipt
                wait("vector", "PE", ipt)
                wait("vector", "AC", qd_rdy)
                op("vector", lambda e, j=j, ps=ps: e.tensor_tensor(
                    out=scrb[:, 0:256], in0=ps[:, 0, :], in1=qdB4[:, j * 256:(j + 1) * 256],
                    op=AL.mult))
                inter_ev[j % 2] = ("DV", op("vector", lambda e, h=h, j=j: e.tensor_tensor(
                    out=o_all[:, h, j * 256:(j + 1) * 256],
                    in0=o_all[:, h, j * 256:(j + 1) * 256],
                    in1=scrb[:, 0:256], op=AL.add), "DV"))
        o_rdy = C["DV"]
        if "dbg_o" in dbg_outs:
            wait("sync", "DV", o_rdy)
            ld(dbg_outs["dbg_o"], o_all)
        mark(5)

        wait("scalar", "AC", pss_free)
        for fc in range(FC):
            wait("scalar", "DV", o_rdy)
            if fc >= 1:
                wait("scalar", "PE", lateget(("st2", fc - 1), "PE"))
            a = op("scalar", lambda e, fc=fc: e.activation(
                out=sqscr[:], in_=o_all[:, fc, :], func=AF.Square), "AC")
            wait("tensor", "AC", a)
            if fc == 0:
                wait("tensor", "AC", pss_free)
                wait("tensor", "DV", o_rdy)
                wait("tensor", "DV", prev_vtm[0])
                wait("tensor", "DV", prev_vtm[1])
            first, lastf = fc == 0, fc == FC - 1
            op("tensor", lambda e, first=first, lastf=lastf: e.matmul(
                stat_ap[2], onesb[:], sqscr[:, 0:512], start=first, stop=lastf))
            late[("st2", fc)] = op("tensor", lambda e, lastf=lastf, first=first: e.matmul(
                stat_ap[3], onesb[:], sqscr[:, 512:1024], start=first, stop=lastf), "PE")
        rms_stats = C["PE"]
        rms_rep, pss_free = stats_pipeline(1, rms_stats, True, None, repB[:],
                                           main_guard[1])
        main_guard[1] = ("DV", rms_rep)

        g_fm = q_all
        gate_ld = []
        for ck in range(16):
            bs = ck % 2
            wait_cast("sync")
            if ck < 2:
                wait("sync", "PE", lateget(("qkvuse", 46 + ck), "PE"))
            else:
                wait("sync", "PE", lateget(("gateuse", ck - 2), "PE"))
            src = bass.AP(wgateT_bf.tensor, ck * 128, [[DM, 128], [128 * DM, FC], [1, 128]])
            dst = strm[:, bs * 2:bs * 2 + 2, :].rearrange("p a b -> p (a b)")
            dst = dst.rearrange("p (a b) -> p a b", a=FC)
            gate_ld.append(("SA" if bs == 0 else "SB", lds(dst, src, "SA" if bs == 0 else "SB")))
        first_gate = True
        for oc in range(FC):
            ps = mainA if oc % 2 == 0 else mainB
            guard = main_guard[oc % 2]
            if first_gate:
                wait("tensor", "DV", o_rdy)
                first_gate = False
            pt = proj(oc, ps, h_fm, guard, gate_ld, "gateuse")
            wait("scalar", "PE", pt)
            main_guard[oc % 2] = ("AC", op("scalar", lambda e, oc=oc, ps=ps: e.activation(
                out=g_fm[:, oc, :], in_=ps[:].rearrange("p a b -> p (a b)"),
                func=AF.Sigmoid), "AC"))
        gate_done = main_guard[1][1]
        wait("vector", "AC", main_guard[0][1])
        wait("vector", "AC", gate_done)
        wait("vector", "DV", rms_rep)
        for fc in range(FC):
            op("vector", lambda e, fc=fc: e.tensor_tensor(
                out=scrb[:], in0=o_all[:, fc, :], in1=repB[:], op=AL.mult))
            op("vector", lambda e, fc=fc: e.scalar_tensor_tensor(
                out=o_all[:, fc, :], in0=scrb[:], scalar=normw_c[:, fc:fc + 1],
                in1=g_fm[:, fc, :], op0=AL.mult, op1=AL.mult),
                "DV" if fc == FC - 1 else None)
        gated_rdy = C["DV"]

        wout_ld = {}
        xo_ld = {}

        def emit_outproj_loads(fc):
            bs = fc % 2
            wait_cast("sync")
            if fc < 2:
                wait("sync", "PE", lateget(("gateuse", 14 + fc), "PE"))
            else:
                wait("sync", "PE", late[("woutuse", fc - 2)])
            s_ = bass.AP(woutT_bf.tensor, fc * 128, [[DM, 128], [128 * DM, FC], [1, 128]])
            d_ = strm[:, bs * 2:bs * 2 + 2, :].rearrange("p a b -> p (a b)")
            d_ = d_.rearrange("p (a b) -> p a b", a=FC)
            wout_ld[fc] = ("SA" if bs == 0 else "SB", lds(d_, s_, "SA" if bs == 0 else "SB"))
            if fc >= 2:
                wait("sync", "DV", late[("x2ev", fc - 2)])
            s_ = bass.AP(xT.tensor, fc * 128 * NT, [[NT, 128], [1, NT]])
            xo_ld[fc] = lds(xorig[bs], s_, "XOa" if bs == 0 else "XOb")

        emit_outproj_loads(0)
        emit_outproj_loads(1)
        x2st = []
        for fc in range(FC):
            bs = fc % 2
            ps = mainA if fc % 2 == 0 else mainB
            wait("tensor", "DV", gated_rdy)
            waitn("tensor", wout_ld[fc][0], wout_ld[fc][1])
            waitg("tensor", main_guard[fc % 2])
            wbs = fc % 2
            for oc in range(FC):
                for th in range(2):
                    lastm = oc == FC - 1 and th == 1

                    def mm(e, fc=fc, oc=oc, th=th, wbs=wbs, ps=ps):
                        t = strm_tile(wbs)
                        return e.matmul(ps[:, th, :], t[:, oc, :],
                                        o_all[:, oc, th * 512:(th + 1) * 512],
                                        start=(oc == 0), stop=(oc == FC - 1))
                    op("tensor", mm, "PE" if lastm else None)
            late[("woutuse", fc)] = C["PE"]
            opj = C["PE"]
            wait("vector", "PE", opj)
            waitn("vector", "XOa" if fc % 2 == 0 else "XOb", xo_ld[fc])
            if fc >= 2:
                waitn("vector", "SPa" if fc % 2 == 0 else "SPb",
                      lateget(("x2st", fc - 2), "LD"))
            op("vector", lambda e, fc=fc, bs=bs, ps=ps: e.scalar_tensor_tensor(
                out=xtile[bs], in0=ps[:].rearrange("p a b -> p (a b)"),
                scalar=dcol[:, fc:fc + 1, 2], in1=xorig[bs], op0=AL.mult, op1=AL.add), "DV")
            x2ev = C["DV"]
            late[("x2ev", fc)] = x2ev
            main_guard[fc % 2] = ("DV", x2ev)
            op("vector", lambda e, fc=fc, bs=bs: e.tensor_copy(
                out=h_fm[:, fc, :], in_=xtile[bs]), "DV")
            x2bf = C["DV"]
            wait("scalar", "DV", x2ev)
            if fc >= 1:
                wait("scalar", "PE", lateget(("st3", fc - 1), "PE"))
            a = op("scalar", lambda e, bs=bs: e.activation(
                out=sqscr[:], in_=xtile[bs], func=AF.Square), "AC")
            wait("tensor", "DV", x2bf)
            wait("tensor", "AC", a)
            if fc == 0:
                wait("tensor", "AC", pss_free)
            first, lastf = fc == 0, fc == FC - 1
            op("tensor", lambda e, fc=fc, first=first, lastf=lastf: e.matmul(
                stat_ap[0], onesb[:], h_fm[:, fc, 0:512], start=first, stop=lastf))
            op("tensor", lambda e, fc=fc, first=first, lastf=lastf: e.matmul(
                stat_ap[1], onesb[:], h_fm[:, fc, 512:1024], start=first, stop=lastf))
            op("tensor", lambda e, fc=fc, first=first, lastf=lastf: e.matmul(
                stat_ap[2], onesb[:], sqscr[:, 0:512], start=first, stop=lastf))
            late[("st3", fc)] = op("tensor", lambda e, fc=fc, lastf=lastf, first=first: e.matmul(
                stat_ap[3], onesb[:], sqscr[:, 512:1024], start=first, stop=lastf), "PE")
            wait("sync", "DV", x2ev)
            st = lds(bass.AP(x2_dram.tensor, fc * 128 * NT, [[NT, 128], [1, NT]]),
                     xtile[bs], "SPa" if bs == 0 else "SPb")
            late[("x2st", fc)] = st
            x2st.append(st)
            if fc + 2 < FC:
                emit_outproj_loads(fc + 2)
        ln2_stats = C["PE"]
        if "dbg_x2" in dbg_outs:
            waitn("sync", "SPa", lambda: NCNT["SPa"])
            waitn("sync", "SPb", lambda: NCNT["SPb"])
            ld(dbg_outs["dbg_x2"], x2_dram)
        ln2_rep, pss_free = stats_pipeline(0, ln2_stats, False, repA[:], repB[:],
                                           main_guard[1])
        main_guard[1] = ("DV", ln2_rep)
        wait("vector", "DV", ln2_rep)
        for fc in range(FC):
            op("vector", lambda e, fc=fc: e.tensor_tensor(
                out=scrb[:], in0=h_fm[:, fc, :], in1=repA[:], op=AL.subtract))
            op("vector", lambda e, fc=fc: e.tensor_tensor(
                out=h_fm[:, fc, :], in0=scrb[:], in1=repB[:], op=AL.mult))
            op("vector", lambda e, fc=fc: e.tensor_scalar(
                out=h_fm[:, fc, :], in0=h_fm[:, fc, :], scalar1=dcol[:, fc:fc + 1, 1],
                scalar2=mod_cols[:, 48 + fc:48 + fc + 1], op0=AL.mult, op1=AL.add),
                "DV" if fc == FC - 1 else None)
        h2_rdy = C["DV"]
        mark(6)

        act = big[:, 0:NIC, :]

        def gu_tile(bs):
            t = strm[:, bs * 2:bs * 2 + 2, :].rearrange("p a b -> p (a b)")
            return t.rearrange("p (a b) -> p a b", a=FC)

        gu_ldp = {}

        def emit_gu_load(i, half):
            oc = i if half == 0 else 44 + i
            bs = half
            em("sync", lambda e: e.wait_ge(sAGG, 1))
            sched["sync"].append(("w", "CC", 2, 0))
            if i == 0:
                wait("sync", "PE", lateget(("woutuse", 14 + half), "PE"))
            else:
                wait("sync", "PE", late[("guuse", (i - 1) if half == 0 else 44 + (i - 1))])
            rnk, jj = oc // 11, oc % 11
            s_ = bass.AP(gu_all.tensor, rnk * DM * GU_O + jj * 128,
                         [[GU_O, 128], [128 * GU_O, FC], [1, 128]])
            gu_ldp[oc] = ("SA" if bs == 0 else "SB", lds(gu_tile(bs), s_,
                                                         "SA" if bs == 0 else "SB"))

        emit_gu_load(0, 0)
        emit_gu_load(0, 1)
        for i in range(NIC):
            for half, oc in ((0, i), (1, 44 + i)):
                bs = half
                ps = mainA if half == 0 else mainB
                waitn("tensor", gu_ldp[oc][0], gu_ldp[oc][1])
                if i == 0 and half == 0:
                    wait("tensor", "DV", h2_rdy)
                waitg("tensor", main_guard[half])
                for fc in range(FC):
                    for th in range(2):
                        lastm = fc == FC - 1 and th == 1
                        op("tensor", lambda e, fc=fc, th=th, bs=bs, ps=ps: e.matmul(
                            ps[:, th, :], gu_tile(bs)[:, fc, :],
                            h_fm[:, fc, th * 512:(th + 1) * 512],
                            start=(fc == 0), stop=(fc == FC - 1)), "PE" if lastm else None)
                late[("guuse", oc)] = C["PE"]
                if i + 1 < NIC:
                    emit_gu_load(i + 1, half)
                if half == 0:
                    gpt = C["PE"]
                else:
                    upt = C["PE"]
            wait("scalar", "PE", gpt)
            main_guard[0] = ("AC", op("scalar", lambda e: e.activation(
                out=scrb[:], in_=mainA[:].rearrange("p a b -> p (a b)"), func=AF.Silu), "AC"))
            wait("vector", "PE", upt)
            wait("vector", "AC", main_guard[0][1])
            main_guard[1] = ("DV", op("vector", lambda e, i=i: e.tensor_tensor(
                out=act[:, i, :], in0=mainB[:].rearrange("p a b -> p (a b)"),
                in1=scrb[:], op=AL.mult), "DV"))
        act_rdy = main_guard[1][1]
        if "dbg_act" in dbg_outs:
            wait("sync", "DV", act_rdy)
            ld(dbg_outs["dbg_act"], act)
        mark(7)

        y_fm = h_fm

        def dn_tile(q):
            flat = strm[:].rearrange("p a b -> p (a b)")
            off = 0 if q % 2 == 0 else 2048
            return flat[:, off:off + 11 * 128].rearrange("p (a b) -> p a b", a=11)

        dn_ldp = {}
        qidx = 0
        for fc2 in range(FC):
            g_, coff = fc2 // 2, (fc2 % 2) * 128
            for q in range(4):
                em("sync", lambda e: e.wait_ge(sAGD, 1))
                sched["sync"].append(("w", "CC", 3, 0))
                if qidx < 2:
                    wait("sync", "PE", lateget(("guuse", 86 + qidx), "PE"))
                else:
                    wait("sync", "PE", lateget(("dnuse", qidx - 2), "PE"))
                src = bass.AP(dn_all.tensor,
                              g_ * I5 * DN_F + q * 11 * 128 * DN_F + coff,
                              [[DN_F, 128], [128 * DN_F, 11], [1, 128]])
                ch_ = "SA" if q % 2 == 0 else "SB"
                dn_ldp[(fc2, q)] = (ch_, lds(dn_tile(q), src, ch_))
                qidx += 1
        qidx = 0
        for fc2 in range(FC):
            ps = mainA if fc2 % 2 == 0 else mainB
            if fc2 == 0:
                wait("tensor", "DV", act_rdy)
            waitg("tensor", main_guard[fc2 % 2])
            for q in range(4):
                waitn("tensor", dn_ldp[(fc2, q)][0], dn_ldp[(fc2, q)][1])
                for icq in range(11):
                    ic = q * 11 + icq
                    for th in range(2):
                        lastm = icq == 10 and th == 1
                        op("tensor", lambda e, q=q, icq=icq, ic=ic, th=th, ps=ps: e.matmul(
                            ps[:, th, :], dn_tile(q)[:, icq, :],
                            act[:, ic, th * 512:(th + 1) * 512],
                            start=(ic == 0), stop=(ic == NIC - 1)), "PE" if lastm else None)
                late[("dnuse", qidx)] = C["PE"]
                qidx += 1
            ypt = C["PE"]
            wait("vector", "PE", ypt)
            yev = op("vector", lambda e, fc2=fc2, ps=ps: e.tensor_copy(
                out=y_fm[:, fc2, :], in_=ps[:].rearrange("p a b -> p (a b)")), "DV")
            main_guard[fc2 % 2] = ("DV", yev)
            wait("scalar", "DV", yev)
            if fc2 >= 1:
                wait("scalar", "PE", lateget(("st4", fc2 - 1), "PE"))
            a = op("scalar", lambda e, fc2=fc2: e.activation(
                out=sqscr[:], in_=y_fm[:, fc2, :], func=AF.Square), "AC")
            wait("tensor", "AC", a)
            if fc2 == 0:
                wait("tensor", "AC", pss_free)
            first, lastf = fc2 == 0, fc2 == FC - 1
            op("tensor", lambda e, first=first, lastf=lastf: e.matmul(
                stat_ap[2], onesb[:], sqscr[:, 0:512], start=first, stop=lastf))
            late[("st4", fc2)] = op("tensor", lambda e, lastf=lastf, first=first: e.matmul(
                stat_ap[3], onesb[:], sqscr[:, 512:1024], start=first, stop=lastf), "PE")
        y_stats = C["PE"]
        if "dbg_y" in dbg_outs:
            wait("sync", "DV", main_guard[0][1])
            wait("sync", "DV", main_guard[1][1])
            ld(dbg_outs["dbg_y"], y_fm[:])
        y_rep, pss_free = stats_pipeline(1, y_stats, True, None, repF,
                                         main_guard[1])
        main_guard[1] = ("DV", y_rep)

        waitn("sync", "SPa", lambda: None if not x2st else NCNT["SPa"])
        waitn("sync", "SPb", lambda: NCNT["SPb"])
        x2re = []
        for fc in range(2):
            src = bass.AP(x2_dram.tensor, fc * 128 * NT, [[NT, 128], [1, NT]])
            x2re.append(lds(xorig[fc], src, "XOa" if fc % 2 == 0 else "XOb"))
        tr_guard = {0: None, 1: None}
        tr_pe = {}
        for fc in range(FC):
            bs = fc % 2
            wait("vector", "DV", y_rep)
            waitn("vector", "XOa" if fc % 2 == 0 else "XOb", x2re[fc])
            if fc >= 2:
                wait("vector", "PE", tr_pe[fc - 2])
            op("vector", lambda e, fc=fc: e.tensor_tensor(
                out=fscr, in0=y_fm[:, fc, :], in1=repF, op=AL.mult))
            op("vector", lambda e, fc=fc, bs=bs: e.scalar_tensor_tensor(
                out=xtile[bs], in0=fscr, scalar=dcol[:, fc:fc + 1, 3], in1=xorig[bs],
                op0=AL.mult, op1=AL.add), "DV")
            cmb = C["DV"]
            ps = mainA if fc % 2 == 0 else mainB
            wait("tensor", "DV", cmb)
            waitg("tensor", main_guard[fc % 2] if fc < 2 else tr_guard[fc % 2])
            for tc in range(8):
                op("tensor", lambda e, tc=tc, bs=bs, ps=ps: e.matmul(
                    ps[:].rearrange("p a b -> p (a b)")[:, tc * 128:(tc + 1) * 128],
                    xtile[bs][:, tc * 128:(tc + 1) * 128], identf[:],
                    start=True, stop=True), "PE" if tc == 7 else None)
            tr_pe[fc] = C["PE"]
            wait("vector", "PE", tr_pe[fc])
            if fc >= 2:
                waitn("vector", "OUTa" if fc % 2 == 0 else "OUTb",
                      lateget(("outst", fc - 2), "LD"))
            tr_guard[fc % 2] = ("DV", op("vector", lambda e, fc=fc, ps=ps, bs=bs: e.tensor_copy(
                out=stg2[:, bs, :], in_=ps[:].rearrange("p a b -> p (a b)")), "DV"))
            wait("sync", "DV", tr_guard[fc % 2][1])
            dst = bass.AP(y_out.tensor, fc * 128, [[DM, 128], [128 * DM, 8], [1, 128]])
            late[("outst", fc)] = lds(dst, stg2[:, bs, :].rearrange("p (a b) -> p a b", a=8),
                                      "OUTa" if bs == 0 else "OUTb")
            if fc + 2 < FC:
                src2 = bass.AP(x2_dram.tensor, (fc + 2) * 128 * NT, [[NT, 128], [1, NT]])
                x2re.append(lds(xorig[bs], src2, "XOa" if bs == 0 else "XOb"))
        waitn("sync", "OUTa", lambda: NCNT["OUTa"])
        waitn("sync", "OUTb", lambda: NCNT["OUTb"])
        wait("sync", "LD", C["LD"])
        for _ch in ("CN", "SA", "SB", "MST", "M1", "XL", "XOa", "XOb", "SPa", "SPb"):
            waitn("sync", _ch, lambda c=_ch: NCNT[c] or None)

        nc._sched = sched
        if stage < 99 and stage in marks:
            mk = marks[stage]
            for e in prog:
                del prog[e][mk[e]:]
                del sched[e][mk[e]:]
            cnt = mk["cnt"]
            em("sync", lambda e: e.wait_ge(sLD, cnt["LD"]))
            em("scalar", lambda e: e.wait_ge(sSC, cnt["SC"]))
            em("gpsimd", lambda e: e.wait_ge(sGP, 0) if cnt["GP"] == 0 else e.wait_ge(sCAST, CASTS_ALL))
        @block.sync
        def _(e):
            for fn in prog["sync"]:
                fn(e)

        @block.gpsimd
        def _(e):
            for fn in prog["gpsimd"]:
                fn(e)

        @block.tensor
        def _(e):
            for fn in prog["tensor"]:
                fn(e)

        @block.vector
        def _(e):
            for fn in prog["vector"]:
                fn(e)

        @block.scalar
        def _(e):
            for fn in prog["scalar"]:
                fn(e)

    return nc


def _consts(core):
    arr = np.arange(1, 257, dtype=np.float32)
    arrB4 = np.tile(np.tile(arr, 4)[None, :], (128, 1))
    m = np.arange(256)
    d = m[None, :] - m[:, None]
    idxT = np.where(d >= 0, d, 1e5).astype(np.float32)
    idxT = np.ascontiguousarray(idxT.reshape(2, 128, 256))
    idxT = np.ascontiguousarray(idxT.transpose(1, 0, 2))
    p = np.arange(128)
    kdcolT = np.zeros((128, 2), np.float32)
    for t in range(2):
        kdcolT[:, t] = 255.0 - (128 * t + p)
    c256 = np.zeros((128, 4), np.float32)
    c256[:, :] = np.array([256.0, 512.0, 768.0, 1024.0])[None, :]
    chunk = core % 4
    expo = np.full((128, 4), 1e9, np.float32)
    for g in range(4):
        if g < chunk:
            expo[:, g] = 1024.0 * (chunk - 1 - g)
    j = (p % 4).astype(np.int16)
    shcol = (4 * j).reshape(128, 1)
    lshcol = (12 - 4 * j).reshape(128, 1)
    isb1 = np.zeros((128, 2), np.float32)
    isb1[:, 0] = 1.0 if core >= 4 else 0.0
    isb1[:, 1] = 1.0 - isb1[:, 0]
    return dict(
        identb_in=np.eye(128, dtype=bf16),
        identf_in=np.eye(128, dtype=np.float32),
        onesb_in=np.ones((128, 1), dtype=bf16),
        onesf_in=np.ones((1, 128), dtype=np.float32),
        arrB4_in=arrB4.astype(np.float32), idxT_in=idxT, kdcolT_in=kdcolT,
        c256_in=c256, expo_in=expo, shcol_in=shcol, lshcol_in=lshcol,
        isb1_in=isb1,
        eps_in=np.array([[1e-5, 1e-6]], dtype=np.float32),
    )


def make_inmaps(inputs):
    x = np.asarray(inputs["L0_attn_hidden_states"], dtype=np.float32)
    slope = np.asarray(inputs["L0_attn_slope_rate"], dtype=np.float32).reshape(1, NH)
    wqkv = np.asarray(inputs["L0_attn_w_qkv"], dtype=np.float32)
    wgate = np.asarray(inputs["L0_attn_w_gate"], dtype=np.float32)
    wout = np.asarray(inputs["L0_attn_w_out"], dtype=np.float32)
    normw = np.ascontiguousarray(
        np.asarray(inputs["L0_attn_norm_w"], dtype=np.float32).reshape(FC, 128).T)
    gup = np.asarray(inputs["L0_ffn_gate_up_packed"], dtype=np.int32)
    gus = np.asarray(inputs["L0_ffn_gate_up_scales"], dtype=np.float16)
    dnp = np.asarray(inputs["L0_ffn_down_packed"], dtype=np.int32)
    dns = np.asarray(inputs["L0_ffn_down_scales"], dtype=np.float16)
    cc = np.asarray(inputs["L0_c"], dtype=np.float32)
    wmod = np.asarray(inputs["L0_wmod"], dtype=np.float32)

    wqkvT = np.ascontiguousarray(wqkv.T)
    wgateT = np.ascontiguousarray(wgate.T)
    woutT = np.ascontiguousarray(wout.T)
    cTm = np.ascontiguousarray(cc.T)
    gu16T = np.ascontiguousarray(gup.view(np.int16).T)
    dn16T = np.ascontiguousarray(dnp.view(np.int16).T)

    maps = []
    for r in range(8):
        b, ch = r // 4, r % 4
        t0 = ch * NT
        m = dict(
            xT=np.ascontiguousarray(x[b, t0:t0 + NT, :].T),
            cT=cTm,
            wmod_s=np.ascontiguousarray(wmod[:, 1536 * r:1536 * (r + 1)]),
            wqkvT=wqkvT, wgateT=wgateT, woutT=woutT,
            s_row=slope, normw=normw,
            gu16=np.ascontiguousarray(gu16T[:, GU_O * r:GU_O * (r + 1)]),
            gusc=np.ascontiguousarray(gus[:, GU_O * r:GU_O * (r + 1)]),
            dn16=np.ascontiguousarray(dn16T[:, DN_F * r:DN_F * (r + 1)]),
            dnsc=np.ascontiguousarray(dns[:, DN_F * r:DN_F * (r + 1)]),
        )
        m.update(_consts(r))
        maps.append(m)
    return maps


def kernel(**inputs):
    maps = make_inmaps(inputs)
    nc = build_nc()
    res = run_bass_kernel_spmd(nc, maps, core_ids=list(range(8)))
    out = np.zeros((B, N, DM), dtype=np.float32)
    for r in range(8):
        b, ch = r // 4, r % 4
        out[b, ch * NT:(ch + 1) * NT, :] = res.results[r]["y_out"]
    return out


# version 11 (final)
# speedup vs baseline: 1.0308x; 1.0161x over previous
import numpy as np
import ml_dtypes
import contextlib
import concourse.bass as bass
import concourse.mybir as mybir
from concourse.bass_utils import run_bass_kernel_spmd

dt = mybir.dt
AF = mybir.ActivationFunctionType
AL = mybir.AluOpType
bf16 = ml_dtypes.bfloat16

B, N, DM = 2, 4096, 2048
NH, HD = 16, 128
NT = 1024
FC = 16
I5 = 5632
GU_O = 1408
DN_F = 256
NIC = 44


def build_nc(dbg=(), stage=99):
    nc = bass.Bass("TRN2", target_bir_lowering=False)

    xT = nc.dram_tensor("xT", [DM, NT], dt.float32, kind="ExternalInput").ap()
    cT = nc.dram_tensor("cT", [DM, B], dt.float32, kind="ExternalInput").ap()
    wmod_s = nc.dram_tensor("wmod_s", [DM, 1536], dt.float32, kind="ExternalInput").ap()
    wqkvT = nc.dram_tensor("wqkvT", [DM, 6144], dt.float32, kind="ExternalInput").ap()
    wgateT = nc.dram_tensor("wgateT", [DM, DM], dt.float32, kind="ExternalInput").ap()
    woutT = nc.dram_tensor("woutT", [DM, DM], dt.float32, kind="ExternalInput").ap()
    s_row = nc.dram_tensor("s_row", [1, NH], dt.float32, kind="ExternalInput").ap()
    normw = nc.dram_tensor("normw", [128, FC], dt.float32, kind="ExternalInput").ap()
    gu16 = nc.dram_tensor("gu16", [512, GU_O], dt.int16, kind="ExternalInput").ap()
    gusc = nc.dram_tensor("gusc", [FC, GU_O], dt.float16, kind="ExternalInput").ap()
    dn16 = nc.dram_tensor("dn16", [1408, DN_F], dt.int16, kind="ExternalInput").ap()
    dnsc = nc.dram_tensor("dnsc", [NIC, DN_F], dt.float16, kind="ExternalInput").ap()
    identb_in = nc.dram_tensor("identb_in", [128, 128], dt.bfloat16, kind="ExternalInput").ap()
    identf_in = nc.dram_tensor("identf_in", [128, 128], dt.float32, kind="ExternalInput").ap()
    onesb_in = nc.dram_tensor("onesb_in", [128, 1], dt.bfloat16, kind="ExternalInput").ap()
    onesf_in = nc.dram_tensor("onesf_in", [1, 128], dt.float32, kind="ExternalInput").ap()
    arrB4_in = nc.dram_tensor("arrB4_in", [128, NT], dt.float32, kind="ExternalInput").ap()
    idxT_in = nc.dram_tensor("idxT_in", [128, 2, 256], dt.float32, kind="ExternalInput").ap()
    kdcolT_in = nc.dram_tensor("kdcolT_in", [128, 2], dt.float32, kind="ExternalInput").ap()
    c256_in = nc.dram_tensor("c256_in", [128, 4], dt.float32, kind="ExternalInput").ap()
    expo_in = nc.dram_tensor("expo_in", [128, 4], dt.float32, kind="ExternalInput").ap()
    shcol_in = nc.dram_tensor("shcol_in", [128, 1], dt.int16, kind="ExternalInput").ap()
    lshcol_in = nc.dram_tensor("lshcol_in", [128, 1], dt.int16, kind="ExternalInput").ap()
    isb1_in = nc.dram_tensor("isb1_in", [128, 2], dt.float32, kind="ExternalInput").ap()
    eps_in = nc.dram_tensor("eps_in", [1, 2], dt.float32, kind="ExternalInput").ap()

    wqkvT_bf = nc.dram_tensor("wqkvT_bf", [DM, 6144], dt.bfloat16).ap()
    wgateT_bf = nc.dram_tensor("wgateT_bf", [DM, DM], dt.bfloat16).ap()
    woutT_bf = nc.dram_tensor("woutT_bf", [DM, DM], dt.bfloat16).ap()
    wmod_bf = nc.dram_tensor("wmod_bf", [DM, 1536], dt.bfloat16).ap()
    mod_in = nc.dram_tensor("mod_in", [B, 1536], dt.float32).ap()
    mod_all = nc.dram_tensor("mod_all", [B * 8, 1536], dt.float32).ap()
    dbt_dram = nc.dram_tensor("dbt_dram", [NH * 128, 512], dt.float32).ap()
    kc_in = nc.dram_tensor("kc_in", [NH * HD, HD], dt.float32).ap()
    kc_all = nc.dram_tensor("kc_all", [4 * NH * HD, HD], dt.float32).ap()
    gu_loc = nc.dram_tensor("gu_loc", [DM, GU_O], dt.bfloat16).ap()
    gu_all = nc.dram_tensor("gu_all", [8 * DM, GU_O], dt.bfloat16).ap()
    dn_loc = nc.dram_tensor("dn_loc", [I5, DN_F], dt.bfloat16).ap()
    dn_all = nc.dram_tensor("dn_all", [8 * I5, DN_F], dt.bfloat16).ap()
    x2_dram = nc.dram_tensor("x2_dram", [DM, NT], dt.float32).ap()
    y_out = nc.dram_tensor("y_out", [NT, DM], dt.float32, kind="ExternalOutput").ap()

    dbg_outs = {}
    for name, shape, d_ in [
        ("dbg_mod", [128, 96], dt.float32),
        ("dbg_modin", [B, 1536], dt.float32),
        ("dbg_wmodbf", [DM, 1536], dt.bfloat16),
        ("dbg_reps", [128, 2, NT], dt.bfloat16),
        ("dbg_wqkvbf", [DM, 256], dt.bfloat16),
        ("dbg_modall", [16, 1536], dt.float32),
        ("dbg_h", [128, FC, NT], dt.bfloat16),
        ("dbg_o", [128, FC, NT], dt.bfloat16),
        ("dbg_x2", [DM, NT], dt.float32),
        ("dbg_act", [128, NIC, NT], dt.bfloat16),
        ("dbg_guw", [DM, GU_O], dt.bfloat16),
        ("dbg_y", [128, FC, NT], dt.bfloat16),
        ("dbg_dnw", [I5, DN_F], dt.bfloat16),
        ("dbg_kc", [NH * HD, HD], dt.float32),
    ]:
        if name in dbg:
            dbg_outs[name] = nc.dram_tensor(name, shape, d_, kind="ExternalOutput").ap()

    es = contextlib.ExitStack()
    with es:
        E = es.enter_context
        h_fm = E(nc.sbuf_tensor("h_fm", [128, FC, NT], dt.bfloat16))
        big = E(nc.sbuf_tensor("big", [128, NIC, NT], dt.bfloat16))
        strm = E(nc.sbuf_tensor("strm", [128, 4, NT], dt.bfloat16))
        dscr = E(nc.sbuf_tensor("dscr", [128, 22528], dt.uint8))
        lall = E(nc.sbuf_tensor("lall", [128, 48, HD], dt.bfloat16))
        rows = E(nc.sbuf_tensor("rows", [1, 2, NT], dt.float32))
        identb = E(nc.sbuf_tensor("identb", [128, 128], dt.bfloat16))
        identf = E(nc.sbuf_tensor("identf", [128, 128], dt.float32))
        onesb = E(nc.sbuf_tensor("onesb", [128, 1], dt.bfloat16))
        onesf = E(nc.sbuf_tensor("onesf", [1, 128], dt.float32))
        idxT = E(nc.sbuf_tensor("idxT", [128, 2, 256], dt.float32))
        kdcolT = E(nc.sbuf_tensor("kdcolT", [128, 2], dt.float32))
        c256B = E(nc.sbuf_tensor("c256B", [128, 4], dt.float32))
        expoB = E(nc.sbuf_tensor("expoB", [128, 4], dt.float32))
        shcol = E(nc.sbuf_tensor("shcol", [128, 1], dt.int16))
        lshcol = E(nc.sbuf_tensor("lshcol", [128, 1], dt.int16))
        isb1 = E(nc.sbuf_tensor("isb1", [128, 2], dt.float32))
        epsc = E(nc.sbuf_tensor("epsc", [1, 2], dt.float32))
        negsB = E(nc.sbuf_tensor("negsB", [128, NH], dt.float32))
        normw_c = E(nc.sbuf_tensor("normw_c", [128, FC], dt.float32))
        cT_sb = E(nc.sbuf_tensor("cT_sb", [128, FC, B], dt.float32))
        cT_bf = E(nc.sbuf_tensor("cT_bf", [128, FC, B], dt.bfloat16))
        mod_cols = E(nc.sbuf_tensor("mod_cols", [128, 96], dt.float32))
        dcol = E(nc.sbuf_tensor("dcol", [128, FC, 4], dt.float32))
        gscr = E(nc.sbuf_tensor("gscr", [128, FC, 2], dt.float32))
        DBT = E(nc.sbuf_tensor("DBT", [128, 4, 256], dt.float32))
        kdcol_all = E(nc.sbuf_tensor("kdcol_all", [128, NH, 2], dt.float32))
        bdj_all = E(nc.sbuf_tensor("bdj_all", [128, NH, 4], dt.float32))
        coef_all = E(nc.sbuf_tensor("coef_all", [128, NH, 4], dt.float32))
        kv = E(nc.sbuf_tensor("kv", [128, 2, HD], dt.float32))
        kvin = E(nc.sbuf_tensor("kvin", [128, HD], dt.float32))
        Sj = E(nc.sbuf_tensor("Sj", [128, 4, HD], dt.bfloat16))
        scrb = E(nc.sbuf_tensor("scrb", [128, NT], dt.bfloat16))
        sqscr = E(nc.sbuf_tensor("sqscr", [128, NT], dt.bfloat16))
        repA = E(nc.sbuf_tensor("repA", [128, NT], dt.bfloat16))
        repB = E(nc.sbuf_tensor("repB", [128, NT], dt.bfloat16))
        modin2 = E(nc.sbuf_tensor("modin2", [B, 2, 128], dt.float32))
        stg = E(nc.sbuf_tensor("stg", [128, 2, NT], dt.float32))
        stg2 = E(nc.sbuf_tensor("stg2", [128, 2, NT], dt.float32))

        def dview(off_b, shape, d_):
            n = 1
            for s in shape[1:]:
                n *= s
            isz = {dt.int16: 2, dt.float16: 2, dt.bfloat16: 2, dt.float32: 4}[d_]
            assert off_b % 4 == 0 and off_b + n * isz <= 24 * 1024
            ap = dscr[:, off_b:off_b + n * isz].bitcast(d_)
            if len(shape) == 3:
                ap = ap.rearrange("p (a b) -> p a b", a=shape[1])
            return ap

        dec_w = [dview(0, [128, GU_O], dt.int16), dview(2816, [128, GU_O], dt.int16)]
        dec_t0 = dview(5632, [128, GU_O], dt.int16)
        dec_t1 = dview(8448, [128, GU_O], dt.int16)
        dec_sc = [dview(11264, [128, GU_O], dt.float16), dview(14080, [128, GU_O], dt.float16)]
        dec_o = [dview(16896, [128, GU_O], dt.bfloat16), dview(19712, [128, GU_O], dt.bfloat16)]
        qdB4 = dview(0, [128, NT], dt.float32)
        fscr = dview(4096, [128, NT], dt.float32)
        arrB4 = dview(8192, [128, NT], dt.float32)
        repF = dview(12288, [128, NT], dt.float32)
        kcbuf = dview(16896, [128, 4, HD], dt.float32)
        xtile = [stg[:, 0, :], stg[:, 1, :]]
        xorig = [stg2[:, 0, :], stg2[:, 1, :]]
        modB0 = stg2[0:96, 0, 0:128]
        modB1 = stg2[0:96, 0, 128:256]
        modsel = stg2[:, 0, 256:352]

        sLD = E(nc.semaphore("sLD"))
        sGP = E(nc.semaphore("sGP"))
        sSC = E(nc.semaphore("sSC"))
        sPE = E(nc.semaphore("sPE"))
        sDV = E(nc.semaphore("sDV"))
        sAC = E(nc.semaphore("sAC"))
        sPO = E(nc.semaphore("sPO"))
        sCC = E(nc.semaphore("sCC"))
        sCAST = E(nc.semaphore("sCAST"))
        sDL = [E(nc.semaphore("sDL0")), E(nc.semaphore("sDL1"))]
        sDS = [E(nc.semaphore("sDS0")), E(nc.semaphore("sDS1"))]
        sAGM = E(nc.semaphore("sAGM"))
        sAGG = E(nc.semaphore("sAGG"))
        sAGD = E(nc.semaphore("sAGD"))
        sAGK = E(nc.semaphore("sAGK"))
        SEMX = {}
        for _n in ("CN", "SA", "SB", "MST", "M1", "XL", "XOa", "XOb",
                   "SPa", "SPb", "OUTa", "OUTb", "CW"):
            SEMX[_n] = E(nc.semaphore("s" + _n))

        mainA = E(nc.psum_tensor("mainA", [128, 2, 512], dt.float32))
        mainB = E(nc.psum_tensor("mainB", [128, 2, 512], dt.float32))
        psQK = E(nc.psum_tensor("psQK", [128, 2, 256], dt.float32))
        psIN = E(nc.psum_tensor("psIN", [128, 2, 256], dt.float32))
        psT = E(nc.psum_tensor("psT", [128, 4, 128], dt.float32))
        psS = E(nc.psum_tensor("psS", [128, 512], dt.float32))

        stat_ap = [
            psQK[0:1, :, :].rearrange("p a b -> p (a b)"),
            psIN[0:1, :, :].rearrange("p a b -> p (a b)"),
            psS[0:1, :],
            psT[0:1, :, :].rearrange("p a b -> p (a b)"),
        ]
        block = E(nc.Block())

        prog = {k: [] for k in ("sync", "gpsimd", "tensor", "vector", "scalar")}
        C = {k: 0 for k in ("LD", "GP", "SC", "PE", "DV", "AC", "PO", "CC")}
        SEM = {"LD": sLD, "GP": sGP, "SC": sSC, "PE": sPE, "DV": sDV,
               "AC": sAC, "PO": sPO, "CC": sCC}

        import sys
        sched = {k: [] for k in prog}

        def _ln():
            f = sys._getframe(2)
            return f.f_lineno

        def em(eng, fn):
            prog[eng].append(fn)

        def wait(eng, cname, val):
            if val is None:
                return
            if isinstance(val, tuple):
                wait(eng, val[0], val[1])
                return
            if callable(val):
                def f(e, g=val, cn=cname):
                    v = g()
                    if v is None:
                        return
                    if isinstance(v, tuple):
                        cn2, vv = v
                    else:
                        cn2, vv = cn, v
                    if vv:
                        e.wait_ge(SEM[cn2], vv)
                em(eng, f)
                sched[eng].append(("w", cname, val, _ln()))
                return
            if val <= 0:
                return
            em(eng, lambda e, s=SEM[cname], v=val: e.wait_ge(s, v))
            sched[eng].append(("w", cname, val, _ln()))

        def waitg(eng, guard):
            if guard is None:
                return
            if callable(guard):
                wait(eng, "PE", guard)
                return
            wait(eng, guard[0], guard[1])

        ENG_CN = {"vector": "DV", "scalar": "AC", "gpsimd": "PO"}

        def op(eng, fn, cname=None, n=1):
            if eng in ("vector", "scalar"):
                cn = ENG_CN[eng]
                if cname is not None and cname != cn:
                    C[cname] += n
                    em(eng, lambda e, f=fn, s=SEM[cname], n=n: f(e).then_inc(s, n))
                    sched[eng].append(("i", cname, n, _ln()))
                    return C[cname]
                prev = C[cn]
                if prev:
                    em(eng, lambda e, s=SEM[cn], v=prev: e.wait_ge(s, v))
                    sched[eng].append(("w", cn, prev, _ln()))
                C[cn] += 1
                em(eng, lambda e, f=fn, s=SEM[cn]: f(e).then_inc(s, 1))
                sched[eng].append(("i", cn, 1, _ln()))
                return C[cn]
            if cname is None:
                em(eng, lambda e, f=fn: f(e))
                sched[eng].append(("i", None, 0, _ln()))
                return None
            C[cname] += n
            em(eng, lambda e, f=fn, s=SEM[cname], n=n: f(e).then_inc(s, n))
            sched[eng].append(("i", cname, n, _ln()))
            return C[cname]

        def acsync():
            wait("scalar", "AC", C["AC"])

        import collections
        NCNT = collections.defaultdict(int)

        def lds(dst, src, ch, eng="sync"):
            NCNT[ch] += 16
            em(eng, lambda e, d=dst, s=src, h=SEMX[ch]: e.dma_start(
                out=d, in_=s).then_inc(h, 16))
            sched[eng].append(("i", ch, 16, _ln()))
            return NCNT[ch]

        def waitn(eng, ch, val):
            if val is None or (isinstance(val, int) and val <= 0):
                return
            if callable(val):
                def getv(g=val):
                    v = g()
                    if isinstance(v, tuple):
                        v = v[1]
                    return v or 0

                def f(e, h=SEMX[ch], gv=getv):
                    v = gv()
                    if v:
                        e.wait_ge(h, v)
                em(eng, f)
                sched[eng].append(("w", ch, getv, _ln()))
                return
            em(eng, lambda e, h=SEMX[ch], v=val: e.wait_ge(h, v))
            sched[eng].append(("w", ch, val, _ln()))

        def ld(dst, src, eng="sync"):
            return op(eng, lambda e, d=dst, s=src: e.dma_start(out=d, in_=s), "LD", 16)

        late = {}
        marks = {}

        def mark(st):
            marks[st] = {e: len(prog[e]) for e in prog}
            marks[st]["cnt"] = dict(C)
            marks[st]["dl"] = list(dl_cnt) if "dl_cnt" in dir() else None

        def lateget(key, kind):
            return lambda: (kind, late.get(key, 0))

        for i in range(4):
            c0, c1 = i * 1536, (i + 1) * 1536
            em("gpsimd", lambda e, c0=c0, c1=c1: e.dma_start(
                out=wqkvT_bf[:, c0:c1], in_=wqkvT[:, c0:c1]).then_inc(sCAST, 16))
        em("gpsimd", lambda e: e.dma_start(out=wgateT_bf, in_=wgateT).then_inc(sCAST, 16))
        em("gpsimd", lambda e: e.dma_start(out=woutT_bf, in_=woutT).then_inc(sCAST, 16))
        for _ in range(6):
            sched["gpsimd"].append(("i", None, 0, 0))
        CASTS_ALL = 6 * 16

        def wait_cast(eng):
            em(eng, lambda e: e.wait_ge(sCAST, CASTS_ALL))
            sched[eng].append(("i", None, 0, 0))
        cast_pts = {}

        for d_, s_ in [(identb[:], identb_in), (identf[:], identf_in),
                       (onesb[:], onesb_in), (onesf[:], onesf_in),
                       (idxT[:], idxT_in), (kdcolT[:], kdcolT_in),
                       (c256B[:], c256_in), (expoB[:], expo_in),
                       (shcol[:], shcol_in), (lshcol[:], lshcol_in),
                       (isb1[:], isb1_in), (epsc[:], eps_in),
                       (negsB[:], bass.AP(s_row.tensor, 0, [[0, 128], [1, NH]])),
                       (normw_c[:], normw),
                       (cT_sb[:], bass.AP(cT.tensor, 0, [[B, 128], [128 * B, FC], [1, B]]))]:
            lds(d_, s_, "CN")
        consts_ld = NCNT["CN"]

        waitn("vector", "CN", consts_ld)
        op("vector", lambda e: e.tensor_scalar(out=negsB[:], in0=negsB[:], scalar1=-1.0,
                                               scalar2=None, op0=AL.mult), "DV")
        negs_rdy = C["DV"]
        waitn("scalar", "CN", consts_ld)
        op("scalar", lambda e: e.activation(out=cT_sb[:], in_=cT_sb[:], func=AF.Silu), "AC")
        ct_rdy = C["AC"]
        wait("scalar", "DV", negs_rdy)
        for h in range(NH):
            op("scalar", lambda e, h=h: e.activation(
                out=bdj_all[:, h, :], in_=c256B[:], func=AF.Exp, scale=negsB[:, h:h + 1]))
            last = h == NH - 1
            op("scalar", lambda e, h=h: e.activation(
                out=coef_all[:, h, :], in_=expoB[:], func=AF.Exp, scale=negsB[:, h:h + 1]),
                "AC" if last else None)
        dbt_st = {}
        for h in range(NH):
            sl = h % 2
            op("scalar", lambda e, h=h: e.activation(
                out=kdcol_all[:, h, :], in_=kdcolT[:], func=AF.Exp,
                scale=negsB[:, h:h + 1]))
            if h >= 2:
                wait("scalar", "SC", dbt_st[h - 2])
            a = op("scalar", lambda e, h=h, sl=sl: e.activation(
                out=DBT[:, 2 * sl:2 * sl + 2, :], in_=idxT[:], func=AF.Exp,
                scale=negsB[:, h:h + 1]), "AC")
            wait("scalar", "AC", a)
            dbt_st[h] = op("scalar", lambda e, h=h, sl=sl: e.dma_start(
                out=bass.AP(dbt_dram.tensor, h * 128 * 512, [[512, 128], [1, 512]]),
                in_=DBT[:, 2 * sl:2 * sl + 2, :]), "SC", 16)
        kdcl_rdy = C["AC"]

        mod_ld = {}
        NMC = 12

        def modbuf(bs):
            t = big[:, 32 + 4 * bs:36 + 4 * bs, :].rearrange("p a b -> p (a b)")
            return t.bitcast(dt.float32).rearrange("p (a b) -> p a b", a=FC)

        def emit_mod_load(mc):
            bs = mc % 2
            if mc >= 2:
                wait("sync", "PE", late[("modmm", mc - 2)])
            s_ = bass.AP(wmod_s.tensor, mc * 128, [[1536, 128], [128 * 1536, FC], [1, 128]])
            mod_ld[mc] = ("SA" if bs == 0 else "SB", lds(modbuf(bs), s_,
                                                         "SA" if bs == 0 else "SB"))

        emit_mod_load(0)
        emit_mod_load(1)
        for mc in range(NMC):
            bs = mc % 2
            waitn("tensor", mod_ld[mc][0], mod_ld[mc][1])
            wait("tensor", "AC", ct_rdy)
            if mc >= 2:
                wait("tensor", "DV", late[("modev", mc - 2)])
            rhs_t = modbuf(bs)
            for fc in range(FC):
                lastm = fc == FC - 1
                op("tensor", lambda e, fc=fc, bs=bs, r=rhs_t, lastm=lastm: e.matmul(
                    mainA[0:2, bs, 0:128], cT_sb[:, fc, :], r[:, fc, :],
                    start=(fc == 0), stop=lastm), "PE" if lastm else None)
            late[("modmm", mc)] = C["PE"]
            wait("vector", "PE", lateget(("modmm", mc), "PE"))
            if mc >= 2:
                waitn("vector", "MST", late[("modst", mc - 2)])
            late[("modev", mc)] = op("vector", lambda e, mc=mc, bs=bs: e.tensor_copy(
                out=modin2[:, bs, :], in_=mainA[0:2, bs, 0:128]), "DV")
            if mc + 2 < NMC:
                emit_mod_load(mc + 2)
            wait("sync", "DV", lateget(("modev", mc), "DV"))
            late[("modst", mc)] = lds(mod_in[:, mc * 128:(mc + 1) * 128],
                                      modin2[:, bs, :], "MST")
        main_guard = {0: ("DV", late[("modev", NMC - 1)]), 1: None}
        modst = late[("modst", NMC - 1)]
        waitn("gpsimd", "MST", modst)
        em("gpsimd", lambda e: e.collective_compute(
            "AllGather", AL.bypass, replica_groups=[list(range(8))],
            ins=[mod_in.opt()], outs=[mod_all.opt()]).then_inc(sAGM, 1))
        C["CC"] += 1
        sched["gpsimd"].append(("i", "CC", 1, 0))
        modag = 1
        em("sync", lambda e: e.wait_ge(sAGM, 1))
        sched["sync"].append(("w", "CC", 1, 0))
        lds(modB0, bass.AP(mod_all.tensor, 0, [[2 * 1536, 8], [128, 12], [1, 128]]), "M1")
        modrows_ld = lds(modB1, bass.AP(mod_all.tensor, 1536,
                                        [[2 * 1536, 8], [128, 12], [1, 128]]), "M1")
        waitn("tensor", "M1", modrows_ld)
        op("tensor", lambda e: e.matmul(psQK[:, 0, 0:96], modB0, identf[0:96, 0:96],
                                        start=True, stop=True))
        op("tensor", lambda e: e.matmul(psQK[:, 1, 0:96], modB1, identf[0:96, 0:96],
                                        start=True, stop=True), "PE")
        modtr = C["PE"]
        wait("vector", "PE", modtr)
        op("vector", lambda e: e.tensor_scalar(out=modsel, in0=psQK[:, 1, 0:96],
                                               scalar1=isb1[:, 0:1], scalar2=None, op0=AL.mult))
        op("vector", lambda e: e.scalar_tensor_tensor(
            out=mod_cols[:], in0=psQK[:, 0, 0:96], scalar=isb1[:, 1:2], in1=modsel,
            op0=AL.mult, op1=AL.add), "DV")
        modc_rdy = C["DV"]
        if "dbg_mod" in dbg_outs:
            wait("sync", "DV", modc_rdy)
            ld(dbg_outs["dbg_mod"], mod_cols[:])
        if "dbg_wmodbf" in dbg_outs:
            wait_cast("sync")
            ld(dbg_outs["dbg_wmodbf"], wmod_bf)
        if "dbg_wqkvbf" in dbg_outs:
            wait_cast("sync")
            ld(dbg_outs["dbg_wqkvbf"], wqkvT_bf[:, 0:256])
        if "dbg_modin" in dbg_outs:
            waitn("sync", "MST", modst)
            ld(dbg_outs["dbg_modin"], mod_in)
        if "dbg_modall" in dbg_outs:
            wait("sync", "CC", modag)
            ld(dbg_outs["dbg_modall"], mod_all)
        op("vector", lambda e: e.tensor_scalar(out=dcol[:, :, 0], in0=mod_cols[:, 16:32],
                                               scalar1=1.0, scalar2=None, op0=AL.add))
        op("vector", lambda e: e.tensor_scalar(out=dcol[:, :, 1], in0=mod_cols[:, 64:80],
                                               scalar1=1.0, scalar2=None, op0=AL.add))
        for j, lo in ((2, 32), (3, 80)):
            op("vector", lambda e, lo=lo: e.tensor_scalar(
                out=gscr[:, :, 0], in0=mod_cols[:, lo:lo + 16], scalar1=0.25, scalar2=None,
                op0=AL.max))
            op("vector", lambda e, lo=lo: e.tensor_scalar(
                out=gscr[:, :, 1], in0=mod_cols[:, lo:lo + 16], scalar1=-0.25, scalar2=None,
                op0=AL.min))
            op("vector", lambda e: e.tensor_tensor(
                out=gscr[:, :, 0], in0=gscr[:, :, 0], in1=gscr[:, :, 1], op=AL.subtract))
            op("vector", lambda e, lo=lo, j=j: e.tensor_scalar(
                out=dcol[:, :, j], in0=mod_cols[:, lo:lo + 16], scalar1=0.0, scalar2=None,
                op0=AL.is_ge))
            op("vector", lambda e, j=j: e.tensor_tensor(
                out=dcol[:, :, j], in0=dcol[:, :, j], in1=gscr[:, :, 0], op=AL.mult))
            op("vector", lambda e, j=j: e.tensor_tensor(
                out=dcol[:, :, j], in0=dcol[:, :, j], in1=gscr[:, :, 1], op=AL.add), "DV")
        dcol_rdy = C["DV"]
        mark(1)

        def xslot(fc):
            t = big[:, 2 * fc:2 * fc + 2, :].rearrange("p a b -> p (a b)")
            return t.bitcast(dt.float32)
        for fc in range(FC):
            src = bass.AP(xT.tensor, fc * 128 * NT, [[NT, 128], [1, NT]])
            lds(xslot(fc), src, "XL")
        xl_all = NCNT["XL"]
        sqslot = [sqscr[:], scrb[:]]
        for fc in range(FC):
            waitn("scalar", "XL", xl_all)
            if fc >= 2:
                wait("scalar", "PE", lateget(("st1", fc - 2), "PE"))
            late[("xsq", fc)] = op("scalar", lambda e, fc=fc: e.activation(
                out=sqslot[fc % 2], in_=xslot(fc), func=AF.Square), "AC")
            waitn("vector", "XL", xl_all)
            late[("xcp", fc)] = op("vector", lambda e, fc=fc: e.tensor_copy(
                out=h_fm[:, fc, :], in_=xslot(fc)), "DV")
        for fc in range(FC):
            lag = min(fc + 1, FC - 1)
            wait("tensor", "DV", late[("xcp", lag)])
            wait("tensor", "AC", late[("xsq", lag)])
            if fc == 0:
                wait("tensor", "DV", modc_rdy)
            first, lastf = fc == 0, fc == FC - 1
            op("tensor", lambda e, fc=fc, first=first, lastf=lastf: e.matmul(
                stat_ap[0], onesb[:], h_fm[:, fc, 0:512], start=first, stop=lastf))
            op("tensor", lambda e, fc=fc, first=first, lastf=lastf: e.matmul(
                stat_ap[1], onesb[:], h_fm[:, fc, 512:1024], start=first, stop=lastf))
            op("tensor", lambda e, fc=fc, first=first, lastf=lastf: e.matmul(
                stat_ap[2], onesb[:], sqslot[fc % 2][:, 0:512], start=first, stop=lastf))
            late[("st1", fc)] = op("tensor", lambda e, fc=fc, lastf=lastf, first=first: e.matmul(
                stat_ap[3], onesb[:], sqslot[fc % 2][:, 512:1024], start=first, stop=lastf),
                "PE")
        ln1_stats = C["PE"]

        def replicate(src_row, dst, guardB):
            waitg("tensor", guardB)
            for half in range(2):
                op("tensor", lambda e, h=half, s=src_row: e.matmul(
                    mainB[:, h, :], onesf[:], s[:, h * 512:(h + 1) * 512],
                    start=True, stop=True), "PE" if half == 1 else None)
            wait("vector", "PE", C["PE"])
            return op("vector", lambda e, t=dst: e.tensor_copy(
                out=t, in_=mainB[:].rearrange("p a b -> p (a b)")), "DV")

        def stats_pipeline(epsi, stats_pt, rms_only, mean_dst, rstd_dst, guardB):
            wait("scalar", "PE", stats_pt)
            rep_m = None
            if not rms_only:
                op("scalar", lambda e: e.activation(out=rows[:, 0, 0:512], in_=stat_ap[0],
                                                    func=AF.Copy, scale=1.0 / DM), "AC")
                mean_rdy = op("scalar", lambda e: e.activation(
                    out=rows[:, 0, 512:1024], in_=stat_ap[1],
                    func=AF.Copy, scale=1.0 / DM), "AC")
                wait("tensor", "AC", mean_rdy)
                rep_m = replicate(rows[:, 0, :], mean_dst, guardB)
                wait("scalar", "PE", C["PE"])
                acsync()
                op("scalar", lambda e: e.activation(out=rows[:, 1, :], in_=rows[:, 0, :],
                                                    func=AF.Square), "AC")
                acsync()
            op("scalar", lambda e: e.activation(out=rows[:, 0, 0:512], in_=stat_ap[2],
                                                func=AF.Copy, scale=1.0 / DM), "AC")
            pss_free = op("scalar", lambda e: e.activation(
                out=rows[:, 0, 512:1024], in_=stat_ap[3],
                func=AF.Copy, scale=1.0 / DM), "AC")
            wait("vector", "AC", pss_free)
            if not rms_only:
                dv = op("vector", lambda e: e.tensor_tensor(
                    out=rows[:, 0, :], in0=rows[:, 0, :], in1=rows[:, 1, :],
                    op=AL.subtract), "DV")
                wait("scalar", "DV", dv)
            op("scalar", lambda e, epsi=epsi: e.activation(
                out=rows[:, 0, :], in_=rows[:, 0, :],
                func=AF.Sqrt, bias=epsc[0:1, epsi:epsi + 1]), "AC")
            wait("vector", "AC", C["AC"])
            rr = op("vector", lambda e: e.reciprocal(out=rows[:, 1, :], in_=rows[:, 0, :]), "DV")
            wait("tensor", "DV", rr)
            gb2 = ("DV", rep_m) if rep_m is not None else guardB
            rep_r = replicate(rows[:, 1, :], rstd_dst, gb2)
            return rep_r, pss_free

        ln1_rep, pss_free = stats_pipeline(0, ln1_stats, False, repA[:], repB[:],
                                           main_guard[1])
        main_guard[1] = ("DV", ln1_rep)
        if "dbg_reps" in dbg_outs:
            wait("sync", "DV", ln1_rep)
            ld(dbg_outs["dbg_reps"][:, 0, :], repA[:])
            ld(dbg_outs["dbg_reps"][:, 1, :], repB[:])
        wait("vector", "DV", ln1_rep)
        wait("vector", "DV", dcol_rdy)
        for fc in range(FC):
            op("vector", lambda e, fc=fc: e.tensor_tensor(
                out=scrb[:], in0=h_fm[:, fc, :], in1=repA[:], op=AL.subtract))
            op("vector", lambda e, fc=fc: e.tensor_tensor(
                out=h_fm[:, fc, :], in0=scrb[:], in1=repB[:], op=AL.mult))
            op("vector", lambda e, fc=fc: e.tensor_scalar(
                out=h_fm[:, fc, :], in0=h_fm[:, fc, :], scalar1=dcol[:, fc:fc + 1, 0],
                scalar2=mod_cols[:, fc:fc + 1], op0=AL.mult, op1=AL.add),
                "DV" if fc == FC - 1 else None)
        h_rdy = C["DV"]
        if "dbg_h" in dbg_outs:
            wait("sync", "DV", h_rdy)
            ld(dbg_outs["dbg_h"], h_fm[:])
        mark(2)

        dec_steps = []
        dl_cnt = [0, 0]
        ds_cnt = [0, 0]

        def emit_decode(n_ch, w_ap, sc_ap, o_ap, t0, t1, src16, srcsc, dst, width, store_l,
                        key):
            for i in range(n_ch):
                bs = i % 2
                if i >= 2:
                    em("gpsimd", lambda e, bs=bs, v=ds_cnt[bs] * 16: e.wait_ge(sDS[bs], v))
                    sched["gpsimd"].append(("w", "DS%d" % bs, ds_cnt[bs] * 16, 0))
                em("gpsimd", lambda e, i=i, bs=bs: e.dma_start(
                    out=w_ap[bs], in_=bass.AP(src16.tensor, 32 * i * width,
                                              [[width, 32], [0, 4], [1, width]])
                ).then_inc(sDL[bs], 16))
                em("gpsimd", lambda e, i=i, bs=bs: e.dma_start(
                    out=sc_ap[bs], in_=bass.AP(srcsc.tensor, i * width,
                                               [[0, 128], [1, width]])
                ).then_inc(sDL[bs], 16))
                C["GP"] += 32
                sched["gpsimd"].append(("i", "DL%d" % bs, 32, 0))
                dl_cnt[bs] += 2
                ldpt = (bs, dl_cnt[bs] * 16)

                def vec_step(i=i, bs=bs, ldpt=ldpt, key=key):
                    em("vector", lambda e, b=ldpt[0], v=ldpt[1]: e.wait_ge(sDL[b], v))
                    sched["vector"].append(("w", "DL%d" % ldpt[0], ldpt[1], 0))
                    waitn("vector", "CN", consts_ld)
                    op("vector", lambda e: e.tensor_scalar(
                        out=t0, in0=w_ap[bs], scalar1=shcol[:, 0:1], scalar2=7,
                        op0=AL.logical_shift_right, op1=AL.bitwise_and))
                    op("vector", lambda e: e.tensor_scalar(
                        out=t1, in0=t0, scalar1=28, scalar2=512,
                        op0=AL.add, op1=AL.mult))
                    op("vector", lambda e: e.tensor_scalar(
                        out=t0, in0=w_ap[bs], scalar1=lshcol[:, 0:1], scalar2=-32768,
                        op0=AL.logical_shift_left, op1=AL.bitwise_and))
                    op("vector", lambda e: e.tensor_scalar(
                        out=w_ap[bs].bitcast(dt.float16), in0=t1.bitcast(dt.float16),
                        scalar1=2.0, scalar2=1.0, op0=AL.mult, op1=AL.subtract))
                    op("vector", lambda e: e.tensor_tensor(
                        out=w_ap[bs].bitcast(dt.float16), in0=t1.bitcast(dt.float16),
                        in1=w_ap[bs].bitcast(dt.float16), op=AL.min))
                    op("vector", lambda e: e.tensor_tensor(
                        out=w_ap[bs], in0=w_ap[bs], in1=t0, op=AL.bitwise_xor))
                    late[(key, i)] = op("vector", lambda e: e.tensor_tensor(
                        out=o_ap[bs], in0=w_ap[bs].bitcast(dt.float16), in1=sc_ap[bs],
                        op=AL.mult), "DV")
                dec_steps.append(vec_step)
                wait("gpsimd", "DV", lateget((key, i), "DV"))
                em("gpsimd", lambda e, i=i, bs=bs: e.dma_start(
                    out=dst[i * 128:(i + 1) * 128, :], in_=o_ap[bs]).then_inc(sDS[bs], 16))
                C["GP"] += 16
                sched["gpsimd"].append(("i", "DS%d" % bs, 16, 0))
                ds_cnt[bs] += 1
                store_l.append(C["GP"])

        gu_store = []
        emit_decode(FC, dec_w, dec_sc, dec_o, dec_t0, dec_t1, gu16, gusc, gu_loc, GU_O,
                    gu_store, "gudec")
        em("gpsimd", lambda e, v=ds_cnt[0] * 16: e.wait_ge(sDS[0], v))
        sched["gpsimd"].append(("w", "DS0", ds_cnt[0] * 16, 0))
        em("gpsimd", lambda e, v=ds_cnt[1] * 16: e.wait_ge(sDS[1], v))
        sched["gpsimd"].append(("w", "DS1", ds_cnt[1] * 16, 0))
        em("gpsimd", lambda e: e.collective_compute(
            "AllGather", AL.bypass, replica_groups=[list(range(8))],
            ins=[gu_loc.opt()], outs=[gu_all.opt()]).then_inc(sAGG, 1))
        C["CC"] += 1
        sched["gpsimd"].append(("i", "CC", 1, 0))
        gu_ag = ("AGG", 1)
        if "dbg_guw" in dbg_outs:
            em("sync", lambda e: e.wait_ge(sAGG, 1))
            sched["sync"].append(("w", "CC", 2, 0))
            ld(dbg_outs["dbg_guw"], gu_loc)
        dn_w = [dview(16896, [128, DN_F], dt.int16), dview(17408, [128, DN_F], dt.int16)]
        dn_t0 = dview(17920, [128, DN_F], dt.int16)
        dn_t1 = dview(18432, [128, DN_F], dt.int16)
        dn_sc = [dview(18944, [128, DN_F], dt.float16), dview(19456, [128, DN_F], dt.float16)]
        dn_o = [dview(19968, [128, DN_F], dt.bfloat16), dview(20480, [128, DN_F], dt.bfloat16)]
        dn_store = []
        emit_decode(NIC, dn_w, dn_sc, dn_o, dn_t0, dn_t1, dn16, dnsc, dn_loc, DN_F, dn_store,
                    "dndec")
        for step in dec_steps:
            step()
        em("gpsimd", lambda e, v=ds_cnt[0] * 16: e.wait_ge(sDS[0], v))
        sched["gpsimd"].append(("w", "DS0", ds_cnt[0] * 16, 0))
        em("gpsimd", lambda e, v=ds_cnt[1] * 16: e.wait_ge(sDS[1], v))
        sched["gpsimd"].append(("w", "DS1", ds_cnt[1] * 16, 0))
        em("gpsimd", lambda e: e.collective_compute(
            "AllGather", AL.bypass, replica_groups=[list(range(8))],
            ins=[dn_loc.opt()], outs=[dn_all.opt()]).then_inc(sAGD, 1))
        C["CC"] += 1
        sched["gpsimd"].append(("i", "CC", 1, 0))
        dn_ag = ("AGD", 1)
        if "dbg_dnw" in dbg_outs:
            em("sync", lambda e: e.wait_ge(sAGD, 1))
            sched["sync"].append(("w", "CC", 3, 0))
            ld(dbg_outs["dbg_dnw"], dn_loc)

        mark(3)
        o_all = big[:, 0:16, :]
        q_all = big[:, 16:32, :]
        k_fm_s = [big[:, 32, :], big[:, 37, :]]
        kdk_tm_s = [big[:, 33, :].rearrange("p (a b) -> p a b", a=8),
                    big[:, 38, :].rearrange("p (a b) -> p a b", a=8)]
        v_tm_s = [big[:, 34, :].rearrange("p (a b) -> p a b", a=8),
                  big[:, 39, :].rearrange("p (a b) -> p a b", a=8)]
        qkD_s = [big[:, 35, :].rearrange("p (a b) -> p a b", a=2),
                 big[:, 40, :].rearrange("p (a b) -> p a b", a=2)]
        vscr = big[:, 36, :]

        qkv_ld = []
        for ck in range(48):
            bs = ck % 2
            wait_cast("sync")
            if ck >= 2:
                wait("sync", "PE", lateget(("qkvuse", ck - 2), "PE"))
            src = bass.AP(wqkvT_bf.tensor, ck * 128, [[6144, 128], [128 * 6144, FC], [1, 128]])
            dst = strm[:, bs * 2:bs * 2 + 2, :].rearrange("p a b -> p (a b)")
            dst = dst.rearrange("p (a b) -> p a b", a=FC)
            qkv_ld.append(("SA" if bs == 0 else "SB", lds(dst, src, "SA" if bs == 0 else "SB")))

        def strm_tile(bs):
            t = strm[:, bs * 2:bs * 2 + 2, :].rearrange("p a b -> p (a b)")
            return t.rearrange("p (a b) -> p a b", a=FC)

        def proj(oc_global, psum, rhs_src, guard, ldpt_list, usekey):
            ck = oc_global
            bs = ck % 2
            waitn("tensor", ldpt_list[ck][0], ldpt_list[ck][1])
            waitg("tensor", guard)
            for fc in range(FC):
                for th in range(2):
                    lastm = fc == FC - 1 and th == 1
                    op("tensor", lambda e, fc=fc, th=th, bs=bs, ps=psum, r=rhs_src:
                       e.matmul(ps[:, th, :], strm_tile(bs)[:, fc, :],
                                r[:, fc, th * 512:(th + 1) * 512],
                                start=(fc == 0), stop=(fc == FC - 1)),
                       "PE" if lastm else None)
            late[(usekey, ck)] = max(late.get((usekey, ck), 0), C["PE"])
            return C["PE"]

        kc_st = []
        kv_done = {}
        qk_ev = modc_rdy
        in_ev = None
        kvup = None
        prev_vtm = {0: None, 1: None}
        prev_kdk = {0: None, 1: None}
        dbt_rdy_s = {}
        wait("tensor", "DV", h_rdy)

        def proj_head(h):
            sl = h % 2
            k_fm = k_fm_s[sl]
            kdk_tm = kdk_tm_s[sl]
            v_tm = v_tm_s[sl]
            qpt = proj(3 * h + 0, mainA, h_fm, main_guard[0], qkv_ld, "qkvuse")
            wait("scalar", "PE", qpt)
            main_guard[0] = ("AC", op("scalar", lambda e, h=h: e.activation(
                out=q_all[:, h, :], in_=mainA[:].rearrange("p a b -> p (a b)"),
                func=AF.Silu), "AC"))
            kpt = proj(3 * h + 1, mainB, h_fm, main_guard[1], qkv_ld, "qkvuse")
            wait("scalar", "PE", kpt)
            main_guard[1] = ("AC", op("scalar", lambda e, k_fm=k_fm: e.activation(
                out=k_fm[:], in_=mainB[:].rearrange("p a b -> p (a b)"),
                func=AF.Silu), "AC"))
            k_ev = main_guard[1][1]
            if h >= 2:
                wait("scalar", "DV", lateget(("qkdone", h - 2), "DV"))
            dbt_rdy_s[h] = op("scalar", lambda e, h=h, sl=sl: e.dma_start(
                out=DBT[:, 2 * sl:2 * sl + 2, :],
                in_=bass.AP(dbt_dram.tensor, h * 128 * 512, [[512, 128], [1, 512]])),
                "SC", 16)
            vpt = proj(3 * h + 2, mainA, h_fm, main_guard[0], qkv_ld, "qkvuse")
            wait("scalar", "PE", vpt)
            main_guard[0] = ("AC", op("scalar", lambda e: e.activation(
                out=vscr[:], in_=mainA[:].rearrange("p a b -> p (a b)"),
                func=AF.Silu), "AC"))
            v_ev = main_guard[0][1]
            if h >= 2:
                wait("scalar", "DV", kv_done[h - 2])
                kc_st.append(op("scalar", lambda e, g=h - 2: e.dma_start(
                    out=kc_in[g * HD:(g + 1) * HD, :], in_=kv[:, g % 2, :]), "SC", 16))

            wait("tensor", "AC", k_ev)
            if h == 0:
                wait("tensor", "AC", pss_free)
            if prev_vtm[sl] is not None:
                wait("tensor", "DV", prev_vtm[sl])
            if prev_vtm[1 - sl] is not None:
                wait("tensor", "DV", prev_vtm[1 - sl])
            kdk_ev = None
            for half in range(2):
                for i in range(4):
                    g = half * 4 + i
                    op("tensor", lambda e, g=g, i=i, k_fm=k_fm: e.matmul(
                        psT[:, i, :], k_fm[:, g * 128:(g + 1) * 128], identb[:],
                        start=True, stop=True), "PE" if i == 3 else None)
                ktr = C["PE"]
                wait("vector", "PE", ktr)
                if half == 0 and h == 0:
                    wait("vector", "AC", kdcl_rdy)
                for i in range(4):
                    g = half * 4 + i
                    op("vector", lambda e, g=g, i=i, h=h, kdk_tm=kdk_tm: e.tensor_scalar(
                        out=kdk_tm[:, g, :], in0=psT[:, i, :],
                        scalar1=kdcol_all[:, h, (g % 2):(g % 2) + 1], scalar2=None,
                        op0=AL.mult), "DV" if i == 3 else None)
                kdk_ev = C["DV"]
                if half == 0:
                    wait("tensor", "DV", kdk_ev)
            prev_kdk[sl] = kdk_ev
            late[("kdkdone", h)] = kdk_ev
            wait("tensor", "AC", v_ev)
            wait("tensor", "DV", kdk_ev)
            vtm_ev = None
            for half in range(2):
                for i in range(4):
                    g = half * 4 + i
                    op("tensor", lambda e, g=g, i=i: e.matmul(
                        psT[:, i, :], vscr[:, g * 128:(g + 1) * 128], identb[:],
                        start=True, stop=True), "PE" if i == 3 else None)
                vtr = C["PE"]
                wait("vector", "PE", vtr)
                vtm_ev = op("vector", lambda e, half=half, v_tm=v_tm: e.tensor_copy(
                    out=v_tm[:, half * 4:half * 4 + 4, :].rearrange("p a b -> p (a b)"),
                    in_=psT[:].rearrange("p a b -> p (a b)")), "DV")
                if half == 0:
                    wait("tensor", "DV", vtm_ev)
            prev_vtm[sl] = vtm_ev
            late[("projdone", h)] = (kdk_ev, vtm_ev)

        def attn_head(h):
            nonlocal qk_ev, in_ev, kvup
            sl = h % 2
            k_fm = k_fm_s[sl]
            kdk_tm = kdk_tm_s[sl]
            v_tm = v_tm_s[sl]
            qkD = qkD_s[sl]
            kdk_ev, vtm_ev = late[("projdone", h)]
            wait("vector", "SC", dbt_rdy_s[h])
            for j in range(4):
                wait("tensor", "DV", qk_ev)
                for nt in range(2):
                    op("tensor", lambda e, j=j, nt=nt, h=h, k_fm=k_fm: e.matmul(
                        psQK[:, nt, :],
                        k_fm[:, j * 256 + nt * 128: j * 256 + nt * 128 + 128],
                        q_all[:, h, j * 256:(j + 1) * 256], start=True, stop=True),
                        "PE" if nt == 1 else None)
                qkpt = C["PE"]
                wait("vector", "PE", qkpt)
                op("vector", lambda e, qkD=qkD, sl=sl: e.tensor_tensor(
                    out=qkD[:, 0:2, 0:256], in0=psQK[:],
                    in1=DBT[:, 2 * sl:2 * sl + 2, :], op=AL.mult), "DV")
                qk_ev = C["DV"]
                wait("tensor", "DV", qk_ev)
                wait("tensor", "DV", in_ev)
                wait("tensor", "DV", kvup)
                wait("tensor", "DV", vtm_ev)
                for nt in range(2):
                    op("tensor", lambda e, j=j, nt=nt, v_tm=v_tm, qkD=qkD: e.matmul(
                        psIN[:, 0, :], v_tm[:, j * 2 + nt, :], qkD[:, nt, 0:256],
                        start=(nt == 0), stop=(nt == 1)), "PE" if nt == 1 else None)
                inpt = C["PE"]
                wait("vector", "PE", inpt)
                in_ev = op("vector", lambda e, j=j, h=h: e.tensor_copy(
                    out=o_all[:, h, j * 256:(j + 1) * 256], in_=psIN[:, 0, :]), "DV")
                wait("tensor", "DV", in_ev)
                wait("tensor", "DV", kdk_ev)
                for nt in range(2):
                    op("tensor", lambda e, j=j, nt=nt, kdk_tm=kdk_tm, v_tm=v_tm: e.matmul(
                        psIN[:, 1, 0:128], kdk_tm[:, j * 2 + nt, :], v_tm[:, j * 2 + nt, :],
                        start=(nt == 0), stop=(nt == 1)), "PE" if nt == 1 else None)
                cpt = C["PE"]
                wait("vector", "PE", cpt)
                dst = kv[:, sl, :] if j == 3 else lall[:, 3 * h + j, :]
                if j == 0:
                    kvup_pt = op("vector", lambda e, dst=dst: e.tensor_copy(
                        out=dst, in_=psIN[:, 1, 0:128]), "DV")
                else:
                    if j == 3 and h > 1:
                        wait("vector", "SC", kc_st[h - 2])
                    src = lall[:, 3 * h + j - 1, :]
                    kvup_pt = op("vector", lambda e, h=h, dst=dst, src=src:
                                 e.scalar_tensor_tensor(
                        out=dst, in0=src, scalar=bdj_all[:, h, 0:1],
                        in1=psIN[:, 1, 0:128], op0=AL.mult, op1=AL.add), "DV")
                kvup = kvup_pt
            late[("qkdone", h)] = qk_ev
            kv_done[h] = C["DV"]

        for h in range(NH):
            proj_head(h)
            attn_head(h)
        for g in (NH - 2, NH - 1):
            wait("scalar", "DV", kv_done[g])
            kc_st.append(op("scalar", lambda e, g=g: e.dma_start(
                out=kc_in[g * HD:(g + 1) * HD, :], in_=kv[:, g % 2, :]), "SC", 16))
        if "dbg_kc" in dbg_outs:
            wait("sync", "SC", kc_st[-1])
            ld(dbg_outs["dbg_kc"], kc_in)

        mark(4)
        wait("gpsimd", "SC", kc_st[-1])
        em("gpsimd", lambda e: e.collective_compute(
            "AllGather", AL.bypass, replica_groups=[[0, 1, 2, 3], [4, 5, 6, 7]],
            ins=[kc_in.opt()], outs=[kc_all.opt()]).then_inc(sAGK, 1))
        C["CC"] += 1
        sched["gpsimd"].append(("i", "CC", 1, 0))
        kc_ag = C["CC"]

        em("sync", lambda e: e.wait_ge(sAGG, 1))
        sched["sync"].append(("w", "CC", 2, 0))
        arr_ld = lds(arrB4, arrB4_in, "M1")
        inter_ev = {0: ("DV", qk_ev), 1: ("DV", max(in_ev or 0, kvup or 0))}
        prev_sj_use = None
        prev_kvin = None
        for h in range(NH):
            em("scalar", lambda e: e.wait_ge(sAGK, 1))
            sched["scalar"].append(("w", "CC", kc_ag, 0))
            if prev_kvin is not None:
                wait("scalar", "DV", prev_kvin)
            kcl = op("scalar", lambda e, h=h: e.dma_start(
                out=kcbuf[:], in_=bass.AP(kc_all.tensor, h * HD * HD,
                                          [[HD, 128], [NH * HD * HD, 4], [1, HD]])), "SC", 16)
            if h == 0:
                waitn("scalar", "M1", arr_ld)
            op("scalar", lambda e, h=h: e.activation(
                out=qdB4, in_=arrB4, func=AF.Exp, scale=negsB[:, h:h + 1]), "AC")
            qd_rdy = C["AC"]
            wait("vector", "SC", kcl)
            op("vector", lambda e, h=h: e.tensor_scalar(
                out=kvin[:], in0=kcbuf[:, 0, :], scalar1=coef_all[:, h, 0:1],
                scalar2=None, op0=AL.mult))
            for g in (1, 2):
                op("vector", lambda e, h=h, g=g: e.scalar_tensor_tensor(
                    out=kvin[:], in0=kcbuf[:, g, :], scalar=coef_all[:, h, g:g + 1],
                    in1=kvin[:], op0=AL.mult, op1=AL.add),
                    "DV" if g == 2 else None)
            prev_kvin = C["DV"]
            if prev_sj_use is not None:
                wait("vector", "PE", prev_sj_use)
            op("vector", lambda e: e.tensor_copy(out=Sj[:, 0, :], in_=kvin[:]))
            for j in (1, 2, 3):
                op("vector", lambda e, h=h, j=j: e.scalar_tensor_tensor(
                    out=Sj[:, j, :], in0=kvin[:], scalar=bdj_all[:, h, j - 1:j],
                    in1=lall[:, 3 * h + j - 1, :], op0=AL.mult, op1=AL.add),
                    "DV" if j == 3 else None)
            s_rdy = C["DV"]
            for j in range(4):
                ps = psQK if j % 2 == 0 else psIN
                wait("tensor", "DV", s_rdy)
                waitg("tensor", inter_ev[j % 2])
                op("tensor", lambda e, h=h, j=j, ps=ps: e.matmul(
                    ps[:, 0, :], Sj[:, j, :], q_all[:, h, j * 256:(j + 1) * 256],
                    start=True, stop=True), "PE")
                ipt = C["PE"]
                if j == 3:
                    prev_sj_use = ipt
                wait("vector", "PE", ipt)
                wait("vector", "AC", qd_rdy)
                op("vector", lambda e, j=j, ps=ps: e.tensor_tensor(
                    out=scrb[:, 0:256], in0=ps[:, 0, :], in1=qdB4[:, j * 256:(j + 1) * 256],
                    op=AL.mult))
                inter_ev[j % 2] = ("DV", op("vector", lambda e, h=h, j=j: e.tensor_tensor(
                    out=o_all[:, h, j * 256:(j + 1) * 256],
                    in0=o_all[:, h, j * 256:(j + 1) * 256],
                    in1=scrb[:, 0:256], op=AL.add), "DV"))
        o_rdy = C["DV"]
        if "dbg_o" in dbg_outs:
            wait("sync", "DV", o_rdy)
            ld(dbg_outs["dbg_o"], o_all)
        mark(5)

        wait("scalar", "AC", pss_free)
        for fc in range(FC):
            wait("scalar", "DV", o_rdy)
            if fc >= 1:
                wait("scalar", "PE", lateget(("st2", fc - 1), "PE"))
            a = op("scalar", lambda e, fc=fc: e.activation(
                out=sqscr[:], in_=o_all[:, fc, :], func=AF.Square), "AC")
            wait("tensor", "AC", a)
            if fc == 0:
                wait("tensor", "AC", pss_free)
                wait("tensor", "DV", o_rdy)
                wait("tensor", "DV", prev_vtm[0])
                wait("tensor", "DV", prev_vtm[1])
            first, lastf = fc == 0, fc == FC - 1
            op("tensor", lambda e, first=first, lastf=lastf: e.matmul(
                stat_ap[2], onesb[:], sqscr[:, 0:512], start=first, stop=lastf))
            late[("st2", fc)] = op("tensor", lambda e, lastf=lastf, first=first: e.matmul(
                stat_ap[3], onesb[:], sqscr[:, 512:1024], start=first, stop=lastf), "PE")
        rms_stats = C["PE"]
        rms_rep, pss_free = stats_pipeline(1, rms_stats, True, None, repB[:],
                                           main_guard[1])
        main_guard[1] = ("DV", rms_rep)

        g_fm = q_all
        gate_ld = []
        for ck in range(16):
            bs = ck % 2
            wait_cast("sync")
            if ck < 2:
                wait("sync", "PE", lateget(("qkvuse", 46 + ck), "PE"))
            else:
                wait("sync", "PE", lateget(("gateuse", ck - 2), "PE"))
            src = bass.AP(wgateT_bf.tensor, ck * 128, [[DM, 128], [128 * DM, FC], [1, 128]])
            dst = strm[:, bs * 2:bs * 2 + 2, :].rearrange("p a b -> p (a b)")
            dst = dst.rearrange("p (a b) -> p a b", a=FC)
            gate_ld.append(("SA" if bs == 0 else "SB", lds(dst, src, "SA" if bs == 0 else "SB")))
        first_gate = True
        for oc in range(FC):
            ps = mainA if oc % 2 == 0 else mainB
            guard = main_guard[oc % 2]
            if first_gate:
                wait("tensor", "DV", o_rdy)
                first_gate = False
            pt = proj(oc, ps, h_fm, guard, gate_ld, "gateuse")
            wait("scalar", "PE", pt)
            main_guard[oc % 2] = ("AC", op("scalar", lambda e, oc=oc, ps=ps: e.activation(
                out=g_fm[:, oc, :], in_=ps[:].rearrange("p a b -> p (a b)"),
                func=AF.Sigmoid), "AC"))
        gate_done = main_guard[1][1]
        wait("vector", "AC", main_guard[0][1])
        wait("vector", "AC", gate_done)
        wait("vector", "DV", rms_rep)
        for fc in range(FC):
            op("vector", lambda e, fc=fc: e.tensor_tensor(
                out=scrb[:], in0=o_all[:, fc, :], in1=repB[:], op=AL.mult))
            op("vector", lambda e, fc=fc: e.scalar_tensor_tensor(
                out=o_all[:, fc, :], in0=scrb[:], scalar=normw_c[:, fc:fc + 1],
                in1=g_fm[:, fc, :], op0=AL.mult, op1=AL.mult),
                "DV" if fc == FC - 1 else None)
        gated_rdy = C["DV"]

        wout_ld = {}
        xo_ld = {}

        def emit_outproj_loads(fc):
            bs = fc % 2
            wait_cast("sync")
            if fc < 2:
                wait("sync", "PE", lateget(("gateuse", 14 + fc), "PE"))
            else:
                wait("sync", "PE", late[("woutuse", fc - 2)])
            s_ = bass.AP(woutT_bf.tensor, fc * 128, [[DM, 128], [128 * DM, FC], [1, 128]])
            d_ = strm[:, bs * 2:bs * 2 + 2, :].rearrange("p a b -> p (a b)")
            d_ = d_.rearrange("p (a b) -> p a b", a=FC)
            wout_ld[fc] = ("SA" if bs == 0 else "SB", lds(d_, s_, "SA" if bs == 0 else "SB"))
            if fc >= 2:
                wait("sync", "DV", late[("x2ev", fc - 2)])
            s_ = bass.AP(xT.tensor, fc * 128 * NT, [[NT, 128], [1, NT]])
            xo_ld[fc] = lds(xorig[bs], s_, "XOa" if bs == 0 else "XOb")

        emit_outproj_loads(0)
        emit_outproj_loads(1)
        x2st = []
        for fc in range(FC):
            bs = fc % 2
            ps = mainA if fc % 2 == 0 else mainB
            wait("tensor", "DV", gated_rdy)
            waitn("tensor", wout_ld[fc][0], wout_ld[fc][1])
            waitg("tensor", main_guard[fc % 2])
            wbs = fc % 2
            for oc in range(FC):
                for th in range(2):
                    lastm = oc == FC - 1 and th == 1

                    def mm(e, fc=fc, oc=oc, th=th, wbs=wbs, ps=ps):
                        t = strm_tile(wbs)
                        return e.matmul(ps[:, th, :], t[:, oc, :],
                                        o_all[:, oc, th * 512:(th + 1) * 512],
                                        start=(oc == 0), stop=(oc == FC - 1))
                    op("tensor", mm, "PE" if lastm else None)
            late[("woutuse", fc)] = C["PE"]
            opj = C["PE"]
            wait("vector", "PE", opj)
            waitn("vector", "XOa" if fc % 2 == 0 else "XOb", xo_ld[fc])
            if fc >= 2:
                waitn("vector", "SPa" if fc % 2 == 0 else "SPb",
                      lateget(("x2st", fc - 2), "LD"))
            op("vector", lambda e, fc=fc, bs=bs, ps=ps: e.scalar_tensor_tensor(
                out=xtile[bs], in0=ps[:].rearrange("p a b -> p (a b)"),
                scalar=dcol[:, fc:fc + 1, 2], in1=xorig[bs], op0=AL.mult, op1=AL.add), "DV")
            x2ev = C["DV"]
            late[("x2ev", fc)] = x2ev
            main_guard[fc % 2] = ("DV", x2ev)
            op("vector", lambda e, fc=fc, bs=bs: e.tensor_copy(
                out=h_fm[:, fc, :], in_=xtile[bs]), "DV")
            x2bf = C["DV"]
            wait("scalar", "DV", x2ev)
            if fc >= 1:
                wait("scalar", "PE", lateget(("st3", fc - 1), "PE"))
            a = op("scalar", lambda e, bs=bs: e.activation(
                out=sqscr[:], in_=xtile[bs], func=AF.Square), "AC")
            wait("tensor", "DV", x2bf)
            wait("tensor", "AC", a)
            if fc == 0:
                wait("tensor", "AC", pss_free)
            first, lastf = fc == 0, fc == FC - 1
            op("tensor", lambda e, fc=fc, first=first, lastf=lastf: e.matmul(
                stat_ap[0], onesb[:], h_fm[:, fc, 0:512], start=first, stop=lastf))
            op("tensor", lambda e, fc=fc, first=first, lastf=lastf: e.matmul(
                stat_ap[1], onesb[:], h_fm[:, fc, 512:1024], start=first, stop=lastf))
            op("tensor", lambda e, fc=fc, first=first, lastf=lastf: e.matmul(
                stat_ap[2], onesb[:], sqscr[:, 0:512], start=first, stop=lastf))
            late[("st3", fc)] = op("tensor", lambda e, fc=fc, lastf=lastf, first=first: e.matmul(
                stat_ap[3], onesb[:], sqscr[:, 512:1024], start=first, stop=lastf), "PE")
            wait("sync", "DV", x2ev)
            st = lds(bass.AP(x2_dram.tensor, fc * 128 * NT, [[NT, 128], [1, NT]]),
                     xtile[bs], "SPa" if bs == 0 else "SPb")
            late[("x2st", fc)] = st
            x2st.append(st)
            if fc + 2 < FC:
                emit_outproj_loads(fc + 2)
        ln2_stats = C["PE"]
        if "dbg_x2" in dbg_outs:
            waitn("sync", "SPa", lambda: NCNT["SPa"])
            waitn("sync", "SPb", lambda: NCNT["SPb"])
            ld(dbg_outs["dbg_x2"], x2_dram)
        ln2_rep, pss_free = stats_pipeline(0, ln2_stats, False, repA[:], repB[:],
                                           main_guard[1])
        main_guard[1] = ("DV", ln2_rep)
        wait("vector", "DV", ln2_rep)
        for fc in range(FC):
            op("vector", lambda e, fc=fc: e.tensor_tensor(
                out=scrb[:], in0=h_fm[:, fc, :], in1=repA[:], op=AL.subtract))
            op("vector", lambda e, fc=fc: e.tensor_tensor(
                out=h_fm[:, fc, :], in0=scrb[:], in1=repB[:], op=AL.mult))
            op("vector", lambda e, fc=fc: e.tensor_scalar(
                out=h_fm[:, fc, :], in0=h_fm[:, fc, :], scalar1=dcol[:, fc:fc + 1, 1],
                scalar2=mod_cols[:, 48 + fc:48 + fc + 1], op0=AL.mult, op1=AL.add),
                "DV" if fc == FC - 1 else None)
        h2_rdy = C["DV"]
        mark(6)

        act = big[:, 0:NIC, :]

        def gu_tile(bs):
            t = strm[:, bs * 2:bs * 2 + 2, :].rearrange("p a b -> p (a b)")
            return t.rearrange("p (a b) -> p a b", a=FC)

        gu_ldp = {}

        def emit_gu_load(i, half):
            oc = i if half == 0 else 44 + i
            bs = half
            em("sync", lambda e: e.wait_ge(sAGG, 1))
            sched["sync"].append(("w", "CC", 2, 0))
            if i == 0:
                wait("sync", "PE", lateget(("woutuse", 14 + half), "PE"))
            else:
                wait("sync", "PE", late[("guuse", (i - 1) if half == 0 else 44 + (i - 1))])
            rnk, jj = oc // 11, oc % 11
            s_ = bass.AP(gu_all.tensor, rnk * DM * GU_O + jj * 128,
                         [[GU_O, 128], [128 * GU_O, FC], [1, 128]])
            gu_ldp[oc] = ("SA" if bs == 0 else "SB", lds(gu_tile(bs), s_,
                                                         "SA" if bs == 0 else "SB"))

        emit_gu_load(0, 0)
        emit_gu_load(0, 1)
        for i in range(NIC):
            for half, oc in ((0, i), (1, 44 + i)):
                bs = half
                ps = mainA if half == 0 else mainB
                waitn("tensor", gu_ldp[oc][0], gu_ldp[oc][1])
                if i == 0 and half == 0:
                    wait("tensor", "DV", h2_rdy)
                waitg("tensor", main_guard[half])
                for fc in range(FC):
                    for th in range(2):
                        lastm = fc == FC - 1 and th == 1
                        op("tensor", lambda e, fc=fc, th=th, bs=bs, ps=ps: e.matmul(
                            ps[:, th, :], gu_tile(bs)[:, fc, :],
                            h_fm[:, fc, th * 512:(th + 1) * 512],
                            start=(fc == 0), stop=(fc == FC - 1)), "PE" if lastm else None)
                late[("guuse", oc)] = C["PE"]
                if i + 1 < NIC:
                    emit_gu_load(i + 1, half)
                if half == 0:
                    gpt = C["PE"]
                else:
                    upt = C["PE"]
            wait("scalar", "PE", gpt)
            main_guard[0] = ("AC", op("scalar", lambda e: e.activation(
                out=scrb[:], in_=mainA[:].rearrange("p a b -> p (a b)"), func=AF.Silu), "AC"))
            wait("vector", "PE", upt)
            wait("vector", "AC", main_guard[0][1])
            main_guard[1] = ("DV", op("vector", lambda e, i=i: e.tensor_tensor(
                out=act[:, i, :], in0=mainB[:].rearrange("p a b -> p (a b)"),
                in1=scrb[:], op=AL.mult), "DV"))
        act_rdy = main_guard[1][1]
        if "dbg_act" in dbg_outs:
            wait("sync", "DV", act_rdy)
            ld(dbg_outs["dbg_act"], act)
        mark(7)

        y_fm = h_fm

        def dn_tile(q):
            flat = strm[:].rearrange("p a b -> p (a b)")
            off = 0 if q % 2 == 0 else 2048
            return flat[:, off:off + 11 * 128].rearrange("p (a b) -> p a b", a=11)

        dn_ldp = {}
        qidx = 0
        for fc2 in range(FC):
            g_, coff = fc2 // 2, (fc2 % 2) * 128
            for q in range(4):
                em("sync", lambda e: e.wait_ge(sAGD, 1))
                sched["sync"].append(("w", "CC", 3, 0))
                if qidx < 2:
                    wait("sync", "PE", lateget(("guuse", 86 + qidx), "PE"))
                else:
                    wait("sync", "PE", lateget(("dnuse", qidx - 2), "PE"))
                src = bass.AP(dn_all.tensor,
                              g_ * I5 * DN_F + q * 11 * 128 * DN_F + coff,
                              [[DN_F, 128], [128 * DN_F, 11], [1, 128]])
                ch_ = "SA" if q % 2 == 0 else "SB"
                dn_ldp[(fc2, q)] = (ch_, lds(dn_tile(q), src, ch_))
                qidx += 1
        qidx = 0
        for fc2 in range(FC):
            ps = mainA if fc2 % 2 == 0 else mainB
            if fc2 == 0:
                wait("tensor", "DV", act_rdy)
            waitg("tensor", main_guard[fc2 % 2])
            for q in range(4):
                waitn("tensor", dn_ldp[(fc2, q)][0], dn_ldp[(fc2, q)][1])
                for icq in range(11):
                    ic = q * 11 + icq
                    for th in range(2):
                        lastm = icq == 10 and th == 1
                        op("tensor", lambda e, q=q, icq=icq, ic=ic, th=th, ps=ps: e.matmul(
                            ps[:, th, :], dn_tile(q)[:, icq, :],
                            act[:, ic, th * 512:(th + 1) * 512],
                            start=(ic == 0), stop=(ic == NIC - 1)), "PE" if lastm else None)
                late[("dnuse", qidx)] = C["PE"]
                qidx += 1
            ypt = C["PE"]
            wait("vector", "PE", ypt)
            yev = op("vector", lambda e, fc2=fc2, ps=ps: e.tensor_copy(
                out=y_fm[:, fc2, :], in_=ps[:].rearrange("p a b -> p (a b)")), "DV")
            main_guard[fc2 % 2] = ("DV", yev)
            wait("scalar", "DV", yev)
            if fc2 >= 1:
                wait("scalar", "PE", lateget(("st4", fc2 - 1), "PE"))
            a = op("scalar", lambda e, fc2=fc2: e.activation(
                out=sqscr[:], in_=y_fm[:, fc2, :], func=AF.Square), "AC")
            wait("tensor", "AC", a)
            if fc2 == 0:
                wait("tensor", "AC", pss_free)
            first, lastf = fc2 == 0, fc2 == FC - 1
            op("tensor", lambda e, first=first, lastf=lastf: e.matmul(
                stat_ap[2], onesb[:], sqscr[:, 0:512], start=first, stop=lastf))
            late[("st4", fc2)] = op("tensor", lambda e, lastf=lastf, first=first: e.matmul(
                stat_ap[3], onesb[:], sqscr[:, 512:1024], start=first, stop=lastf), "PE")
        y_stats = C["PE"]
        if "dbg_y" in dbg_outs:
            wait("sync", "DV", main_guard[0][1])
            wait("sync", "DV", main_guard[1][1])
            ld(dbg_outs["dbg_y"], y_fm[:])
        y_rep, pss_free = stats_pipeline(1, y_stats, True, None, repF,
                                         main_guard[1])
        main_guard[1] = ("DV", y_rep)

        waitn("sync", "SPa", lambda: None if not x2st else NCNT["SPa"])
        waitn("sync", "SPb", lambda: NCNT["SPb"])
        x2re = []
        for fc in range(2):
            src = bass.AP(x2_dram.tensor, fc * 128 * NT, [[NT, 128], [1, NT]])
            x2re.append(lds(xorig[fc], src, "XOa" if fc % 2 == 0 else "XOb"))
        tr_guard = {0: None, 1: None}
        tr_pe = {}
        for fc in range(FC):
            bs = fc % 2
            wait("vector", "DV", y_rep)
            waitn("vector", "XOa" if fc % 2 == 0 else "XOb", x2re[fc])
            if fc >= 2:
                wait("vector", "PE", tr_pe[fc - 2])
            op("vector", lambda e, fc=fc: e.tensor_tensor(
                out=fscr, in0=y_fm[:, fc, :], in1=repF, op=AL.mult))
            op("vector", lambda e, fc=fc, bs=bs: e.scalar_tensor_tensor(
                out=xtile[bs], in0=fscr, scalar=dcol[:, fc:fc + 1, 3], in1=xorig[bs],
                op0=AL.mult, op1=AL.add), "DV")
            cmb = C["DV"]
            ps = mainA if fc % 2 == 0 else mainB
            wait("tensor", "DV", cmb)
            waitg("tensor", main_guard[fc % 2] if fc < 2 else tr_guard[fc % 2])
            for tc in range(8):
                op("tensor", lambda e, tc=tc, bs=bs, ps=ps: e.matmul(
                    ps[:].rearrange("p a b -> p (a b)")[:, tc * 128:(tc + 1) * 128],
                    xtile[bs][:, tc * 128:(tc + 1) * 128], identf[:],
                    start=True, stop=True), "PE" if tc == 7 else None)
            tr_pe[fc] = C["PE"]
            wait("vector", "PE", tr_pe[fc])
            if fc >= 2:
                waitn("vector", "OUTa" if fc % 2 == 0 else "OUTb",
                      lateget(("outst", fc - 2), "LD"))
            tr_guard[fc % 2] = ("DV", op("vector", lambda e, fc=fc, ps=ps, bs=bs: e.tensor_copy(
                out=stg2[:, bs, :], in_=ps[:].rearrange("p a b -> p (a b)")), "DV"))
            wait("sync", "DV", tr_guard[fc % 2][1])
            dst = bass.AP(y_out.tensor, fc * 128, [[DM, 128], [128 * DM, 8], [1, 128]])
            late[("outst", fc)] = lds(dst, stg2[:, bs, :].rearrange("p (a b) -> p a b", a=8),
                                      "OUTa" if bs == 0 else "OUTb")
            if fc + 2 < FC:
                src2 = bass.AP(x2_dram.tensor, (fc + 2) * 128 * NT, [[NT, 128], [1, NT]])
                x2re.append(lds(xorig[bs], src2, "XOa" if bs == 0 else "XOb"))
        waitn("sync", "OUTa", lambda: NCNT["OUTa"])
        waitn("sync", "OUTb", lambda: NCNT["OUTb"])
        wait("sync", "LD", C["LD"])
        for _ch in ("CN", "SA", "SB", "MST", "M1", "XL", "XOa", "XOb", "SPa", "SPb"):
            waitn("sync", _ch, lambda c=_ch: NCNT[c] or None)

        nc._sched = sched
        if stage < 99 and stage in marks:
            mk = marks[stage]
            for e in prog:
                del prog[e][mk[e]:]
                del sched[e][mk[e]:]
            cnt = mk["cnt"]
            em("sync", lambda e: e.wait_ge(sLD, cnt["LD"]))
            em("scalar", lambda e: e.wait_ge(sSC, cnt["SC"]))
            em("gpsimd", lambda e: e.wait_ge(sGP, 0) if cnt["GP"] == 0 else e.wait_ge(sCAST, CASTS_ALL))
        @block.sync
        def _(e):
            for fn in prog["sync"]:
                fn(e)

        @block.gpsimd
        def _(e):
            for fn in prog["gpsimd"]:
                fn(e)

        @block.tensor
        def _(e):
            for fn in prog["tensor"]:
                fn(e)

        @block.vector
        def _(e):
            for fn in prog["vector"]:
                fn(e)

        @block.scalar
        def _(e):
            for fn in prog["scalar"]:
                fn(e)

    return nc


def _consts(core):
    arr = np.arange(1, 257, dtype=np.float32)
    arrB4 = np.tile(np.tile(arr, 4)[None, :], (128, 1))
    m = np.arange(256)
    d = m[None, :] - m[:, None]
    idxT = np.where(d >= 0, d, 1e5).astype(np.float32)
    idxT = np.ascontiguousarray(idxT.reshape(2, 128, 256))
    idxT = np.ascontiguousarray(idxT.transpose(1, 0, 2))
    p = np.arange(128)
    kdcolT = np.zeros((128, 2), np.float32)
    for t in range(2):
        kdcolT[:, t] = 255.0 - (128 * t + p)
    c256 = np.zeros((128, 4), np.float32)
    c256[:, :] = np.array([256.0, 512.0, 768.0, 1024.0])[None, :]
    chunk = core % 4
    expo = np.full((128, 4), 1e9, np.float32)
    for g in range(4):
        if g < chunk:
            expo[:, g] = 1024.0 * (chunk - 1 - g)
    j = (p % 4).astype(np.int16)
    shcol = (4 * j).reshape(128, 1)
    lshcol = (12 - 4 * j).reshape(128, 1)
    isb1 = np.zeros((128, 2), np.float32)
    isb1[:, 0] = 1.0 if core >= 4 else 0.0
    isb1[:, 1] = 1.0 - isb1[:, 0]
    return dict(
        identb_in=np.eye(128, dtype=bf16),
        identf_in=np.eye(128, dtype=np.float32),
        onesb_in=np.ones((128, 1), dtype=bf16),
        onesf_in=np.ones((1, 128), dtype=np.float32),
        arrB4_in=arrB4.astype(np.float32), idxT_in=idxT, kdcolT_in=kdcolT,
        c256_in=c256, expo_in=expo, shcol_in=shcol, lshcol_in=lshcol,
        isb1_in=isb1,
        eps_in=np.array([[1e-5, 1e-6]], dtype=np.float32),
    )


def make_inmaps(inputs):
    x = np.asarray(inputs["L0_attn_hidden_states"], dtype=np.float32)
    slope = np.asarray(inputs["L0_attn_slope_rate"], dtype=np.float32).reshape(1, NH)
    wqkv = np.asarray(inputs["L0_attn_w_qkv"], dtype=np.float32)
    wgate = np.asarray(inputs["L0_attn_w_gate"], dtype=np.float32)
    wout = np.asarray(inputs["L0_attn_w_out"], dtype=np.float32)
    normw = np.ascontiguousarray(
        np.asarray(inputs["L0_attn_norm_w"], dtype=np.float32).reshape(FC, 128).T)
    gup = np.asarray(inputs["L0_ffn_gate_up_packed"], dtype=np.int32)
    gus = np.asarray(inputs["L0_ffn_gate_up_scales"], dtype=np.float16)
    dnp = np.asarray(inputs["L0_ffn_down_packed"], dtype=np.int32)
    dns = np.asarray(inputs["L0_ffn_down_scales"], dtype=np.float16)
    cc = np.asarray(inputs["L0_c"], dtype=np.float32)
    wmod = np.asarray(inputs["L0_wmod"], dtype=np.float32)

    wqkvT = np.ascontiguousarray(wqkv.T)
    wgateT = np.ascontiguousarray(wgate.T)
    woutT = np.ascontiguousarray(wout.T)
    cTm = np.ascontiguousarray(cc.T)
    gu16T = np.ascontiguousarray(gup.view(np.int16).T)
    dn16T = np.ascontiguousarray(dnp.view(np.int16).T)

    maps = []
    for r in range(8):
        b, ch = r // 4, r % 4
        t0 = ch * NT
        m = dict(
            xT=np.ascontiguousarray(x[b, t0:t0 + NT, :].T),
            cT=cTm,
            wmod_s=np.ascontiguousarray(wmod[:, 1536 * r:1536 * (r + 1)]),
            wqkvT=wqkvT, wgateT=wgateT, woutT=woutT,
            s_row=slope, normw=normw,
            gu16=np.ascontiguousarray(gu16T[:, GU_O * r:GU_O * (r + 1)]),
            gusc=np.ascontiguousarray(gus[:, GU_O * r:GU_O * (r + 1)]),
            dn16=np.ascontiguousarray(dn16T[:, DN_F * r:DN_F * (r + 1)]),
            dnsc=np.ascontiguousarray(dns[:, DN_F * r:DN_F * (r + 1)]),
        )
        m.update(_consts(r))
        maps.append(m)
    return maps


def kernel(**inputs):
    maps = make_inmaps(inputs)
    nc = build_nc()
    res = run_bass_kernel_spmd(nc, maps, core_ids=list(range(8)))
    out = np.zeros((B, N, DM), dtype=np.float32)
    for r in range(8):
        b, ch = r // 4, r % 4
        out[b, ch * NT:(ch + 1) * NT, :] = res.results[r]["y_out"]
    return out


# version 12
# speedup vs baseline: 1.0481x; 1.0167x over previous
import numpy as np
import ml_dtypes
import contextlib
import concourse.bass as bass
import concourse.mybir as mybir
from concourse.bass_utils import run_bass_kernel_spmd

dt = mybir.dt
AF = mybir.ActivationFunctionType
AL = mybir.AluOpType
bf16 = ml_dtypes.bfloat16

B, N, DM = 2, 4096, 2048
NH, HD = 16, 128
NT = 1024
FC = 16
I5 = 5632
GU_O = 1408
DN_F = 256
NIC = 44


def build_nc(dbg=(), stage=99):
    nc = bass.Bass("TRN2", target_bir_lowering=False)

    xT = nc.dram_tensor("xT", [DM, NT], dt.float32, kind="ExternalInput").ap()
    cT = nc.dram_tensor("cT", [DM, B], dt.float32, kind="ExternalInput").ap()
    wmod_s = nc.dram_tensor("wmod_s", [DM, 1536], dt.float32, kind="ExternalInput").ap()
    wqkvT = nc.dram_tensor("wqkvT", [DM, 6144], dt.float32, kind="ExternalInput").ap()
    wgateT = nc.dram_tensor("wgateT", [DM, DM], dt.float32, kind="ExternalInput").ap()
    woutT = nc.dram_tensor("woutT", [DM, DM], dt.float32, kind="ExternalInput").ap()
    s_row = nc.dram_tensor("s_row", [1, NH], dt.float32, kind="ExternalInput").ap()
    normw = nc.dram_tensor("normw", [128, FC], dt.float32, kind="ExternalInput").ap()
    gu16 = nc.dram_tensor("gu16", [512, GU_O], dt.int16, kind="ExternalInput").ap()
    gusc = nc.dram_tensor("gusc", [FC, GU_O], dt.float16, kind="ExternalInput").ap()
    dn16 = nc.dram_tensor("dn16", [1408, DN_F], dt.int16, kind="ExternalInput").ap()
    dnsc = nc.dram_tensor("dnsc", [NIC, DN_F], dt.float16, kind="ExternalInput").ap()
    identb_in = nc.dram_tensor("identb_in", [128, 128], dt.bfloat16, kind="ExternalInput").ap()
    identf_in = nc.dram_tensor("identf_in", [128, 128], dt.float32, kind="ExternalInput").ap()
    onesb_in = nc.dram_tensor("onesb_in", [128, 1], dt.bfloat16, kind="ExternalInput").ap()
    onesf_in = nc.dram_tensor("onesf_in", [1, 128], dt.float32, kind="ExternalInput").ap()
    arrB4_in = nc.dram_tensor("arrB4_in", [128, NT], dt.float32, kind="ExternalInput").ap()
    idxT_in = nc.dram_tensor("idxT_in", [128, 2, 256], dt.float32, kind="ExternalInput").ap()
    kdcolT_in = nc.dram_tensor("kdcolT_in", [128, 2], dt.float32, kind="ExternalInput").ap()
    c256_in = nc.dram_tensor("c256_in", [128, 4], dt.float32, kind="ExternalInput").ap()
    expo_in = nc.dram_tensor("expo_in", [128, 4], dt.float32, kind="ExternalInput").ap()
    shcol_in = nc.dram_tensor("shcol_in", [128, 1], dt.int16, kind="ExternalInput").ap()
    lshcol_in = nc.dram_tensor("lshcol_in", [128, 1], dt.int16, kind="ExternalInput").ap()
    isb1_in = nc.dram_tensor("isb1_in", [128, 2], dt.float32, kind="ExternalInput").ap()
    eps_in = nc.dram_tensor("eps_in", [1, 2], dt.float32, kind="ExternalInput").ap()

    wqkvT_bf = nc.dram_tensor("wqkvT_bf", [DM, 6144], dt.bfloat16).ap()
    wgateT_bf = nc.dram_tensor("wgateT_bf", [DM, DM], dt.bfloat16).ap()
    woutT_bf = nc.dram_tensor("woutT_bf", [DM, DM], dt.bfloat16).ap()
    wmod_bf = nc.dram_tensor("wmod_bf", [DM, 1536], dt.bfloat16).ap()
    mod_in = nc.dram_tensor("mod_in", [B, 1536], dt.float32).ap()
    mod_all = nc.dram_tensor("mod_all", [B * 8, 1536], dt.float32).ap()
    dbt_dram = nc.dram_tensor("dbt_dram", [NH * 128, 512], dt.float32).ap()
    kc_in = nc.dram_tensor("kc_in", [NH * HD, HD], dt.float32).ap()
    kc_all = nc.dram_tensor("kc_all", [4 * NH * HD, HD], dt.float32).ap()
    gu_loc = nc.dram_tensor("gu_loc", [DM, GU_O], dt.bfloat16).ap()
    gu_all = nc.dram_tensor("gu_all", [8 * DM, GU_O], dt.bfloat16).ap()
    dn_loc = nc.dram_tensor("dn_loc", [I5, DN_F], dt.bfloat16).ap()
    dn_all = nc.dram_tensor("dn_all", [8 * I5, DN_F], dt.bfloat16).ap()
    x2_dram = nc.dram_tensor("x2_dram", [DM, NT], dt.float32).ap()
    y_out = nc.dram_tensor("y_out", [NT, DM], dt.float32, kind="ExternalOutput").ap()

    dbg_outs = {}
    for name, shape, d_ in [
        ("dbg_mod", [128, 96], dt.float32),
        ("dbg_modin", [B, 1536], dt.float32),
        ("dbg_wmodbf", [DM, 1536], dt.bfloat16),
        ("dbg_reps", [128, 2, NT], dt.bfloat16),
        ("dbg_wqkvbf", [DM, 256], dt.bfloat16),
        ("dbg_modall", [16, 1536], dt.float32),
        ("dbg_h", [128, FC, NT], dt.bfloat16),
        ("dbg_o", [128, FC, NT], dt.bfloat16),
        ("dbg_x2", [DM, NT], dt.float32),
        ("dbg_act", [128, NIC, NT], dt.bfloat16),
        ("dbg_guw", [DM, GU_O], dt.bfloat16),
        ("dbg_y", [128, FC, NT], dt.bfloat16),
        ("dbg_dnw", [I5, DN_F], dt.bfloat16),
        ("dbg_kc", [NH * HD, HD], dt.float32),
    ]:
        if name in dbg:
            dbg_outs[name] = nc.dram_tensor(name, shape, d_, kind="ExternalOutput").ap()

    es = contextlib.ExitStack()
    with es:
        E = es.enter_context
        h_fm = E(nc.sbuf_tensor("h_fm", [128, FC, NT], dt.bfloat16))
        big = E(nc.sbuf_tensor("big", [128, NIC, NT], dt.bfloat16))
        strm = E(nc.sbuf_tensor("strm", [128, 4, NT], dt.bfloat16))
        dscr = E(nc.sbuf_tensor("dscr", [128, 22528], dt.uint8))
        lall = E(nc.sbuf_tensor("lall", [128, 48, HD], dt.bfloat16))
        rows = E(nc.sbuf_tensor("rows", [1, 2, NT], dt.float32))
        identb = E(nc.sbuf_tensor("identb", [128, 128], dt.bfloat16))
        identf = E(nc.sbuf_tensor("identf", [128, 128], dt.float32))
        onesb = E(nc.sbuf_tensor("onesb", [128, 1], dt.bfloat16))
        onesf = E(nc.sbuf_tensor("onesf", [1, 128], dt.float32))
        idxT = E(nc.sbuf_tensor("idxT", [128, 2, 256], dt.float32))
        kdcolT = E(nc.sbuf_tensor("kdcolT", [128, 2], dt.float32))
        c256B = E(nc.sbuf_tensor("c256B", [128, 4], dt.float32))
        expoB = E(nc.sbuf_tensor("expoB", [128, 4], dt.float32))
        shcol = E(nc.sbuf_tensor("shcol", [128, 1], dt.int16))
        lshcol = E(nc.sbuf_tensor("lshcol", [128, 1], dt.int16))
        isb1 = E(nc.sbuf_tensor("isb1", [128, 2], dt.float32))
        epsc = E(nc.sbuf_tensor("epsc", [1, 2], dt.float32))
        negsB = E(nc.sbuf_tensor("negsB", [128, NH], dt.float32))
        normw_c = E(nc.sbuf_tensor("normw_c", [128, FC], dt.float32))
        cT_sb = E(nc.sbuf_tensor("cT_sb", [128, FC, B], dt.float32))
        cT_bf = E(nc.sbuf_tensor("cT_bf", [128, FC, B], dt.bfloat16))
        mod_cols = E(nc.sbuf_tensor("mod_cols", [128, 96], dt.float32))
        dcol = E(nc.sbuf_tensor("dcol", [128, FC, 4], dt.float32))
        gscr = E(nc.sbuf_tensor("gscr", [128, FC, 2], dt.float32))
        DBT = E(nc.sbuf_tensor("DBT", [128, 4, 256], dt.float32))
        kdcol_all = E(nc.sbuf_tensor("kdcol_all", [128, NH, 2], dt.float32))
        bdj_all = E(nc.sbuf_tensor("bdj_all", [128, NH, 4], dt.float32))
        coef_all = E(nc.sbuf_tensor("coef_all", [128, NH, 4], dt.float32))
        kv = E(nc.sbuf_tensor("kv", [128, 2, HD], dt.float32))
        kvin = E(nc.sbuf_tensor("kvin", [128, HD], dt.float32))
        Sj = E(nc.sbuf_tensor("Sj", [128, 4, HD], dt.bfloat16))
        scrb = E(nc.sbuf_tensor("scrb", [128, NT], dt.bfloat16))
        sqscr = E(nc.sbuf_tensor("sqscr", [128, NT], dt.bfloat16))
        repA = E(nc.sbuf_tensor("repA", [128, NT], dt.bfloat16))
        repB = E(nc.sbuf_tensor("repB", [128, NT], dt.bfloat16))
        modin2 = E(nc.sbuf_tensor("modin2", [B, 2, 128], dt.float32))
        stg = E(nc.sbuf_tensor("stg", [128, 2, NT], dt.float32))
        stg2 = E(nc.sbuf_tensor("stg2", [128, 2, NT], dt.float32))

        def dview(off_b, shape, d_):
            n = 1
            for s in shape[1:]:
                n *= s
            isz = {dt.int16: 2, dt.float16: 2, dt.bfloat16: 2, dt.float32: 4}[d_]
            assert off_b % 4 == 0 and off_b + n * isz <= 24 * 1024
            ap = dscr[:, off_b:off_b + n * isz].bitcast(d_)
            if len(shape) == 3:
                ap = ap.rearrange("p (a b) -> p a b", a=shape[1])
            return ap

        dec_w = [dview(0, [128, GU_O], dt.int16), dview(2816, [128, GU_O], dt.int16)]
        dec_t0 = dview(5632, [128, GU_O], dt.int16)
        dec_t1 = dview(8448, [128, GU_O], dt.int16)
        dec_sc = [dview(11264, [128, GU_O], dt.float16), dview(14080, [128, GU_O], dt.float16)]
        dec_o = [dview(16896, [128, GU_O], dt.bfloat16), dview(19712, [128, GU_O], dt.bfloat16)]
        qdB4 = dview(0, [128, NT], dt.float32)
        fscr = dview(4096, [128, NT], dt.float32)
        arrB4 = dview(8192, [128, NT], dt.float32)
        repF = dview(12288, [128, NT], dt.float32)
        kcbuf = dview(16896, [128, 4, HD], dt.float32)
        xtile = [stg[:, 0, :], stg[:, 1, :]]
        xorig = [stg2[:, 0, :], stg2[:, 1, :]]
        modB0 = stg2[0:96, 0, 0:128]
        modB1 = stg2[0:96, 0, 128:256]
        modsel = stg2[:, 0, 256:352]

        sLD = E(nc.semaphore("sLD"))
        sGP = E(nc.semaphore("sGP"))
        sSC = E(nc.semaphore("sSC"))
        sPE = E(nc.semaphore("sPE"))
        sDV = E(nc.semaphore("sDV"))
        sAC = E(nc.semaphore("sAC"))
        sPO = E(nc.semaphore("sPO"))
        sCC = E(nc.semaphore("sCC"))
        sCAST = E(nc.semaphore("sCAST"))
        sDL = [E(nc.semaphore("sDL0")), E(nc.semaphore("sDL1"))]
        sDS = [E(nc.semaphore("sDS0")), E(nc.semaphore("sDS1"))]
        sAGM = E(nc.semaphore("sAGM"))
        sAGG = E(nc.semaphore("sAGG"))
        sAGD = E(nc.semaphore("sAGD"))
        sAGK = E(nc.semaphore("sAGK"))
        SEMX = {}
        for _n in ("CN", "SA", "SB", "MST", "M1", "XL", "XOa", "XOb",
                   "SPa", "SPb", "OUTa", "OUTb", "CW"):
            SEMX[_n] = E(nc.semaphore("s" + _n))

        mainA = E(nc.psum_tensor("mainA", [128, 2, 512], dt.float32))
        mainB = E(nc.psum_tensor("mainB", [128, 2, 512], dt.float32))
        psQK = E(nc.psum_tensor("psQK", [128, 2, 256], dt.float32))
        psIN = E(nc.psum_tensor("psIN", [128, 2, 256], dt.float32))
        psT = E(nc.psum_tensor("psT", [128, 4, 128], dt.float32))
        psS = E(nc.psum_tensor("psS", [128, 512], dt.float32))

        stat_ap = [
            psQK[0:1, :, :].rearrange("p a b -> p (a b)"),
            psIN[0:1, :, :].rearrange("p a b -> p (a b)"),
            psS[0:1, :],
            psT[0:1, :, :].rearrange("p a b -> p (a b)"),
        ]
        block = E(nc.Block())

        prog = {k: [] for k in ("sync", "gpsimd", "tensor", "vector", "scalar")}
        C = {k: 0 for k in ("LD", "GP", "SC", "PE", "DV", "AC", "PO", "CC")}
        SEM = {"LD": sLD, "GP": sGP, "SC": sSC, "PE": sPE, "DV": sDV,
               "AC": sAC, "PO": sPO, "CC": sCC}

        import sys
        sched = {k: [] for k in prog}

        def _ln():
            f = sys._getframe(2)
            return f.f_lineno

        def em(eng, fn):
            prog[eng].append(fn)

        def wait(eng, cname, val):
            if val is None:
                return
            if isinstance(val, tuple):
                wait(eng, val[0], val[1])
                return
            if callable(val):
                def f(e, g=val, cn=cname):
                    v = g()
                    if v is None:
                        return
                    if isinstance(v, tuple):
                        cn2, vv = v
                    else:
                        cn2, vv = cn, v
                    if vv:
                        e.wait_ge(SEM[cn2], vv)
                em(eng, f)
                sched[eng].append(("w", cname, val, _ln()))
                return
            if val <= 0:
                return
            em(eng, lambda e, s=SEM[cname], v=val: e.wait_ge(s, v))
            sched[eng].append(("w", cname, val, _ln()))

        def waitg(eng, guard):
            if guard is None:
                return
            if callable(guard):
                wait(eng, "PE", guard)
                return
            wait(eng, guard[0], guard[1])

        ENG_CN = {"vector": "DV", "scalar": "AC", "gpsimd": "PO"}

        def op(eng, fn, cname=None, n=1):
            if eng in ("vector", "scalar"):
                cn = ENG_CN[eng]
                if cname is not None and cname != cn:
                    C[cname] += n
                    em(eng, lambda e, f=fn, s=SEM[cname], n=n: f(e).then_inc(s, n))
                    sched[eng].append(("i", cname, n, _ln()))
                    return C[cname]
                prev = C[cn]
                if prev:
                    em(eng, lambda e, s=SEM[cn], v=prev: e.wait_ge(s, v))
                    sched[eng].append(("w", cn, prev, _ln()))
                C[cn] += 1
                em(eng, lambda e, f=fn, s=SEM[cn]: f(e).then_inc(s, 1))
                sched[eng].append(("i", cn, 1, _ln()))
                return C[cn]
            if cname is None:
                em(eng, lambda e, f=fn: f(e))
                sched[eng].append(("i", None, 0, _ln()))
                return None
            C[cname] += n
            em(eng, lambda e, f=fn, s=SEM[cname], n=n: f(e).then_inc(s, n))
            sched[eng].append(("i", cname, n, _ln()))
            return C[cname]

        def acsync():
            wait("scalar", "AC", C["AC"])

        import collections
        NCNT = collections.defaultdict(int)

        def lds(dst, src, ch, eng="sync"):
            NCNT[ch] += 16
            em(eng, lambda e, d=dst, s=src, h=SEMX[ch]: e.dma_start(
                out=d, in_=s).then_inc(h, 16))
            sched[eng].append(("i", ch, 16, _ln()))
            return NCNT[ch]

        def waitn(eng, ch, val):
            if val is None or (isinstance(val, int) and val <= 0):
                return
            if callable(val):
                def getv(g=val):
                    v = g()
                    if isinstance(v, tuple):
                        v = v[1]
                    return v or 0

                def f(e, h=SEMX[ch], gv=getv):
                    v = gv()
                    if v:
                        e.wait_ge(h, v)
                em(eng, f)
                sched[eng].append(("w", ch, getv, _ln()))
                return
            em(eng, lambda e, h=SEMX[ch], v=val: e.wait_ge(h, v))
            sched[eng].append(("w", ch, val, _ln()))

        def ld(dst, src, eng="sync"):
            return op(eng, lambda e, d=dst, s=src: e.dma_start(out=d, in_=s), "LD", 16)

        late = {}
        marks = {}

        def mark(st):
            marks[st] = {e: len(prog[e]) for e in prog}
            marks[st]["cnt"] = dict(C)
            marks[st]["dl"] = list(dl_cnt) if "dl_cnt" in dir() else None

        def lateget(key, kind):
            return lambda: (kind, late.get(key, 0))

        for i in range(4):
            c0, c1 = i * 1536, (i + 1) * 1536
            em("gpsimd", lambda e, c0=c0, c1=c1: e.dma_start(
                out=wqkvT_bf[:, c0:c1], in_=wqkvT[:, c0:c1]).then_inc(sCAST, 16))
        em("gpsimd", lambda e: e.dma_start(out=wgateT_bf, in_=wgateT).then_inc(sCAST, 16))
        em("gpsimd", lambda e: e.dma_start(out=woutT_bf, in_=woutT).then_inc(sCAST, 16))
        for _ in range(6):
            sched["gpsimd"].append(("i", None, 0, 0))
        CASTS_ALL = 6 * 16

        def wait_cast(eng):
            em(eng, lambda e: e.wait_ge(sCAST, CASTS_ALL))
            sched[eng].append(("i", None, 0, 0))
        cast_pts = {}

        for d_, s_ in [(identb[:], identb_in), (identf[:], identf_in),
                       (onesb[:], onesb_in), (onesf[:], onesf_in),
                       (idxT[:], idxT_in), (kdcolT[:], kdcolT_in),
                       (c256B[:], c256_in), (expoB[:], expo_in),
                       (shcol[:], shcol_in), (lshcol[:], lshcol_in),
                       (isb1[:], isb1_in), (epsc[:], eps_in),
                       (negsB[:], bass.AP(s_row.tensor, 0, [[0, 128], [1, NH]])),
                       (normw_c[:], normw),
                       (cT_sb[:], bass.AP(cT.tensor, 0, [[B, 128], [128 * B, FC], [1, B]]))]:
            lds(d_, s_, "CN")
        consts_ld = NCNT["CN"]

        waitn("vector", "CN", consts_ld)
        op("vector", lambda e: e.tensor_scalar(out=negsB[:], in0=negsB[:], scalar1=-1.0,
                                               scalar2=None, op0=AL.mult), "DV")
        negs_rdy = C["DV"]
        waitn("scalar", "CN", consts_ld)
        op("scalar", lambda e: e.activation(out=cT_sb[:], in_=cT_sb[:], func=AF.Silu), "AC")
        ct_rdy = C["AC"]
        wait("scalar", "DV", negs_rdy)
        for h in range(NH):
            op("scalar", lambda e, h=h: e.activation(
                out=bdj_all[:, h, :], in_=c256B[:], func=AF.Exp, scale=negsB[:, h:h + 1]))
            last = h == NH - 1
            op("scalar", lambda e, h=h: e.activation(
                out=coef_all[:, h, :], in_=expoB[:], func=AF.Exp, scale=negsB[:, h:h + 1]),
                "AC" if last else None)
        dbt_st = {}
        for h in range(NH):
            sl = h % 2
            op("scalar", lambda e, h=h: e.activation(
                out=kdcol_all[:, h, :], in_=kdcolT[:], func=AF.Exp,
                scale=negsB[:, h:h + 1]))
            if h >= 2:
                wait("scalar", "SC", dbt_st[h - 2])
            a = op("scalar", lambda e, h=h, sl=sl: e.activation(
                out=DBT[:, 2 * sl:2 * sl + 2, :], in_=idxT[:], func=AF.Exp,
                scale=negsB[:, h:h + 1]), "AC")
            wait("scalar", "AC", a)
            dbt_st[h] = op("scalar", lambda e, h=h, sl=sl: e.dma_start(
                out=bass.AP(dbt_dram.tensor, h * 128 * 512, [[512, 128], [1, 512]]),
                in_=DBT[:, 2 * sl:2 * sl + 2, :]), "SC", 16)
        kdcl_rdy = C["AC"]

        mod_ld = {}
        NMC = 12

        def modbuf(bs):
            t = big[:, 32 + 4 * bs:36 + 4 * bs, :].rearrange("p a b -> p (a b)")
            return t.bitcast(dt.float32).rearrange("p (a b) -> p a b", a=FC)

        def emit_mod_load(mc):
            bs = mc % 2
            if mc >= 2:
                wait("sync", "PE", late[("modmm", mc - 2)])
            s_ = bass.AP(wmod_s.tensor, mc * 128, [[1536, 128], [128 * 1536, FC], [1, 128]])
            mod_ld[mc] = ("SA" if bs == 0 else "SB", lds(modbuf(bs), s_,
                                                         "SA" if bs == 0 else "SB"))

        emit_mod_load(0)
        emit_mod_load(1)
        for mc in range(NMC):
            bs = mc % 2
            waitn("tensor", mod_ld[mc][0], mod_ld[mc][1])
            wait("tensor", "AC", ct_rdy)
            if mc >= 2:
                wait("tensor", "DV", late[("modev", mc - 2)])
            rhs_t = modbuf(bs)
            for fc in range(FC):
                lastm = fc == FC - 1
                op("tensor", lambda e, fc=fc, bs=bs, r=rhs_t, lastm=lastm: e.matmul(
                    mainA[0:2, bs, 0:128], cT_sb[:, fc, :], r[:, fc, :],
                    start=(fc == 0), stop=lastm), "PE" if lastm else None)
            late[("modmm", mc)] = C["PE"]
            wait("vector", "PE", lateget(("modmm", mc), "PE"))
            if mc >= 2:
                waitn("vector", "MST", late[("modst", mc - 2)])
            late[("modev", mc)] = op("vector", lambda e, mc=mc, bs=bs: e.tensor_copy(
                out=modin2[:, bs, :], in_=mainA[0:2, bs, 0:128]), "DV")
            if mc + 2 < NMC:
                emit_mod_load(mc + 2)
            wait("sync", "DV", lateget(("modev", mc), "DV"))
            late[("modst", mc)] = lds(mod_in[:, mc * 128:(mc + 1) * 128],
                                      modin2[:, bs, :], "MST")
        main_guard = {0: ("DV", late[("modev", NMC - 1)]), 1: None}
        modst = late[("modst", NMC - 1)]
        waitn("gpsimd", "MST", modst)
        em("gpsimd", lambda e: e.collective_compute(
            "AllGather", AL.bypass, replica_groups=[list(range(8))],
            ins=[mod_in.opt()], outs=[mod_all.opt()]).then_inc(sAGM, 1))
        C["CC"] += 1
        sched["gpsimd"].append(("i", "CC", 1, 0))
        modag = 1
        em("sync", lambda e: e.wait_ge(sAGM, 1))
        sched["sync"].append(("w", "CC", 1, 0))
        lds(modB0, bass.AP(mod_all.tensor, 0, [[2 * 1536, 8], [128, 12], [1, 128]]), "M1")
        modrows_ld = lds(modB1, bass.AP(mod_all.tensor, 1536,
                                        [[2 * 1536, 8], [128, 12], [1, 128]]), "M1")
        waitn("tensor", "M1", modrows_ld)
        op("tensor", lambda e: e.matmul(psQK[:, 0, 0:96], modB0, identf[0:96, 0:96],
                                        start=True, stop=True))
        op("tensor", lambda e: e.matmul(psQK[:, 1, 0:96], modB1, identf[0:96, 0:96],
                                        start=True, stop=True), "PE")
        modtr = C["PE"]
        wait("vector", "PE", modtr)
        op("vector", lambda e: e.tensor_scalar(out=modsel, in0=psQK[:, 1, 0:96],
                                               scalar1=isb1[:, 0:1], scalar2=None, op0=AL.mult))
        op("vector", lambda e: e.scalar_tensor_tensor(
            out=mod_cols[:], in0=psQK[:, 0, 0:96], scalar=isb1[:, 1:2], in1=modsel,
            op0=AL.mult, op1=AL.add), "DV")
        modc_rdy = C["DV"]
        if "dbg_mod" in dbg_outs:
            wait("sync", "DV", modc_rdy)
            ld(dbg_outs["dbg_mod"], mod_cols[:])
        if "dbg_wmodbf" in dbg_outs:
            wait_cast("sync")
            ld(dbg_outs["dbg_wmodbf"], wmod_bf)
        if "dbg_wqkvbf" in dbg_outs:
            wait_cast("sync")
            ld(dbg_outs["dbg_wqkvbf"], wqkvT_bf[:, 0:256])
        if "dbg_modin" in dbg_outs:
            waitn("sync", "MST", modst)
            ld(dbg_outs["dbg_modin"], mod_in)
        if "dbg_modall" in dbg_outs:
            wait("sync", "CC", modag)
            ld(dbg_outs["dbg_modall"], mod_all)
        op("vector", lambda e: e.tensor_scalar(out=dcol[:, :, 0], in0=mod_cols[:, 16:32],
                                               scalar1=1.0, scalar2=None, op0=AL.add))
        op("vector", lambda e: e.tensor_scalar(out=dcol[:, :, 1], in0=mod_cols[:, 64:80],
                                               scalar1=1.0, scalar2=None, op0=AL.add))
        for j, lo in ((2, 32), (3, 80)):
            op("vector", lambda e, lo=lo: e.tensor_scalar(
                out=gscr[:, :, 0], in0=mod_cols[:, lo:lo + 16], scalar1=0.25, scalar2=None,
                op0=AL.max))
            op("vector", lambda e, lo=lo: e.tensor_scalar(
                out=gscr[:, :, 1], in0=mod_cols[:, lo:lo + 16], scalar1=-0.25, scalar2=None,
                op0=AL.min))
            op("vector", lambda e: e.tensor_tensor(
                out=gscr[:, :, 0], in0=gscr[:, :, 0], in1=gscr[:, :, 1], op=AL.subtract))
            op("vector", lambda e, lo=lo, j=j: e.tensor_scalar(
                out=dcol[:, :, j], in0=mod_cols[:, lo:lo + 16], scalar1=0.0, scalar2=None,
                op0=AL.is_ge))
            op("vector", lambda e, j=j: e.tensor_tensor(
                out=dcol[:, :, j], in0=dcol[:, :, j], in1=gscr[:, :, 0], op=AL.mult))
            op("vector", lambda e, j=j: e.tensor_tensor(
                out=dcol[:, :, j], in0=dcol[:, :, j], in1=gscr[:, :, 1], op=AL.add), "DV")
        dcol_rdy = C["DV"]
        mark(1)

        def xslot(fc):
            t = big[:, 2 * fc:2 * fc + 2, :].rearrange("p a b -> p (a b)")
            return t.bitcast(dt.float32)
        for fc in range(FC):
            src = bass.AP(xT.tensor, fc * 128 * NT, [[NT, 128], [1, NT]])
            lds(xslot(fc), src, "XL")
        xl_all = NCNT["XL"]
        sqslot = [sqscr[:], scrb[:]]
        for fc in range(FC):
            waitn("scalar", "XL", xl_all)
            if fc >= 2:
                wait("scalar", "PE", lateget(("st1", fc - 2), "PE"))
            late[("xsq", fc)] = op("scalar", lambda e, fc=fc: e.activation(
                out=sqslot[fc % 2], in_=xslot(fc), func=AF.Square), "AC")
            waitn("vector", "XL", xl_all)
            late[("xcp", fc)] = op("vector", lambda e, fc=fc: e.tensor_copy(
                out=h_fm[:, fc, :], in_=xslot(fc)), "DV")
        for fc in range(FC):
            lag = min(fc + 1, FC - 1)
            wait("tensor", "DV", late[("xcp", lag)])
            wait("tensor", "AC", late[("xsq", lag)])
            if fc == 0:
                wait("tensor", "DV", modc_rdy)
            first, lastf = fc == 0, fc == FC - 1
            op("tensor", lambda e, fc=fc, first=first, lastf=lastf: e.matmul(
                stat_ap[0], onesb[:], h_fm[:, fc, 0:512], start=first, stop=lastf))
            op("tensor", lambda e, fc=fc, first=first, lastf=lastf: e.matmul(
                stat_ap[1], onesb[:], h_fm[:, fc, 512:1024], start=first, stop=lastf))
            op("tensor", lambda e, fc=fc, first=first, lastf=lastf: e.matmul(
                stat_ap[2], onesb[:], sqslot[fc % 2][:, 0:512], start=first, stop=lastf))
            late[("st1", fc)] = op("tensor", lambda e, fc=fc, lastf=lastf, first=first: e.matmul(
                stat_ap[3], onesb[:], sqslot[fc % 2][:, 512:1024], start=first, stop=lastf),
                "PE")
        ln1_stats = C["PE"]

        def replicate(src_row, dst, guardB):
            waitg("tensor", guardB)
            for half in range(2):
                op("tensor", lambda e, h=half, s=src_row: e.matmul(
                    mainB[:, h, :], onesf[:], s[:, h * 512:(h + 1) * 512],
                    start=True, stop=True), "PE" if half == 1 else None)
            wait("vector", "PE", C["PE"])
            return op("vector", lambda e, t=dst: e.tensor_copy(
                out=t, in_=mainB[:].rearrange("p a b -> p (a b)")), "DV")

        def stats_pipeline(epsi, stats_pt, rms_only, mean_dst, rstd_dst, guardB):
            wait("scalar", "PE", stats_pt)
            rep_m = None
            if not rms_only:
                op("scalar", lambda e: e.activation(out=rows[:, 0, 0:512], in_=stat_ap[0],
                                                    func=AF.Copy, scale=1.0 / DM), "AC")
                mean_rdy = op("scalar", lambda e: e.activation(
                    out=rows[:, 0, 512:1024], in_=stat_ap[1],
                    func=AF.Copy, scale=1.0 / DM), "AC")
                wait("tensor", "AC", mean_rdy)
                rep_m = replicate(rows[:, 0, :], mean_dst, guardB)
                wait("scalar", "PE", C["PE"])
                acsync()
                op("scalar", lambda e: e.activation(out=rows[:, 1, :], in_=rows[:, 0, :],
                                                    func=AF.Square), "AC")
                acsync()
            op("scalar", lambda e: e.activation(out=rows[:, 0, 0:512], in_=stat_ap[2],
                                                func=AF.Copy, scale=1.0 / DM), "AC")
            pss_free = op("scalar", lambda e: e.activation(
                out=rows[:, 0, 512:1024], in_=stat_ap[3],
                func=AF.Copy, scale=1.0 / DM), "AC")
            wait("vector", "AC", pss_free)
            if not rms_only:
                dv = op("vector", lambda e: e.tensor_tensor(
                    out=rows[:, 0, :], in0=rows[:, 0, :], in1=rows[:, 1, :],
                    op=AL.subtract), "DV")
                wait("scalar", "DV", dv)
            op("scalar", lambda e, epsi=epsi: e.activation(
                out=rows[:, 0, :], in_=rows[:, 0, :],
                func=AF.Sqrt, bias=epsc[0:1, epsi:epsi + 1]), "AC")
            wait("vector", "AC", C["AC"])
            rr = op("vector", lambda e: e.reciprocal(out=rows[:, 1, :], in_=rows[:, 0, :]), "DV")
            wait("tensor", "DV", rr)
            gb2 = ("DV", rep_m) if rep_m is not None else guardB
            rep_r = replicate(rows[:, 1, :], rstd_dst, gb2)
            return rep_r, pss_free

        ln1_rep, pss_free = stats_pipeline(0, ln1_stats, False, repA[:], repB[:],
                                           main_guard[1])
        main_guard[1] = ("DV", ln1_rep)
        if "dbg_reps" in dbg_outs:
            wait("sync", "DV", ln1_rep)
            ld(dbg_outs["dbg_reps"][:, 0, :], repA[:])
            ld(dbg_outs["dbg_reps"][:, 1, :], repB[:])
        wait("vector", "DV", ln1_rep)
        wait("vector", "DV", dcol_rdy)
        for fc in range(FC):
            op("vector", lambda e, fc=fc: e.tensor_tensor(
                out=scrb[:], in0=h_fm[:, fc, :], in1=repA[:], op=AL.subtract))
            op("vector", lambda e, fc=fc: e.tensor_tensor(
                out=h_fm[:, fc, :], in0=scrb[:], in1=repB[:], op=AL.mult))
            op("vector", lambda e, fc=fc: e.tensor_scalar(
                out=h_fm[:, fc, :], in0=h_fm[:, fc, :], scalar1=dcol[:, fc:fc + 1, 0],
                scalar2=mod_cols[:, fc:fc + 1], op0=AL.mult, op1=AL.add),
                "DV" if fc == FC - 1 else None)
        h_rdy = C["DV"]
        if "dbg_h" in dbg_outs:
            wait("sync", "DV", h_rdy)
            ld(dbg_outs["dbg_h"], h_fm[:])
        mark(2)

        dec_steps = []
        dl_cnt = [0, 0]
        ds_cnt = [0, 0]

        def emit_decode(n_ch, w_ap, sc_ap, o_ap, t0, t1, src16, srcsc, dst, width, store_l,
                        key):
            for i in range(n_ch):
                bs = i % 2
                if i >= 2:
                    em("gpsimd", lambda e, bs=bs, v=ds_cnt[bs] * 16: e.wait_ge(sDS[bs], v))
                    sched["gpsimd"].append(("w", "DS%d" % bs, ds_cnt[bs] * 16, 0))
                em("gpsimd", lambda e, i=i, bs=bs: e.dma_start(
                    out=w_ap[bs], in_=bass.AP(src16.tensor, 32 * i * width,
                                              [[width, 32], [0, 4], [1, width]])
                ).then_inc(sDL[bs], 16))
                em("gpsimd", lambda e, i=i, bs=bs: e.dma_start(
                    out=sc_ap[bs], in_=bass.AP(srcsc.tensor, i * width,
                                               [[0, 128], [1, width]])
                ).then_inc(sDL[bs], 16))
                C["GP"] += 32
                sched["gpsimd"].append(("i", "DL%d" % bs, 32, 0))
                dl_cnt[bs] += 2
                ldpt = (bs, dl_cnt[bs] * 16)

                def vec_step(i=i, bs=bs, ldpt=ldpt, key=key):
                    em("vector", lambda e, b=ldpt[0], v=ldpt[1]: e.wait_ge(sDL[b], v))
                    sched["vector"].append(("w", "DL%d" % ldpt[0], ldpt[1], 0))
                    waitn("vector", "CN", consts_ld)
                    op("vector", lambda e: e.tensor_scalar(
                        out=t0, in0=w_ap[bs], scalar1=shcol[:, 0:1], scalar2=7,
                        op0=AL.logical_shift_right, op1=AL.bitwise_and))
                    op("vector", lambda e: e.tensor_scalar(
                        out=t1, in0=t0, scalar1=28, scalar2=512,
                        op0=AL.add, op1=AL.mult))
                    op("vector", lambda e: e.tensor_scalar(
                        out=t0, in0=w_ap[bs], scalar1=lshcol[:, 0:1], scalar2=-32768,
                        op0=AL.logical_shift_left, op1=AL.bitwise_and))
                    op("vector", lambda e: e.tensor_scalar(
                        out=w_ap[bs].bitcast(dt.float16), in0=t1.bitcast(dt.float16),
                        scalar1=2.0, scalar2=1.0, op0=AL.mult, op1=AL.subtract))
                    op("vector", lambda e: e.tensor_tensor(
                        out=w_ap[bs].bitcast(dt.float16), in0=t1.bitcast(dt.float16),
                        in1=w_ap[bs].bitcast(dt.float16), op=AL.min))
                    op("vector", lambda e: e.tensor_tensor(
                        out=w_ap[bs], in0=w_ap[bs], in1=t0, op=AL.bitwise_xor))
                    late[(key, i)] = op("vector", lambda e: e.tensor_tensor(
                        out=o_ap[bs], in0=w_ap[bs].bitcast(dt.float16), in1=sc_ap[bs],
                        op=AL.mult), "DV")
                dec_steps.append(vec_step)
                wait("gpsimd", "DV", lateget((key, i), "DV"))
                em("gpsimd", lambda e, i=i, bs=bs: e.dma_start(
                    out=dst[i * 128:(i + 1) * 128, :], in_=o_ap[bs]).then_inc(sDS[bs], 16))
                C["GP"] += 16
                sched["gpsimd"].append(("i", "DS%d" % bs, 16, 0))
                ds_cnt[bs] += 1
                store_l.append(C["GP"])

        gu_store = []
        emit_decode(FC, dec_w, dec_sc, dec_o, dec_t0, dec_t1, gu16, gusc, gu_loc, GU_O,
                    gu_store, "gudec")
        em("gpsimd", lambda e, v=ds_cnt[0] * 16: e.wait_ge(sDS[0], v))
        sched["gpsimd"].append(("w", "DS0", ds_cnt[0] * 16, 0))
        em("gpsimd", lambda e, v=ds_cnt[1] * 16: e.wait_ge(sDS[1], v))
        sched["gpsimd"].append(("w", "DS1", ds_cnt[1] * 16, 0))
        em("gpsimd", lambda e: e.collective_compute(
            "AllGather", AL.bypass, replica_groups=[list(range(8))],
            ins=[gu_loc.opt()], outs=[gu_all.opt()]).then_inc(sAGG, 1))
        C["CC"] += 1
        sched["gpsimd"].append(("i", "CC", 1, 0))
        gu_ag = ("AGG", 1)
        if "dbg_guw" in dbg_outs:
            em("sync", lambda e: e.wait_ge(sAGG, 1))
            sched["sync"].append(("w", "CC", 2, 0))
            ld(dbg_outs["dbg_guw"], gu_loc)
        dn_w = [dview(16896, [128, DN_F], dt.int16), dview(17408, [128, DN_F], dt.int16)]
        dn_t0 = dview(17920, [128, DN_F], dt.int16)
        dn_t1 = dview(18432, [128, DN_F], dt.int16)
        dn_sc = [dview(18944, [128, DN_F], dt.float16), dview(19456, [128, DN_F], dt.float16)]
        dn_o = [dview(19968, [128, DN_F], dt.bfloat16), dview(20480, [128, DN_F], dt.bfloat16)]
        dn_store = []
        emit_decode(NIC, dn_w, dn_sc, dn_o, dn_t0, dn_t1, dn16, dnsc, dn_loc, DN_F, dn_store,
                    "dndec")
        for step in dec_steps:
            step()
        em("gpsimd", lambda e, v=ds_cnt[0] * 16: e.wait_ge(sDS[0], v))
        sched["gpsimd"].append(("w", "DS0", ds_cnt[0] * 16, 0))
        em("gpsimd", lambda e, v=ds_cnt[1] * 16: e.wait_ge(sDS[1], v))
        sched["gpsimd"].append(("w", "DS1", ds_cnt[1] * 16, 0))
        em("gpsimd", lambda e: e.collective_compute(
            "AllGather", AL.bypass, replica_groups=[list(range(8))],
            ins=[dn_loc.opt()], outs=[dn_all.opt()]).then_inc(sAGD, 1))
        C["CC"] += 1
        sched["gpsimd"].append(("i", "CC", 1, 0))
        dn_ag = ("AGD", 1)
        if "dbg_dnw" in dbg_outs:
            em("sync", lambda e: e.wait_ge(sAGD, 1))
            sched["sync"].append(("w", "CC", 3, 0))
            ld(dbg_outs["dbg_dnw"], dn_loc)

        mark(3)
        o_all = big[:, 0:16, :]
        q_all = big[:, 16:32, :]
        k_fm_s = [big[:, 32, :], big[:, 37, :]]
        kdk_tm_s = [big[:, 33, :].rearrange("p (a b) -> p a b", a=8),
                    big[:, 38, :].rearrange("p (a b) -> p a b", a=8)]
        v_tm_s = [big[:, 34, :].rearrange("p (a b) -> p a b", a=8),
                  big[:, 39, :].rearrange("p (a b) -> p a b", a=8)]
        qkD_s = [big[:, 35, :].rearrange("p (a b) -> p a b", a=2),
                 big[:, 40, :].rearrange("p (a b) -> p a b", a=2)]
        vscr = big[:, 36, :]

        qkv_ld = []
        for ck in range(48):
            bs = ck % 2
            wait_cast("sync")
            if ck >= 2:
                wait("sync", "PE", lateget(("qkvuse", ck - 2), "PE"))
            src = bass.AP(wqkvT_bf.tensor, ck * 128, [[6144, 128], [128 * 6144, FC], [1, 128]])
            dst = strm[:, bs * 2:bs * 2 + 2, :].rearrange("p a b -> p (a b)")
            dst = dst.rearrange("p (a b) -> p a b", a=FC)
            qkv_ld.append(("SA" if bs == 0 else "SB", lds(dst, src, "SA" if bs == 0 else "SB")))

        def strm_tile(bs):
            t = strm[:, bs * 2:bs * 2 + 2, :].rearrange("p a b -> p (a b)")
            return t.rearrange("p (a b) -> p a b", a=FC)

        def proj(oc_global, psum, rhs_src, guard, ldpt_list, usekey):
            ck = oc_global
            bs = ck % 2
            waitn("tensor", ldpt_list[ck][0], ldpt_list[ck][1])
            waitg("tensor", guard)
            for fc in range(FC):
                for th in range(2):
                    lastm = fc == FC - 1 and th == 1
                    op("tensor", lambda e, fc=fc, th=th, bs=bs, ps=psum, r=rhs_src:
                       e.matmul(ps[:, th, :], strm_tile(bs)[:, fc, :],
                                r[:, fc, th * 512:(th + 1) * 512],
                                start=(fc == 0), stop=(fc == FC - 1)),
                       "PE" if lastm else None)
            late[(usekey, ck)] = max(late.get((usekey, ck), 0), C["PE"])
            return C["PE"]

        kc_st = []
        kv_done = {}
        qk_ev = modc_rdy
        in_ev = None
        kvup = None
        prev_vtm = {0: None, 1: None}
        prev_kdk = {0: None, 1: None}
        dbt_rdy_s = {}
        wait("tensor", "DV", h_rdy)

        def proj_head(h):
            sl = h % 2
            k_fm = k_fm_s[sl]
            kdk_tm = kdk_tm_s[sl]
            v_tm = v_tm_s[sl]
            qpt = proj(3 * h + 0, mainA, h_fm, main_guard[0], qkv_ld, "qkvuse")
            wait("scalar", "PE", qpt)
            main_guard[0] = ("AC", op("scalar", lambda e, h=h: e.activation(
                out=q_all[:, h, :], in_=mainA[:].rearrange("p a b -> p (a b)"),
                func=AF.Silu), "AC"))
            kpt = proj(3 * h + 1, mainB, h_fm, main_guard[1], qkv_ld, "qkvuse")
            wait("scalar", "PE", kpt)
            main_guard[1] = ("AC", op("scalar", lambda e, k_fm=k_fm: e.activation(
                out=k_fm[:], in_=mainB[:].rearrange("p a b -> p (a b)"),
                func=AF.Silu), "AC"))
            k_ev = main_guard[1][1]
            if h >= 2:
                wait("scalar", "DV", lateget(("qkdone", h - 2), "DV"))
            dbt_rdy_s[h] = op("scalar", lambda e, h=h, sl=sl: e.dma_start(
                out=DBT[:, 2 * sl:2 * sl + 2, :],
                in_=bass.AP(dbt_dram.tensor, h * 128 * 512, [[512, 128], [1, 512]])),
                "SC", 16)
            vpt = proj(3 * h + 2, mainA, h_fm, main_guard[0], qkv_ld, "qkvuse")
            wait("scalar", "PE", vpt)
            main_guard[0] = ("AC", op("scalar", lambda e: e.activation(
                out=vscr[:], in_=mainA[:].rearrange("p a b -> p (a b)"),
                func=AF.Silu), "AC"))
            v_ev = main_guard[0][1]
            if h >= 2:
                wait("scalar", "DV", kv_done[h - 2])
                kc_st.append(op("scalar", lambda e, g=h - 2: e.dma_start(
                    out=kc_in[g * HD:(g + 1) * HD, :], in_=kv[:, g % 2, :]), "SC", 16))

            wait("tensor", "AC", k_ev)
            if h == 0:
                wait("tensor", "AC", pss_free)
            if prev_vtm[sl] is not None:
                wait("tensor", "DV", prev_vtm[sl])
            if prev_vtm[1 - sl] is not None:
                wait("tensor", "DV", prev_vtm[1 - sl])
            psS_t = psS[:].rearrange("p (a b) -> p a b", a=4)
            kdk_ev = None
            for half in range(2):
                pst = psT if half == 0 else psS_t
                for i in range(4):
                    g = half * 4 + i
                    op("tensor", lambda e, g=g, i=i, k_fm=k_fm, pst=pst: e.matmul(
                        pst[:, i, :], k_fm[:, g * 128:(g + 1) * 128], identb[:],
                        start=True, stop=True), "PE" if i == 3 else None)
            ktr = C["PE"]
            wait("vector", "PE", ktr)
            if h == 0:
                wait("vector", "AC", kdcl_rdy)
            for half in range(2):
                pst = psT if half == 0 else psS_t
                for i in range(4):
                    g = half * 4 + i
                    op("vector", lambda e, g=g, i=i, h=h, kdk_tm=kdk_tm, pst=pst:
                       e.tensor_scalar(
                        out=kdk_tm[:, g, :], in0=pst[:, i, :],
                        scalar1=kdcol_all[:, h, (g % 2):(g % 2) + 1], scalar2=None,
                        op0=AL.mult), "DV" if half == 1 and i == 3 else None)
            kdk_ev = C["DV"]
            prev_kdk[sl] = kdk_ev
            late[("kdkdone", h)] = kdk_ev
            wait("tensor", "AC", v_ev)
            wait("tensor", "DV", kdk_ev)
            for half in range(2):
                pst = psT if half == 0 else psS_t
                for i in range(4):
                    g = half * 4 + i
                    op("tensor", lambda e, g=g, i=i, pst=pst: e.matmul(
                        pst[:, i, :], vscr[:, g * 128:(g + 1) * 128], identb[:],
                        start=True, stop=True), "PE" if i == 3 else None)
            vtr = C["PE"]
            wait("vector", "PE", vtr)
            vtm_ev = None
            for half in range(2):
                pst = psT if half == 0 else psS_t
                vtm_ev = op("vector", lambda e, half=half, v_tm=v_tm, pst=pst:
                            e.tensor_copy(
                    out=v_tm[:, half * 4:half * 4 + 4, :].rearrange("p a b -> p (a b)"),
                    in_=pst[:].rearrange("p a b -> p (a b)")), "DV")
            prev_vtm[sl] = vtm_ev
            late[("projdone", h)] = (kdk_ev, vtm_ev)

        def attn_head(h):
            nonlocal qk_ev, in_ev, kvup
            sl = h % 2
            k_fm = k_fm_s[sl]
            kdk_tm = kdk_tm_s[sl]
            v_tm = v_tm_s[sl]
            qkD = qkD_s[sl]
            kdk_ev, vtm_ev = late[("projdone", h)]
            wait("vector", "SC", dbt_rdy_s[h])
            for j in range(4):
                wait("tensor", "DV", qk_ev)
                for nt in range(2):
                    op("tensor", lambda e, j=j, nt=nt, h=h, k_fm=k_fm: e.matmul(
                        psQK[:, nt, :],
                        k_fm[:, j * 256 + nt * 128: j * 256 + nt * 128 + 128],
                        q_all[:, h, j * 256:(j + 1) * 256], start=True, stop=True),
                        "PE" if nt == 1 else None)
                qkpt = C["PE"]
                wait("vector", "PE", qkpt)
                op("vector", lambda e, qkD=qkD, sl=sl: e.tensor_tensor(
                    out=qkD[:, 0:2, 0:256], in0=psQK[:],
                    in1=DBT[:, 2 * sl:2 * sl + 2, :], op=AL.mult), "DV")
                qk_ev = C["DV"]
                wait("tensor", "DV", qk_ev)
                wait("tensor", "DV", in_ev)
                wait("tensor", "DV", kvup)
                wait("tensor", "DV", vtm_ev)
                for nt in range(2):
                    op("tensor", lambda e, j=j, nt=nt, v_tm=v_tm, qkD=qkD: e.matmul(
                        psIN[:, 0, :], v_tm[:, j * 2 + nt, :], qkD[:, nt, 0:256],
                        start=(nt == 0), stop=(nt == 1)), "PE" if nt == 1 else None)
                inpt = C["PE"]
                wait("vector", "PE", inpt)
                in_ev = op("vector", lambda e, j=j, h=h: e.tensor_copy(
                    out=o_all[:, h, j * 256:(j + 1) * 256], in_=psIN[:, 0, :]), "DV")
                wait("tensor", "DV", in_ev)
                wait("tensor", "DV", kdk_ev)
                for nt in range(2):
                    op("tensor", lambda e, j=j, nt=nt, kdk_tm=kdk_tm, v_tm=v_tm: e.matmul(
                        psIN[:, 1, 0:128], kdk_tm[:, j * 2 + nt, :], v_tm[:, j * 2 + nt, :],
                        start=(nt == 0), stop=(nt == 1)), "PE" if nt == 1 else None)
                cpt = C["PE"]
                wait("vector", "PE", cpt)
                dst = kv[:, sl, :] if j == 3 else lall[:, 3 * h + j, :]
                if j == 0:
                    kvup_pt = op("vector", lambda e, dst=dst: e.tensor_copy(
                        out=dst, in_=psIN[:, 1, 0:128]), "DV")
                else:
                    if j == 3 and h > 1:
                        wait("vector", "SC", kc_st[h - 2])
                    src = lall[:, 3 * h + j - 1, :]
                    kvup_pt = op("vector", lambda e, h=h, dst=dst, src=src:
                                 e.scalar_tensor_tensor(
                        out=dst, in0=src, scalar=bdj_all[:, h, 0:1],
                        in1=psIN[:, 1, 0:128], op0=AL.mult, op1=AL.add), "DV")
                kvup = kvup_pt
            late[("qkdone", h)] = qk_ev
            kv_done[h] = C["DV"]

        for h in range(NH):
            proj_head(h)
            attn_head(h)
        for g in (NH - 2, NH - 1):
            wait("scalar", "DV", kv_done[g])
            kc_st.append(op("scalar", lambda e, g=g: e.dma_start(
                out=kc_in[g * HD:(g + 1) * HD, :], in_=kv[:, g % 2, :]), "SC", 16))
        if "dbg_kc" in dbg_outs:
            wait("sync", "SC", kc_st[-1])
            ld(dbg_outs["dbg_kc"], kc_in)

        mark(4)
        wait("gpsimd", "SC", kc_st[-1])
        em("gpsimd", lambda e: e.collective_compute(
            "AllGather", AL.bypass, replica_groups=[[0, 1, 2, 3], [4, 5, 6, 7]],
            ins=[kc_in.opt()], outs=[kc_all.opt()]).then_inc(sAGK, 1))
        C["CC"] += 1
        sched["gpsimd"].append(("i", "CC", 1, 0))
        kc_ag = C["CC"]

        em("sync", lambda e: e.wait_ge(sAGG, 1))
        sched["sync"].append(("w", "CC", 2, 0))
        arr_ld = lds(arrB4, arrB4_in, "M1")
        inter_ev = {0: ("DV", qk_ev), 1: ("DV", max(in_ev or 0, kvup or 0))}
        prev_sj_use = None
        prev_kvin = None
        for h in range(NH):
            em("scalar", lambda e: e.wait_ge(sAGK, 1))
            sched["scalar"].append(("w", "CC", kc_ag, 0))
            if prev_kvin is not None:
                wait("scalar", "DV", prev_kvin)
            kcl = op("scalar", lambda e, h=h: e.dma_start(
                out=kcbuf[:], in_=bass.AP(kc_all.tensor, h * HD * HD,
                                          [[HD, 128], [NH * HD * HD, 4], [1, HD]])), "SC", 16)
            if h == 0:
                waitn("scalar", "M1", arr_ld)
            op("scalar", lambda e, h=h: e.activation(
                out=qdB4, in_=arrB4, func=AF.Exp, scale=negsB[:, h:h + 1]), "AC")
            qd_rdy = C["AC"]
            wait("vector", "SC", kcl)
            op("vector", lambda e, h=h: e.tensor_scalar(
                out=kvin[:], in0=kcbuf[:, 0, :], scalar1=coef_all[:, h, 0:1],
                scalar2=None, op0=AL.mult))
            for g in (1, 2):
                op("vector", lambda e, h=h, g=g: e.scalar_tensor_tensor(
                    out=kvin[:], in0=kcbuf[:, g, :], scalar=coef_all[:, h, g:g + 1],
                    in1=kvin[:], op0=AL.mult, op1=AL.add),
                    "DV" if g == 2 else None)
            prev_kvin = C["DV"]
            if prev_sj_use is not None:
                wait("vector", "PE", prev_sj_use)
            op("vector", lambda e: e.tensor_copy(out=Sj[:, 0, :], in_=kvin[:]))
            for j in (1, 2, 3):
                op("vector", lambda e, h=h, j=j: e.scalar_tensor_tensor(
                    out=Sj[:, j, :], in0=kvin[:], scalar=bdj_all[:, h, j - 1:j],
                    in1=lall[:, 3 * h + j - 1, :], op0=AL.mult, op1=AL.add),
                    "DV" if j == 3 else None)
            s_rdy = C["DV"]
            for j in range(4):
                ps = psQK if j % 2 == 0 else psIN
                wait("tensor", "DV", s_rdy)
                waitg("tensor", inter_ev[j % 2])
                op("tensor", lambda e, h=h, j=j, ps=ps: e.matmul(
                    ps[:, 0, :], Sj[:, j, :], q_all[:, h, j * 256:(j + 1) * 256],
                    start=True, stop=True), "PE")
                ipt = C["PE"]
                if j == 3:
                    prev_sj_use = ipt
                wait("vector", "PE", ipt)
                wait("vector", "AC", qd_rdy)
                op("vector", lambda e, j=j, ps=ps: e.tensor_tensor(
                    out=scrb[:, 0:256], in0=ps[:, 0, :], in1=qdB4[:, j * 256:(j + 1) * 256],
                    op=AL.mult))
                inter_ev[j % 2] = ("DV", op("vector", lambda e, h=h, j=j: e.tensor_tensor(
                    out=o_all[:, h, j * 256:(j + 1) * 256],
                    in0=o_all[:, h, j * 256:(j + 1) * 256],
                    in1=scrb[:, 0:256], op=AL.add), "DV"))
        o_rdy = C["DV"]
        if "dbg_o" in dbg_outs:
            wait("sync", "DV", o_rdy)
            ld(dbg_outs["dbg_o"], o_all)
        mark(5)

        wait("scalar", "AC", pss_free)
        for fc in range(FC):
            wait("scalar", "DV", o_rdy)
            if fc >= 1:
                wait("scalar", "PE", lateget(("st2", fc - 1), "PE"))
            a = op("scalar", lambda e, fc=fc: e.activation(
                out=sqscr[:], in_=o_all[:, fc, :], func=AF.Square), "AC")
            wait("tensor", "AC", a)
            if fc == 0:
                wait("tensor", "AC", pss_free)
                wait("tensor", "DV", o_rdy)
                wait("tensor", "DV", prev_vtm[0])
                wait("tensor", "DV", prev_vtm[1])
            first, lastf = fc == 0, fc == FC - 1
            op("tensor", lambda e, first=first, lastf=lastf: e.matmul(
                stat_ap[2], onesb[:], sqscr[:, 0:512], start=first, stop=lastf))
            late[("st2", fc)] = op("tensor", lambda e, lastf=lastf, first=first: e.matmul(
                stat_ap[3], onesb[:], sqscr[:, 512:1024], start=first, stop=lastf), "PE")
        rms_stats = C["PE"]
        rms_rep, pss_free = stats_pipeline(1, rms_stats, True, None, repB[:],
                                           main_guard[1])
        main_guard[1] = ("DV", rms_rep)

        g_fm = q_all
        gate_ld = []
        for ck in range(16):
            bs = ck % 2
            wait_cast("sync")
            if ck < 2:
                wait("sync", "PE", lateget(("qkvuse", 46 + ck), "PE"))
            else:
                wait("sync", "PE", lateget(("gateuse", ck - 2), "PE"))
            src = bass.AP(wgateT_bf.tensor, ck * 128, [[DM, 128], [128 * DM, FC], [1, 128]])
            dst = strm[:, bs * 2:bs * 2 + 2, :].rearrange("p a b -> p (a b)")
            dst = dst.rearrange("p (a b) -> p a b", a=FC)
            gate_ld.append(("SA" if bs == 0 else "SB", lds(dst, src, "SA" if bs == 0 else "SB")))
        first_gate = True
        for oc in range(FC):
            ps = mainA if oc % 2 == 0 else mainB
            guard = main_guard[oc % 2]
            if first_gate:
                wait("tensor", "DV", o_rdy)
                first_gate = False
            pt = proj(oc, ps, h_fm, guard, gate_ld, "gateuse")
            wait("scalar", "PE", pt)
            main_guard[oc % 2] = ("AC", op("scalar", lambda e, oc=oc, ps=ps: e.activation(
                out=g_fm[:, oc, :], in_=ps[:].rearrange("p a b -> p (a b)"),
                func=AF.Sigmoid), "AC"))
        gate_done = main_guard[1][1]
        wait("vector", "AC", main_guard[0][1])
        wait("vector", "AC", gate_done)
        wait("vector", "DV", rms_rep)
        for fc in range(FC):
            op("vector", lambda e, fc=fc: e.tensor_tensor(
                out=scrb[:], in0=o_all[:, fc, :], in1=repB[:], op=AL.mult))
            op("vector", lambda e, fc=fc: e.scalar_tensor_tensor(
                out=o_all[:, fc, :], in0=scrb[:], scalar=normw_c[:, fc:fc + 1],
                in1=g_fm[:, fc, :], op0=AL.mult, op1=AL.mult),
                "DV" if fc == FC - 1 else None)
        gated_rdy = C["DV"]

        wout_ld = {}
        xo_ld = {}

        def emit_outproj_loads(fc):
            bs = fc % 2
            wait_cast("sync")
            if fc < 2:
                wait("sync", "PE", lateget(("gateuse", 14 + fc), "PE"))
            else:
                wait("sync", "PE", late[("woutuse", fc - 2)])
            s_ = bass.AP(woutT_bf.tensor, fc * 128, [[DM, 128], [128 * DM, FC], [1, 128]])
            d_ = strm[:, bs * 2:bs * 2 + 2, :].rearrange("p a b -> p (a b)")
            d_ = d_.rearrange("p (a b) -> p a b", a=FC)
            wout_ld[fc] = ("SA" if bs == 0 else "SB", lds(d_, s_, "SA" if bs == 0 else "SB"))
            if fc >= 2:
                wait("sync", "DV", late[("x2ev", fc - 2)])
            s_ = bass.AP(xT.tensor, fc * 128 * NT, [[NT, 128], [1, NT]])
            xo_ld[fc] = lds(xorig[bs], s_, "XOa" if bs == 0 else "XOb")

        emit_outproj_loads(0)
        emit_outproj_loads(1)
        x2st = []
        for fc in range(FC):
            bs = fc % 2
            ps = mainA if fc % 2 == 0 else mainB
            wait("tensor", "DV", gated_rdy)
            waitn("tensor", wout_ld[fc][0], wout_ld[fc][1])
            waitg("tensor", main_guard[fc % 2])
            wbs = fc % 2
            for oc in range(FC):
                for th in range(2):
                    lastm = oc == FC - 1 and th == 1

                    def mm(e, fc=fc, oc=oc, th=th, wbs=wbs, ps=ps):
                        t = strm_tile(wbs)
                        return e.matmul(ps[:, th, :], t[:, oc, :],
                                        o_all[:, oc, th * 512:(th + 1) * 512],
                                        start=(oc == 0), stop=(oc == FC - 1))
                    op("tensor", mm, "PE" if lastm else None)
            late[("woutuse", fc)] = C["PE"]
            opj = C["PE"]
            wait("vector", "PE", opj)
            waitn("vector", "XOa" if fc % 2 == 0 else "XOb", xo_ld[fc])
            if fc >= 2:
                waitn("vector", "SPa" if fc % 2 == 0 else "SPb",
                      lateget(("x2st", fc - 2), "LD"))
            op("vector", lambda e, fc=fc, bs=bs, ps=ps: e.scalar_tensor_tensor(
                out=xtile[bs], in0=ps[:].rearrange("p a b -> p (a b)"),
                scalar=dcol[:, fc:fc + 1, 2], in1=xorig[bs], op0=AL.mult, op1=AL.add), "DV")
            x2ev = C["DV"]
            late[("x2ev", fc)] = x2ev
            main_guard[fc % 2] = ("DV", x2ev)
            op("vector", lambda e, fc=fc, bs=bs: e.tensor_copy(
                out=h_fm[:, fc, :], in_=xtile[bs]), "DV")
            x2bf = C["DV"]
            wait("scalar", "DV", x2ev)
            if fc >= 1:
                wait("scalar", "PE", lateget(("st3", fc - 1), "PE"))
            a = op("scalar", lambda e, bs=bs: e.activation(
                out=sqscr[:], in_=xtile[bs], func=AF.Square), "AC")
            wait("tensor", "DV", x2bf)
            wait("tensor", "AC", a)
            if fc == 0:
                wait("tensor", "AC", pss_free)
            first, lastf = fc == 0, fc == FC - 1
            op("tensor", lambda e, fc=fc, first=first, lastf=lastf: e.matmul(
                stat_ap[0], onesb[:], h_fm[:, fc, 0:512], start=first, stop=lastf))
            op("tensor", lambda e, fc=fc, first=first, lastf=lastf: e.matmul(
                stat_ap[1], onesb[:], h_fm[:, fc, 512:1024], start=first, stop=lastf))
            op("tensor", lambda e, fc=fc, first=first, lastf=lastf: e.matmul(
                stat_ap[2], onesb[:], sqscr[:, 0:512], start=first, stop=lastf))
            late[("st3", fc)] = op("tensor", lambda e, fc=fc, lastf=lastf, first=first: e.matmul(
                stat_ap[3], onesb[:], sqscr[:, 512:1024], start=first, stop=lastf), "PE")
            wait("sync", "DV", x2ev)
            st = lds(bass.AP(x2_dram.tensor, fc * 128 * NT, [[NT, 128], [1, NT]]),
                     xtile[bs], "SPa" if bs == 0 else "SPb")
            late[("x2st", fc)] = st
            x2st.append(st)
            if fc + 2 < FC:
                emit_outproj_loads(fc + 2)
        ln2_stats = C["PE"]
        if "dbg_x2" in dbg_outs:
            waitn("sync", "SPa", lambda: NCNT["SPa"])
            waitn("sync", "SPb", lambda: NCNT["SPb"])
            ld(dbg_outs["dbg_x2"], x2_dram)
        ln2_rep, pss_free = stats_pipeline(0, ln2_stats, False, repA[:], repB[:],
                                           main_guard[1])
        main_guard[1] = ("DV", ln2_rep)
        wait("vector", "DV", ln2_rep)
        for fc in range(FC):
            op("vector", lambda e, fc=fc: e.tensor_tensor(
                out=scrb[:], in0=h_fm[:, fc, :], in1=repA[:], op=AL.subtract))
            op("vector", lambda e, fc=fc: e.tensor_tensor(
                out=h_fm[:, fc, :], in0=scrb[:], in1=repB[:], op=AL.mult))
            op("vector", lambda e, fc=fc: e.tensor_scalar(
                out=h_fm[:, fc, :], in0=h_fm[:, fc, :], scalar1=dcol[:, fc:fc + 1, 1],
                scalar2=mod_cols[:, 48 + fc:48 + fc + 1], op0=AL.mult, op1=AL.add),
                "DV" if fc == FC - 1 else None)
        h2_rdy = C["DV"]
        mark(6)

        act = big[:, 0:NIC, :]

        def gu_tile(bs):
            t = strm[:, bs * 2:bs * 2 + 2, :].rearrange("p a b -> p (a b)")
            return t.rearrange("p (a b) -> p a b", a=FC)

        gu_ldp = {}

        def emit_gu_load(i, half):
            oc = i if half == 0 else 44 + i
            bs = half
            em("sync", lambda e: e.wait_ge(sAGG, 1))
            sched["sync"].append(("w", "CC", 2, 0))
            if i == 0:
                wait("sync", "PE", lateget(("woutuse", 14 + half), "PE"))
            else:
                wait("sync", "PE", late[("guuse", (i - 1) if half == 0 else 44 + (i - 1))])
            rnk, jj = oc // 11, oc % 11
            s_ = bass.AP(gu_all.tensor, rnk * DM * GU_O + jj * 128,
                         [[GU_O, 128], [128 * GU_O, FC], [1, 128]])
            gu_ldp[oc] = ("SA" if bs == 0 else "SB", lds(gu_tile(bs), s_,
                                                         "SA" if bs == 0 else "SB"))

        emit_gu_load(0, 0)
        emit_gu_load(0, 1)
        for i in range(NIC):
            for half, oc in ((0, i), (1, 44 + i)):
                bs = half
                ps = mainA if half == 0 else mainB
                waitn("tensor", gu_ldp[oc][0], gu_ldp[oc][1])
                if i == 0 and half == 0:
                    wait("tensor", "DV", h2_rdy)
                waitg("tensor", main_guard[half])
                for fc in range(FC):
                    for th in range(2):
                        lastm = fc == FC - 1 and th == 1
                        op("tensor", lambda e, fc=fc, th=th, bs=bs, ps=ps: e.matmul(
                            ps[:, th, :], gu_tile(bs)[:, fc, :],
                            h_fm[:, fc, th * 512:(th + 1) * 512],
                            start=(fc == 0), stop=(fc == FC - 1)), "PE" if lastm else None)
                late[("guuse", oc)] = C["PE"]
                if i + 1 < NIC:
                    emit_gu_load(i + 1, half)
                if half == 0:
                    gpt = C["PE"]
                else:
                    upt = C["PE"]
            wait("scalar", "PE", gpt)
            main_guard[0] = ("AC", op("scalar", lambda e: e.activation(
                out=scrb[:], in_=mainA[:].rearrange("p a b -> p (a b)"), func=AF.Silu), "AC"))
            wait("vector", "PE", upt)
            wait("vector", "AC", main_guard[0][1])
            main_guard[1] = ("DV", op("vector", lambda e, i=i: e.tensor_tensor(
                out=act[:, i, :], in0=mainB[:].rearrange("p a b -> p (a b)"),
                in1=scrb[:], op=AL.mult), "DV"))
        act_rdy = main_guard[1][1]
        if "dbg_act" in dbg_outs:
            wait("sync", "DV", act_rdy)
            ld(dbg_outs["dbg_act"], act)
        mark(7)

        y_fm = h_fm

        def dn_tile(q):
            flat = strm[:].rearrange("p a b -> p (a b)")
            off = 0 if q % 2 == 0 else 2048
            return flat[:, off:off + 11 * 128].rearrange("p (a b) -> p a b", a=11)

        dn_ldp = {}
        qidx = 0
        for fc2 in range(FC):
            g_, coff = fc2 // 2, (fc2 % 2) * 128
            for q in range(4):
                em("sync", lambda e: e.wait_ge(sAGD, 1))
                sched["sync"].append(("w", "CC", 3, 0))
                if qidx < 2:
                    wait("sync", "PE", lateget(("guuse", 86 + qidx), "PE"))
                else:
                    wait("sync", "PE", lateget(("dnuse", qidx - 2), "PE"))
                src = bass.AP(dn_all.tensor,
                              g_ * I5 * DN_F + q * 11 * 128 * DN_F + coff,
                              [[DN_F, 128], [128 * DN_F, 11], [1, 128]])
                ch_ = "SA" if q % 2 == 0 else "SB"
                dn_ldp[(fc2, q)] = (ch_, lds(dn_tile(q), src, ch_))
                qidx += 1
        qidx = 0
        for fc2 in range(FC):
            ps = mainA if fc2 % 2 == 0 else mainB
            if fc2 == 0:
                wait("tensor", "DV", act_rdy)
            waitg("tensor", main_guard[fc2 % 2])
            for q in range(4):
                waitn("tensor", dn_ldp[(fc2, q)][0], dn_ldp[(fc2, q)][1])
                for icq in range(11):
                    ic = q * 11 + icq
                    for th in range(2):
                        lastm = icq == 10 and th == 1
                        op("tensor", lambda e, q=q, icq=icq, ic=ic, th=th, ps=ps: e.matmul(
                            ps[:, th, :], dn_tile(q)[:, icq, :],
                            act[:, ic, th * 512:(th + 1) * 512],
                            start=(ic == 0), stop=(ic == NIC - 1)), "PE" if lastm else None)
                late[("dnuse", qidx)] = C["PE"]
                qidx += 1
            ypt = C["PE"]
            wait("vector", "PE", ypt)
            yev = op("vector", lambda e, fc2=fc2, ps=ps: e.tensor_copy(
                out=y_fm[:, fc2, :], in_=ps[:].rearrange("p a b -> p (a b)")), "DV")
            main_guard[fc2 % 2] = ("DV", yev)
            wait("scalar", "DV", yev)
            if fc2 >= 1:
                wait("scalar", "PE", lateget(("st4", fc2 - 1), "PE"))
            a = op("scalar", lambda e, fc2=fc2: e.activation(
                out=sqscr[:], in_=y_fm[:, fc2, :], func=AF.Square), "AC")
            wait("tensor", "AC", a)
            if fc2 == 0:
                wait("tensor", "AC", pss_free)
            first, lastf = fc2 == 0, fc2 == FC - 1
            op("tensor", lambda e, first=first, lastf=lastf: e.matmul(
                stat_ap[2], onesb[:], sqscr[:, 0:512], start=first, stop=lastf))
            late[("st4", fc2)] = op("tensor", lambda e, lastf=lastf, first=first: e.matmul(
                stat_ap[3], onesb[:], sqscr[:, 512:1024], start=first, stop=lastf), "PE")
        y_stats = C["PE"]
        if "dbg_y" in dbg_outs:
            wait("sync", "DV", main_guard[0][1])
            wait("sync", "DV", main_guard[1][1])
            ld(dbg_outs["dbg_y"], y_fm[:])
        y_rep, pss_free = stats_pipeline(1, y_stats, True, None, repF,
                                         main_guard[1])
        main_guard[1] = ("DV", y_rep)

        waitn("sync", "SPa", lambda: None if not x2st else NCNT["SPa"])
        waitn("sync", "SPb", lambda: NCNT["SPb"])
        x2re = []
        for fc in range(2):
            src = bass.AP(x2_dram.tensor, fc * 128 * NT, [[NT, 128], [1, NT]])
            x2re.append(lds(xorig[fc], src, "XOa" if fc % 2 == 0 else "XOb"))
        tr_guard = {0: None, 1: None}
        tr_pe = {}
        for fc in range(FC):
            bs = fc % 2
            wait("vector", "DV", y_rep)
            waitn("vector", "XOa" if fc % 2 == 0 else "XOb", x2re[fc])
            if fc >= 2:
                wait("vector", "PE", tr_pe[fc - 2])
            op("vector", lambda e, fc=fc: e.tensor_tensor(
                out=fscr, in0=y_fm[:, fc, :], in1=repF, op=AL.mult))
            op("vector", lambda e, fc=fc, bs=bs: e.scalar_tensor_tensor(
                out=xtile[bs], in0=fscr, scalar=dcol[:, fc:fc + 1, 3], in1=xorig[bs],
                op0=AL.mult, op1=AL.add), "DV")
            cmb = C["DV"]
            ps = mainA if fc % 2 == 0 else mainB
            wait("tensor", "DV", cmb)
            waitg("tensor", main_guard[fc % 2] if fc < 2 else tr_guard[fc % 2])
            for tc in range(8):
                op("tensor", lambda e, tc=tc, bs=bs, ps=ps: e.matmul(
                    ps[:].rearrange("p a b -> p (a b)")[:, tc * 128:(tc + 1) * 128],
                    xtile[bs][:, tc * 128:(tc + 1) * 128], identf[:],
                    start=True, stop=True), "PE" if tc == 7 else None)
            tr_pe[fc] = C["PE"]
            wait("vector", "PE", tr_pe[fc])
            if fc >= 2:
                waitn("vector", "OUTa" if fc % 2 == 0 else "OUTb",
                      lateget(("outst", fc - 2), "LD"))
            tr_guard[fc % 2] = ("DV", op("vector", lambda e, fc=fc, ps=ps, bs=bs: e.tensor_copy(
                out=stg2[:, bs, :], in_=ps[:].rearrange("p a b -> p (a b)")), "DV"))
            wait("sync", "DV", tr_guard[fc % 2][1])
            dst = bass.AP(y_out.tensor, fc * 128, [[DM, 128], [128 * DM, 8], [1, 128]])
            late[("outst", fc)] = lds(dst, stg2[:, bs, :].rearrange("p (a b) -> p a b", a=8),
                                      "OUTa" if bs == 0 else "OUTb")
            if fc + 2 < FC:
                src2 = bass.AP(x2_dram.tensor, (fc + 2) * 128 * NT, [[NT, 128], [1, NT]])
                x2re.append(lds(xorig[bs], src2, "XOa" if bs == 0 else "XOb"))
        waitn("sync", "OUTa", lambda: NCNT["OUTa"])
        waitn("sync", "OUTb", lambda: NCNT["OUTb"])
        wait("sync", "LD", C["LD"])
        for _ch in ("CN", "SA", "SB", "MST", "M1", "XL", "XOa", "XOb", "SPa", "SPb"):
            waitn("sync", _ch, lambda c=_ch: NCNT[c] or None)

        nc._sched = sched
        if stage < 99 and stage in marks:
            mk = marks[stage]
            for e in prog:
                del prog[e][mk[e]:]
                del sched[e][mk[e]:]
            cnt = mk["cnt"]
            em("sync", lambda e: e.wait_ge(sLD, cnt["LD"]))
            em("scalar", lambda e: e.wait_ge(sSC, cnt["SC"]))
            em("gpsimd", lambda e: e.wait_ge(sGP, 0) if cnt["GP"] == 0 else e.wait_ge(sCAST, CASTS_ALL))
        @block.sync
        def _(e):
            for fn in prog["sync"]:
                fn(e)

        @block.gpsimd
        def _(e):
            for fn in prog["gpsimd"]:
                fn(e)

        @block.tensor
        def _(e):
            for fn in prog["tensor"]:
                fn(e)

        @block.vector
        def _(e):
            for fn in prog["vector"]:
                fn(e)

        @block.scalar
        def _(e):
            for fn in prog["scalar"]:
                fn(e)

    return nc


def _consts(core):
    arr = np.arange(1, 257, dtype=np.float32)
    arrB4 = np.tile(np.tile(arr, 4)[None, :], (128, 1))
    m = np.arange(256)
    d = m[None, :] - m[:, None]
    idxT = np.where(d >= 0, d, 1e5).astype(np.float32)
    idxT = np.ascontiguousarray(idxT.reshape(2, 128, 256))
    idxT = np.ascontiguousarray(idxT.transpose(1, 0, 2))
    p = np.arange(128)
    kdcolT = np.zeros((128, 2), np.float32)
    for t in range(2):
        kdcolT[:, t] = 255.0 - (128 * t + p)
    c256 = np.zeros((128, 4), np.float32)
    c256[:, :] = np.array([256.0, 512.0, 768.0, 1024.0])[None, :]
    chunk = core % 4
    expo = np.full((128, 4), 1e9, np.float32)
    for g in range(4):
        if g < chunk:
            expo[:, g] = 1024.0 * (chunk - 1 - g)
    j = (p % 4).astype(np.int16)
    shcol = (4 * j).reshape(128, 1)
    lshcol = (12 - 4 * j).reshape(128, 1)
    isb1 = np.zeros((128, 2), np.float32)
    isb1[:, 0] = 1.0 if core >= 4 else 0.0
    isb1[:, 1] = 1.0 - isb1[:, 0]
    return dict(
        identb_in=np.eye(128, dtype=bf16),
        identf_in=np.eye(128, dtype=np.float32),
        onesb_in=np.ones((128, 1), dtype=bf16),
        onesf_in=np.ones((1, 128), dtype=np.float32),
        arrB4_in=arrB4.astype(np.float32), idxT_in=idxT, kdcolT_in=kdcolT,
        c256_in=c256, expo_in=expo, shcol_in=shcol, lshcol_in=lshcol,
        isb1_in=isb1,
        eps_in=np.array([[1e-5, 1e-6]], dtype=np.float32),
    )


def make_inmaps(inputs):
    x = np.asarray(inputs["L0_attn_hidden_states"], dtype=np.float32)
    slope = np.asarray(inputs["L0_attn_slope_rate"], dtype=np.float32).reshape(1, NH)
    wqkv = np.asarray(inputs["L0_attn_w_qkv"], dtype=np.float32)
    wgate = np.asarray(inputs["L0_attn_w_gate"], dtype=np.float32)
    wout = np.asarray(inputs["L0_attn_w_out"], dtype=np.float32)
    normw = np.ascontiguousarray(
        np.asarray(inputs["L0_attn_norm_w"], dtype=np.float32).reshape(FC, 128).T)
    gup = np.asarray(inputs["L0_ffn_gate_up_packed"], dtype=np.int32)
    gus = np.asarray(inputs["L0_ffn_gate_up_scales"], dtype=np.float16)
    dnp = np.asarray(inputs["L0_ffn_down_packed"], dtype=np.int32)
    dns = np.asarray(inputs["L0_ffn_down_scales"], dtype=np.float16)
    cc = np.asarray(inputs["L0_c"], dtype=np.float32)
    wmod = np.asarray(inputs["L0_wmod"], dtype=np.float32)

    wqkvT = np.ascontiguousarray(wqkv.T)
    wgateT = np.ascontiguousarray(wgate.T)
    woutT = np.ascontiguousarray(wout.T)
    cTm = np.ascontiguousarray(cc.T)
    gu16T = np.ascontiguousarray(gup.view(np.int16).T)
    dn16T = np.ascontiguousarray(dnp.view(np.int16).T)

    maps = []
    for r in range(8):
        b, ch = r // 4, r % 4
        t0 = ch * NT
        m = dict(
            xT=np.ascontiguousarray(x[b, t0:t0 + NT, :].T),
            cT=cTm,
            wmod_s=np.ascontiguousarray(wmod[:, 1536 * r:1536 * (r + 1)]),
            wqkvT=wqkvT, wgateT=wgateT, woutT=woutT,
            s_row=slope, normw=normw,
            gu16=np.ascontiguousarray(gu16T[:, GU_O * r:GU_O * (r + 1)]),
            gusc=np.ascontiguousarray(gus[:, GU_O * r:GU_O * (r + 1)]),
            dn16=np.ascontiguousarray(dn16T[:, DN_F * r:DN_F * (r + 1)]),
            dnsc=np.ascontiguousarray(dns[:, DN_F * r:DN_F * (r + 1)]),
        )
        m.update(_consts(r))
        maps.append(m)
    return maps


def kernel(**inputs):
    maps = make_inmaps(inputs)
    nc = build_nc()
    res = run_bass_kernel_spmd(nc, maps, core_ids=list(range(8)))
    out = np.zeros((B, N, DM), dtype=np.float32)
    for r in range(8):
        b, ch = r // 4, r % 4
        out[b, ch * NT:(ch + 1) * NT, :] = res.results[r]["y_out"]
    return out
